# version 01
# baseline (speedup 1.0000x reference)
import math
import numpy as np
import ml_dtypes

import concourse.bass as bass
import concourse.mybir as mybir
from concourse.bass_utils import run_bass_kernel_spmd

F32 = mybir.dt.float32
BF16 = mybir.dt.bfloat16
AX = mybir.AxisListType
ALU = mybir.AluOpType
ACT = mybir.ActivationFunctionType

D = 1024
T = 2048
NG = 16
NSUB = 4
SEQ = 8192
DCH = 8
EPS_LN = 1e-5
EPS_SUM = 1e-7
EPS_NORM = 1e-12
BIG = 1e30


class Buf:
    def __init__(self):
        self.cons = []

    def consume(self, tok):
        self.cons.append(tok)

    def war(self):
        c, self.cons = self.cons, []
        return c


class Sched:
    DMAQ = {"dY": "Y", "dS": "S", "dG": "G"}

    def __init__(self):
        self.ops = []

    def op(self, eng, fn, deps=(), tag=""):
        self.ops.append(dict(eng=eng, fn=fn, deps=sorted(set(deps)), q=eng,
                             step=1, tag=tag))
        return len(self.ops) - 1

    def dma(self, q, fn, deps=(), tag=""):
        self.ops.append(dict(eng=self.DMAQ[q], fn=fn, deps=sorted(set(deps)),
                             q=q, step=16, tag=tag))
        return len(self.ops) - 1

    def prepare(self, nc, serialize=False, ser_range=None):
        self.serialize = serialize
        if serialize:
            ser_range = (0, len(self.ops))
        if ser_range is not None:
            lo, hi = ser_range
            for i in range(max(1, lo), min(hi, len(self.ops))):
                self.ops[i]["deps"] = sorted(set(self.ops[i]["deps"] + [i - 1]))
            self.serialize = True

        self.fsem = nc.alloc_semaphore(name="vfence")
        self.fcount = [0]
        n = len(self.ops)
        needed = set()
        for o in self.ops:
            for d in o["deps"]:
                assert 0 <= d < n
                needed.add(d)
        for i, o in enumerate(self.ops):
            if o["step"] == 16:
                needed.add(i)
        self.qs = {}
        for i, o in enumerate(self.ops):
            q = o["q"]
            if q not in self.qs:
                self.qs[q] = dict(count=0, sem=nc.alloc_semaphore(name=f"q{q}"))
            o["inc"] = i in needed
            if o["inc"]:
                self.qs[q]["count"] += o["step"]
            o["after"] = (q, self.qs[q]["count"])

    def vf(self, eng, inst):
        self.fcount[0] += 1
        inst.then_inc(self.fsem, 1)
        eng.wait_ge(self.fsem, self.fcount[0])
        return inst

    def emit_engine(self, ename, eng):
        for i, o in enumerate(self.ops):
            if o["eng"] != ename:
                continue
            waits = {}
            for d in self.ops[i]["deps"]:
                do = self.ops[d]
                if (do["q"] == o["q"] and o["step"] == 1 and do["step"] == 1
                        and not self.serialize):
                    continue
                dq, dv = do["after"]
                waits[dq] = max(waits.get(dq, 0), dv)
            for wq in sorted(waits):
                eng.wait_ge(self.qs[wq]["sem"], waits[wq])
            inst = o["fn"](eng)
            if o["inc"]:
                assert inst is not None, f"op {i} ({o['tag']}) returned None"
                inst.then_inc(self.qs[o["q"]]["sem"], o["step"])


def build(taps=False, serialize=False, ser_range=None):
    nc = bass.Bass("TRN2", target_bir_lowering=False, debug=False)

    xt = nc.dram_tensor("xt", [D, T], F32, kind="ExternalInput").ap()
    wqkv_d = nc.dram_tensor("wqkv", [D, 3 * D], F32, kind="ExternalInput").ap()
    bqkv_d = nc.dram_tensor("bqkv", [1, 3 * D], F32, kind="ExternalInput").ap()
    wln_d = nc.dram_tensor("wln", [D, D], F32, kind="ExternalInput").ap()
    blnc_d = nc.dram_tensor("blnc", [128, 8], F32, kind="ExternalInput").ap()
    w1t_d = nc.dram_tensor("w1t", [D, 512], F32, kind="ExternalInput").ap()
    w2t_d = nc.dram_tensor("w2t", [512, D], F32, kind="ExternalInput").ap()
    wgt_d = nc.dram_tensor("wgt", [D, 128], F32, kind="ExternalInput").ap()
    bg_d = nc.dram_tensor("bg", [1, 8], F32, kind="ExternalInput").ap()
    iden32_d = nc.dram_tensor("iden32", [128, 128], F32, kind="ExternalInput").ap()
    idenb_d = nc.dram_tensor("idenb", [128, 128], BF16, kind="ExternalInput").ap()
    selb_d = nc.dram_tensor("selb", [128, 512], BF16, kind="ExternalInput").ap()
    jsel_d = nc.dram_tensor("jsel", [128, 512], BF16, kind="ExternalInput").ap()
    out_t = nc.dram_tensor("out_t", [D, T], F32, kind="ExternalOutput").ap()

    q2_dram = nc.dram_tensor("q2_dram", [T, D], BF16, kind="Internal")
    ar_in = nc.dram_tensor("ar_in", [D + 1, D], F32, kind="Internal")
    ar_out = nc.dram_tensor("ar_out", [D + 1, D], F32, kind="Internal")

    tap = {}
    if taps:
        for name, shape, dt in [
            ("tap_f1", [D, T], F32), ("tap_x2", [D, T], F32),
            ("tap_u", [T, D], F32), ("tap_v", [T, D], F32),
            ("tap_q2", [T, D], F32), ("tap_arin", [D + 1, D], F32),
            ("tap_arout", [D + 1, D], F32), ("tap_kvt", [D, D], F32),
            ("tap_stat", [128, 112], F32), ("tap_gw", [128, 128], F32),
            ("tap_gl1", [128, 128], F32), ("tap_st1", [128, 48], F32),
            ("tap_gw1", [128, 128], F32),
        ]:
            tap[name] = nc.dram_tensor(name, shape, dt, kind="ExternalOutput").ap()

    arena = nc.alloc_sbuf_tensor("arena", [128, 208500], mybir.dt.uint8)
    base = nc.lookup_mloc(arena).addr
    off = [0]
    def sball(name, shape, dtype):
        nbytes = int(np.prod(shape[1:])) * mybir.dt.size(dtype)
        t = nc.alloc_sbuf_tensor_at(name, shape, dtype, offset=base + off[0])
        off[0] += (nbytes + 31) & ~31
        return t

    sx = sball("sx", [128, DCH * T], F32)
    wgt = sball("wgt", [128, DCH * 128], BF16)
    onescol = sball("onescol", [128, 128], BF16)
    onesrow = sball("onesrow", [128, 128], BF16)
    iden32 = sball("iden32", [128, 128], F32)
    idenb = sball("idenb", [128, 128], BF16)
    selb = sball("selb", [128, 512], BF16)
    aux_w = sball("aux_w", [128, 2048], BF16)
    aux_w1 = sball("aux_w1", [128, 512], BF16)
    bgrow = sball("bgrow", [1, 8], BF16)
    cswgrow = sball("cswgrow", [1, 8], BF16)
    aux4 = [sball(f"aux4_{j}", [128, 128], BF16) for j in range(4)]
    trh = sball("trh", [128, 512], BF16)
    trl = sball("trl", [128, 512], BF16)
    trC = sball("trC", [128, 256], BF16)
    r4pad = sball("r4pad", [128, 128], BF16)
    jsel = sball("jsel", [128, 512], BF16)
    onescolB = sball("onescolB", [128, 128], BF16)
    blnc = sball("blnc", [128, 8], F32)
    czero = sball("czero", [128, 1], F32)
    vdmy = sball("vdmy", [128, 2], F32)
    cepsln = sball("cepsln", [128, 1], F32)
    statc = sball("statc", [128, 112], F32)
    gl = sball("gl", [128, 128], F32)
    gtmp = sball("gtmp", [128, 128], F32)
    gw = sball("gw", [128, 128], F32)
    gwb = sball("gwb", [128, 256], BF16)
    csbg = sball("csbg", [128, 16], F32)
    lrowh = sball("lrowh", [128, 512], BF16)
    lrowl = sball("lrowl", [128, 512], BF16)
    wtg128 = sball("wtg128", [128, 128], BF16)
    arstage = sball("arstage", [128, 1024], F32)
    u2st = sball("u2st", [128, 1024], BF16)
    q2st = sball("q2st", [128, 1024], BF16)
    xs2 = sball("xs2", [128, DCH * 512], BF16)
    shared = sball("shared", [128, 38912], mybir.dt.uint8)
    vtm = sball("vtm", [128, NG * D], BF16)
    utm = sball("utm", [128, NG * D], BF16)
    assert off[0] <= 208500, f"SBUF over budget: {off[0]}"

    shbase = nc.lookup_mloc(shared).addr
    def shv(name, shape, dtype, byte_off):
        nbytes = int(np.prod(shape[1:])) * mybir.dt.size(dtype)
        assert byte_off + nbytes <= 38912
        return nc.alloc_sbuf_tensor_at(name, shape, dtype,
                                       offset=shbase + byte_off)

    w1t = shv("w1t", [128, DCH * 512], BF16, 0)
    w2t = shv("w2t", [128, 4 * 1024], BF16, 8192)
    xbsub = shv("xbsub", [128, DCH * 512], BF16, 16384)
    xqsub = shv("xqsub", [128, DCH * 512], BF16, 24576)
    xssub = shv("xssub", [128, DCH * 512], BF16, 16384)
    hg = shv("hg", [128, 4 * 512], F32, 24576)
    hp = shv("hp", [128, 4 * 512], BF16, 32768)
    wqkv = shv("wqkv", [128, DCH * 2048], BF16, 0)
    wlns = shv("wlns", [128, DCH * 1024], BF16, 0)
    kvtb = shv("kvtb", [128, DCH * 1024], BF16, 16384)
    kvwb = shv("kvwb", [128, DCH * 1024], BF16, 16384)

    vF = nc.alloc_sbuf_tensor_at("vF", [128, DCH * T], BF16,
                                 offset=nc.lookup_mloc(utm).addr)
    q2F = nc.alloc_sbuf_tensor_at("q2F", [128, DCH * T], BF16,
                                  offset=nc.lookup_mloc(vtm).addr)

    pall = nc.alloc_psum_tensor("pall", [128, 4096], F32)
    def bank(b, p0=0, p1=128, c0=0, c1=512):
        return pall[p0:p1, 512 * b + c0: 512 * b + c1]
    def bankb16(b, p0=0, p1=128, c0=0, c1=1024):
        return pall[p0:p1, 512 * b: 512 * (b + 1)].bitcast(BF16)[:, c0:c1]

    C_M, C_RSTD, C_MR = 0, 16, 32
    C_T0, C_T1, C_T2 = 48, 64, 80
    C_RCP, C_G = 96, 104

    S = Sched()
    S.dbg = {}
    bank_tok = {}
    def bdep(*bs):
        return [bank_tok[b] for b in bs if b in bank_tok]
    def set_bank(b, tok):
        bank_tok[b] = tok

    bufs = dict(xbsub=Buf(), xqsub=Buf(), xssub=Buf(), xs2=Buf(),
                trC=Buf(), r4pad=Buf(), aux4=Buf(), rowstage=Buf(),
                lrow=Buf(), wtg=Buf(), hg=Buf(), hp=Buf(), arstage=Buf(),
                aux_w=Buf(), cg=Buf(), q2st=Buf(), u2st=Buf())

    def mm(e, out, lhsT, rhs, start, stop=True):
        return e.matmul(out, lhsT, rhs, start=start, stop=stop)

    def vtail(e):
        return e.tensor_copy(vdmy[:, 0:1], vdmy[:, 1:2])

    tk_x = [S.dma("dY", lambda e, ch=ch: e.dma_start(
        out=sx[:, ch * T:(ch + 1) * T], in_=xt[128 * ch:128 * (ch + 1), :]),
        tag=f"ldx{ch}") for ch in range(DCH)]
    tk_iden32 = S.dma("dY", lambda e: e.dma_start(out=iden32[:], in_=iden32_d))
    tk_idenb = S.dma("dY", lambda e: e.dma_start(out=idenb[:], in_=idenb_d))
    tk_selb = S.dma("dY", lambda e: e.dma_start(out=selb[:], in_=selb_d))
    tk_jsel = S.dma("dY", lambda e: e.dma_start(out=jsel[:], in_=jsel_d))
    tk_blnc = S.dma("dY", lambda e: e.dma_start(out=blnc[:], in_=blnc_d))

    tk_w1t = S.dma("dG", lambda e: e.dma_start(
        out=w1t[:].rearrange("p (c r) -> p c r", c=DCH),
        in_=w1t_d.rearrange("(c p) r -> p c r", p=128)), tag="ldw1t")
    tk_w2t = S.dma("dG", lambda e: e.dma_start(
        out=w2t[:].rearrange("p (c d) -> p c d", c=4),
        in_=w2t_d.rearrange("(c p) d -> p c d", p=128)), tag="ldw2t")
    tk_wgt = S.dma("dG", lambda e: e.dma_start(
        out=wgt[:].rearrange("p (c f) -> p c f", c=DCH),
        in_=wgt_d.rearrange("(c p) f -> p c f", p=128)), tag="ldwgt")
    tk_bg = S.dma("dG", lambda e: e.dma_start(out=bgrow[:], in_=bg_d))

    tk_consts = S.op("V", lambda e: (
        e.memset(onescol[:], 0.0), e.memset(onescol[:, 0:1], 1.0),
        e.memset(onesrow[:], 0.0), e.memset(onesrow[0:1, :], 1.0),
        e.memset(aux4[0][:], 0.0), e.memset(aux4[1][:], 0.0),
        e.memset(aux4[2][:], 0.0), e.memset(aux4[3][:], 0.0),
        e.memset(aux4[0][0:1, :], 1.0), e.memset(aux4[1][0:1, :], 1.0),
        e.memset(aux4[2][0:1, :], 1.0), e.memset(aux4[3][0:1, :], 1.0),
        e.memset(trh[:], 0.0), e.memset(trl[:], 0.0),
        e.memset(trC[:], 0.0), e.memset(r4pad[:], 0.0),
        e.memset(onescolB[:], 0.0), e.memset(onescolB[:, 1:2], 1.0),
        e.memset(lrowh[:], 0.0), e.memset(lrowl[:], 0.0),
        e.memset(wtg128[:], 0.0), e.memset(aux_w[:], 0.0),
        e.memset(aux_w1[:], 0.0), e.memset(czero[:], 0.0),
        e.memset(vdmy[:], 0.0),
        e.memset(cepsln[:], EPS_LN))[-1], tag="consts")
    nc.const_aps.register(F32, 0.0, czero[:])

    def t_csw1(e):
        for ch in range(DCH):
            i = mm(e, bank(0), onescol[:], w1t[:, ch * 512:(ch + 1) * 512],
                   start=ch == 0)
        return i
    tk_csw1m = S.op("T", t_csw1, deps=[tk_w1t, tk_consts], tag="csw1m")
    tk_csw1 = S.op("V", lambda e: e.tensor_copy(aux_w1[32:33, :],
                                                bank(0, 0, 1)),
                   deps=[tk_csw1m], tag="csw1")
    set_bank(0, tk_csw1)

    def t_cswg(e):
        for ch in range(DCH):
            i = mm(e, bank(1, c0=0, c1=128), onescol[:],
                   wgt[:, ch * 128:(ch + 1) * 128], start=ch == 0)
        return i
    tk_cswgm = S.op("T", t_cswg, deps=[tk_wgt, tk_consts], tag="cswgm")
    tk_cswg = S.op("V", lambda e: e.tensor_copy(cswgrow[:],
                                                bank(1, 0, 1, 0, 8)),
                   deps=[tk_cswgm], tag="cswg")
    set_bank(1, tk_cswg)
    tk_csbgm = S.op("T", lambda e: (
        mm(e, bank(2, c0=0, c1=8), onesrow[0:1, :], cswgrow[:], start=True),
        mm(e, bank(2, c0=8, c1=16), onesrow[0:1, :], bgrow[:],
           start=True))[-1], deps=[tk_cswg, tk_bg, tk_consts], tag="csbgm")
    tk_csbg = S.op("V", lambda e: e.tensor_copy(csbg[:], bank(2, c0=0, c1=16)),
                   deps=[tk_csbgm], tag="csbg")
    set_bank(2, tk_csbg)

    def emit_stats(xdeps, extra=None, tag=""):
        tr_toks = []
        for sub in range(NSUB):
            t0 = sub * 512
            def v_xb(e, t0=t0):
                for ch in range(DCH):
                    i = e.tensor_copy(xbsub[:, ch * 512:(ch + 1) * 512],
                                      sx[:, ch * T + t0: ch * T + t0 + 512])
                return i
            tk_xb = S.op("V", v_xb, deps=list(xdeps) + bufs["xbsub"].war(),
                         tag=f"xb.{tag}{sub}")
            def v_xq(e, t0=t0):
                for ch in range(DCH):
                    i = e.tensor_tensor(
                        xqsub[:, ch * 512:(ch + 1) * 512],
                        sx[:, ch * T + t0: ch * T + t0 + 512],
                        sx[:, ch * T + t0: ch * T + t0 + 512], ALU.mult)
                return i
            tk_xq = S.op("V", v_xq, deps=list(xdeps) + bufs["xqsub"].war(),
                         tag=f"xq.{tag}{sub}")
            def t_sx(e):
                for ch in range(DCH):
                    mm(e, bank(7), onescol[:],
                       xbsub[:, ch * 512:(ch + 1) * 512], start=ch == 0,
                       stop=False)
                for ch in range(DCH):
                    i = mm(e, bank(7), onescolB[:],
                           xqsub[:, ch * 512:(ch + 1) * 512], start=False,
                           stop=ch == DCH - 1)
                return i
            tk_sx = S.op("T", t_sx, deps=[tk_xb, tk_xq, tk_consts] + bdep(7),
                         tag=f"Sx.{tag}{sub}")
            bufs["xbsub"].consume(tk_sx)
            bufs["xqsub"].consume(tk_sx)
            tk_rs = S.op("V", lambda e: (
                S.vf(e, e.tensor_copy(trh[0:2, :], bank(7, 0, 2))),
                e.scalar_tensor_tensor(trl[0:2, :], bank(7, 0, 2), 1.0,
                                       trh[0:2, :], ALU.mult, ALU.subtract)
            )[-1], deps=[tk_sx] + bufs["rowstage"].war(),
                tag=f"rows.{tag}{sub}")
            set_bank(7, tk_rs)
            def t_tr(e, sub=sub):
                for j in range(4):
                    g = sub * 4 + j
                    mm(e, bank(5, c0=2 * g, c1=2 * g + 2),
                       trh[:, 128 * j:128 * (j + 1)], idenb[:, 0:2],
                       start=True, stop=False)
                    i = mm(e, bank(5, c0=2 * g, c1=2 * g + 2),
                           trl[:, 128 * j:128 * (j + 1)], idenb[:, 0:2],
                           start=False, stop=True)
                return i
            tk_tr = S.op("T", t_tr,
                         deps=[tk_rs, tk_idenb] + (bdep(5) if sub == 0 else []),
                         tag=f"trS.{tag}{sub}")
            bufs["rowstage"].consume(tk_tr)
            tr_toks.append(tk_tr)
            if extra is not None:
                extra(sub, tk_xb, tk_sx)
        sx_cols = pall[:, 5 * 512:5 * 512 + 32].rearrange(
            "p (g t) -> p g t", t=2)
        def v_cm(e):
            S.vf(e, e.tensor_scalar(statc[:, C_M:C_M + 16],
                                    sx_cols[:, :, 0:1].squeeze(2),
                                    -1.0 / D, None, ALU.mult))
            S.vf(e, e.tensor_scalar(statc[:, C_T0:C_T0 + 16],
                                    sx_cols[:, :, 1:2].squeeze(2),
                                    1.0 / D, None, ALU.mult))
            S.vf(e, e.tensor_tensor(statc[:, C_T1:C_T1 + 16],
                                    statc[:, C_M:C_M + 16],
                                    statc[:, C_M:C_M + 16], ALU.mult))
            i = e.tensor_tensor(statc[:, C_T2:C_T2 + 16],
                                statc[:, C_T0:C_T0 + 16],
                                statc[:, C_T1:C_T1 + 16], ALU.subtract)
            return i
        tk_cm = S.op("V", v_cm, deps=tr_toks, tag=f"colm.{tag}")
        set_bank(5, tk_cm)
        tk_sq = S.op("S", lambda e: e.activation(
            out=statc[:, C_T0:C_T0 + 16], in_=statc[:, C_T2:C_T2 + 16],
            func=ACT.Sqrt, bias=cepsln[:]), deps=[tk_cm], tag=f"sqrt.{tag}")
        def v_cm2(e):
            S.vf(e, e.reciprocal(statc[:, C_RSTD:C_RSTD + 16],
                                 statc[:, C_T0:C_T0 + 16]))
            S.vf(e, e.tensor_tensor(statc[:, C_MR:C_MR + 16],
                                    statc[:, C_M:C_M + 16],
                                    statc[:, C_RSTD:C_RSTD + 16], ALU.mult))
            return vtail(e)
        tk_cm2 = S.op("V", v_cm2, deps=[tk_sq], tag=f"colm2.{tag}")
        return tk_cm2

    def emit_rows(sub, tk_stats, tag=""):
        def v_tc(e, sub=sub):
            e.tensor_copy(trC[:, 0:128:32],
                          statc[:, C_RSTD + 4 * sub:C_RSTD + 4 * sub + 4])
            i = e.tensor_copy(trC[:, 128:256:32],
                              statc[:, C_MR + 4 * sub:C_MR + 4 * sub + 4])
            return i
        tk_tc = S.op("V", v_tc, deps=[tk_stats] + bufs["trC"].war(),
                     tag=f"trC.{tag}{sub}")
        def t_cr(e):
            mm(e, bank(7, c0=0, c1=128), trC[:, 0:128], idenb[:], start=True)
            i = mm(e, bank(7, c0=128, c1=256), trC[:, 128:256], idenb[:],
                   start=True)
            return i
        tk_cr = S.op("T", t_cr, deps=[tk_tc, tk_idenb] + bdep(7),
                     tag=f"crMM.{tag}{sub}")
        bufs["trC"].consume(tk_cr)
        def v_rows(e):
            for j in range(4):
                e.tensor_copy(r4pad[32 * j:32 * j + 1, :],
                              bank(7, p0=32 * j, p1=32 * j + 1, c0=0, c1=128))
            for j in range(4):
                i = e.tensor_copy(
                    aux4[j][32:33, :],
                    bank(7, p0=32 * j, p1=32 * j + 1, c0=128, c1=256))
            return i
        tk_rows = S.op("V", v_rows,
                       deps=[tk_cr] + bufs["r4pad"].war() + bufs["aux4"].war(),
                       tag=f"rows4.{tag}{sub}")
        set_bank(7, tk_rows)
        return tk_rows, tk_rows

    def emit_xs(sub, tk_rp, dst, dstbuf, tag=""):
        def t_bc(e):
            for j in range(4):
                i = mm(e, bank(6, c0=128 * j, c1=128 * (j + 1)),
                       jsel[:, 128 * j:128 * (j + 1)], r4pad[:],
                       start=True)
            return i
        tk_bc = S.op("T", t_bc, deps=[tk_rp, tk_jsel] + bdep(6),
                     tag=f"bc.{tag}{sub}")
        bufs["r4pad"].consume(tk_bc)
        t0 = sub * 512
        def v_xs(e, t0=t0):
            for ch in range(DCH):
                i = e.tensor_tensor(dst[:, ch * 512:(ch + 1) * 512],
                                    sx[:, ch * T + t0: ch * T + t0 + 512],
                                    bank(6), ALU.mult)
            return i
        tk_xs = S.op("V", v_xs, deps=[tk_bc] + bufs[dstbuf].war(),
                     tag=f"xs.{tag}{sub}")
        set_bank(6, tk_xs)
        return tk_xs

    def emit_ffn(xdeps, wdeps, tag):
        logit_tr = []
        def sweep1_extra(sub, tk_xb, tk_sx):
            def t_lg(e):
                for ch in range(DCH):
                    i = mm(e, bank(4), wgt[:, ch * 128:(ch + 1) * 128],
                           xbsub[:, ch * 512:(ch + 1) * 512], start=ch == 0)
                return i
            tk_lgm = S.op("T", t_lg, deps=[tk_xb] + list(wdeps) + bdep(4),
                          tag=f"lgm.{tag}{sub}")
            bufs["xbsub"].consume(tk_lgm)
            tk_lr = S.op("V", lambda e: (
                S.vf(e, e.tensor_copy(lrowh[0:8, :], bank(4, 0, 8))),
                e.scalar_tensor_tensor(lrowl[0:8, :], bank(4, 0, 8), 1.0,
                                       lrowh[0:8, :], ALU.mult, ALU.subtract)
            )[-1], deps=[tk_lgm] + bufs["lrow"].war(),
                tag=f"lrow.{tag}{sub}")
            set_bank(4, tk_lr)
            def t_ltr(e, sub=sub):
                for j in range(4):
                    g = sub * 4 + j
                    mm(e, bank(3, c0=8 * g, c1=8 * g + 8),
                       lrowh[:, 128 * j:128 * (j + 1)], idenb[:, 0:8],
                       start=True, stop=False)
                    i = mm(e, bank(3, c0=8 * g, c1=8 * g + 8),
                           lrowl[:, 128 * j:128 * (j + 1)], idenb[:, 0:8],
                           start=False, stop=True)
                return i
            tk_ltr = S.op("T", t_ltr,
                          deps=[tk_lr, tk_idenb]
                          + (bdep(3) if sub == 0 else []),
                          tag=f"ltr.{tag}{sub}")
            bufs["lrow"].consume(tk_ltr)
            logit_tr.append(tk_ltr)

        tk_stats = emit_stats(xdeps, extra=sweep1_extra, tag=tag)

        l3 = gl[:].rearrange("p (g e) -> p g e", e=8)
        g3 = gtmp[:].rearrange("p (g e) -> p g e", e=8)
        w3 = gw[:].rearrange("p (g e) -> p g e", e=8)
        def v_gfix(e):
            rstd3 = statc[:, C_RSTD:C_RSTD + 16].unsqueeze(2).broadcast_to(
                [128, 16, 8])
            mr3 = statc[:, C_MR:C_MR + 16].unsqueeze(2).broadcast_to(
                [128, 16, 8])
            cs3 = csbg[:, 0:8].unsqueeze(1).broadcast_to([128, 16, 8])
            bg3 = csbg[:, 8:16].unsqueeze(1).broadcast_to([128, 16, 8])
            p3 = bank(3, c0=0, c1=128).rearrange("p (g e) -> p g e", e=8)
            S.vf(e, e.tensor_tensor(l3, p3, rstd3, ALU.mult))
            S.vf(e, e.tensor_tensor(g3, mr3, cs3, ALU.mult))
            S.vf(e, e.tensor_tensor(l3, l3, g3, ALU.add))
            S.vf(e, e.tensor_tensor(l3, l3, bg3, ALU.add))
            return vtail(e)
        tk_gfix = S.op("V", v_gfix, deps=logit_tr + [tk_stats, tk_csbg],
                       tag=f"gfix.{tag}")
        set_bank(3, tk_gfix)
        S.dbg[f"gfix.{tag}"] = tk_gfix
        S.dbg[f"stats.{tag}"] = tk_stats
        set_bank(3, tk_gfix)
        def v_gate(e):
            m1 = statc[:, C_T0:C_T0 + 16]
            m2 = statc[:, C_T1:C_T1 + 16]
            S.vf(e, e.tensor_reduce(m1, l3, AX.X, ALU.max))
            S.vf(e, e.tensor_tensor(
                g3, l3, m1.unsqueeze(2).broadcast_to([128, 16, 8]), ALU.is_ge))
            S.vf(e, e.scalar_tensor_tensor(w3, g3, -BIG, l3, ALU.mult,
                                           ALU.add))
            S.vf(e, e.tensor_reduce(m2, w3, AX.X, ALU.max))
            S.vf(e, e.tensor_tensor(
                w3, w3, m2.unsqueeze(2).broadcast_to([128, 16, 8]), ALU.is_ge))
            S.vf(e, e.tensor_tensor(statc[:, C_T2:C_T2 + 16], m1, m2,
                                    ALU.subtract))
            return vtail(e)
        tk_g1 = S.op("V", v_gate, deps=[tk_gfix], tag=f"gate.{tag}")
        tk_sg = S.op("S", lambda e: (
            e.activation(out=statc[:, C_T0:C_T0 + 16],
                         in_=statc[:, C_T2:C_T2 + 16], func=ACT.Sigmoid),
            e.activation(out=statc[:, C_T1:C_T1 + 16],
                         in_=statc[:, C_T2:C_T2 + 16], func=ACT.Sigmoid,
                         scale=-1.0))[-1], deps=[tk_g1], tag=f"sig.{tag}")
        def v_comb(e):
            S.vf(e, e.tensor_tensor(w3, w3, statc[:, C_T1:C_T1 + 16]
                                    .unsqueeze(2).broadcast_to([128, 16, 8]),
                                    ALU.mult))
            S.vf(e, e.tensor_tensor(g3, g3, statc[:, C_T0:C_T0 + 16]
                                    .unsqueeze(2).broadcast_to([128, 16, 8]),
                                    ALU.mult))
            S.vf(e, e.tensor_tensor(w3, w3, g3, ALU.add))
            i = e.tensor_copy(gwb[:, 0:128], gw[:])
            return i
        tk_w = S.op("V", v_comb, deps=[tk_sg], tag=f"gw.{tag}")
        S.dbg[f"gw.{tag}"] = tk_w

        res_toks = []
        for sub in range(NSUB):
            t0 = sub * 512
            tk_rows, _ = emit_rows(sub, tk_stats, tag=tag)
            tk_xs = emit_xs(sub, tk_rows, xssub, "xssub", tag=tag)
            def t_w1(e):
                for rch in range(4):
                    for ch in range(DCH):
                        mm(e, bank(rch),
                           w1t[:, ch * 512 + 128 * rch: ch * 512 + 128 * rch + 128],
                           xssub[:, ch * 512:(ch + 1) * 512],
                           start=ch == 0, stop=False)
                    for j in range(4):
                        i = mm(e, bank(rch, c0=128 * j, c1=128 * (j + 1)),
                               aux_w1[:, 128 * rch:128 * (rch + 1)],
                               aux4[j][:], start=False, stop=j == 3)
                return i
            tk_w1m = S.op("T", t_w1,
                          deps=[tk_xs, tk_rows, tk_csw1] + list(wdeps)
                          + bdep(0, 1, 2, 3), tag=f"w1m.{tag}{sub}")
            bufs["xssub"].consume(tk_w1m)
            bufs["aux4"].consume(tk_w1m)
            tk_gel = []
            for rch in range(4):
                tk_g = S.op("S", lambda e, rch=rch: e.activation(
                    out=hg[:, rch * 512:(rch + 1) * 512], in_=bank(rch),
                    func=ACT.Gelu),
                    deps=[tk_w1m] + (bufs["hg"].war() if rch == 0 else []),
                    tag=f"gelu.{tag}{sub}{rch}")
                set_bank(rch, tk_g)
                tk_gel.append(tk_g)
            tk_hp = []
            for j in range(4):
                g = sub * 4 + j
                tk_gt = S.op("T", lambda e, g=g: mm(
                    e, bank(4, c0=0, c1=128), gwb[:, 8 * g:8 * g + 128],
                    idenb[:], start=True), deps=[tk_w, tk_idenb] + bdep(4),
                    tag=f"gwt.{tag}{sub}{j}")
                tk_wc = S.op("V", lambda e: e.tensor_copy(
                    wtg128[0:8, :], bank(4, 0, 8, 0, 128)),
                    deps=[tk_gt] + bufs["wtg"].war(), tag=f"wtg.{tag}{sub}{j}")
                set_bank(4, tk_wc)
                def t_sel(e):
                    for rch in range(4):
                        i = mm(e, bank(5, c0=128 * rch, c1=128 * (rch + 1)),
                               selb[:, 128 * rch:128 * (rch + 1)], wtg128[:],
                               start=True)
                    return i
                tk_sel = S.op("T", t_sel, deps=[tk_wc, tk_selb] + bdep(5),
                              tag=f"sel.{tag}{sub}{j}")
                bufs["wtg"].consume(tk_sel)
                def v_h(e, j=j):
                    for rch in range(4):
                        i = e.tensor_tensor(
                            hp[:, rch * 512 + 128 * j: rch * 512 + 128 * j + 128],
                            hg[:, rch * 512 + 128 * j: rch * 512 + 128 * j + 128],
                            bank(5, c0=128 * rch, c1=128 * (rch + 1)), ALU.mult)
                    return i
                tk_h = S.op("V", v_h,
                            deps=[tk_sel] + tk_gel
                            + (bufs["hp"].war() if j == 0 else []),
                            tag=f"hprime.{tag}{sub}{j}")
                bufs["hg"].consume(tk_h)
                set_bank(5, tk_h)
                tk_hp.append(tk_h)
            for m in range(DCH):
                fb = 6 + (m % 2)
                def t_w2(e, m=m, fb=fb):
                    for rch in range(4):
                        i = mm(e, bank(fb),
                               w2t[:, rch * 1024 + 128 * m:
                                   rch * 1024 + 128 * m + 128],
                               hp[:, rch * 512:(rch + 1) * 512],
                               start=rch == 0, stop=rch == 3)
                    return i
                tk_w2m = S.op("T", t_w2, deps=tk_hp + list(wdeps) + bdep(fb),
                              tag=f"w2m.{tag}{sub}{m}")
                bufs["hp"].consume(tk_w2m)
                def v_res(e, m=m, fb=fb, t0=t0):
                    return e.scalar_tensor_tensor(
                        sx[:, m * T + t0: m * T + t0 + 512], bank(fb), 1.0,
                        sx[:, m * T + t0: m * T + t0 + 512], ALU.mult, ALU.add)
                tk_res = S.op("V", v_res, deps=[tk_w2m],
                              tag=f"res.{tag}{sub}{m}")
                set_bank(fb, tk_res)
                res_toks.append(tk_res)
        return res_toks

    ffn1_w = [tk_w1t, tk_w2t, tk_csw1, tk_csbg, tk_consts]
    res_f1 = emit_ffn(tk_x + ffn1_w, ffn1_w, tag="f1")

    if taps:
        tap_f1 = [S.dma("dY", lambda e, ch=ch: e.dma_start(
            out=tap["tap_f1"][128 * ch:128 * (ch + 1), :],
            in_=sx[:, ch * T:(ch + 1) * T]), deps=res_f1) for ch in range(DCH)]
        tk_tap_gl1 = S.dma("dY", lambda e: e.dma_start(
            out=tap["tap_gl1"], in_=gl[:]), deps=[S.dbg["gfix.f1"]])
        tk_tap_st1 = S.dma("dY", lambda e: e.dma_start(
            out=tap["tap_st1"], in_=statc[:, 0:48]), deps=[S.dbg["stats.f1"]])
        tk_tap_gw1 = S.dma("dY", lambda e: e.dma_start(
            out=tap["tap_gw1"], in_=gw[:]), deps=[S.dbg["gw.f1"]])
        res_f1 = res_f1 + [tk_tap_gl1, tk_tap_st1, tk_tap_gw1]

    tk_ln2 = emit_stats(res_f1, tag="ln2")

    tk_wq = S.dma("dG", lambda e: e.dma_start(
        out=wqkv[:, 0:DCH * 1024].rearrange("p (c f) -> p c f", c=DCH),
        in_=wqkv_d[:, 0:1024].rearrange("(c p) f -> p c f", p=128)),
        deps=res_f1, tag="ldwq")
    tk_bq = S.dma("dG", lambda e: e.dma_start(
        out=aux_w[0:1, 0:1024], in_=bqkv_d[:, 0:1024]),
        deps=[tk_consts] + bufs["aux_w"].war(), tag="ldbq")

    tk_csq = []
    for fs in range(2):
        def t_cs(e, fs=fs):
            for ch in range(DCH):
                i = mm(e, bank(7), onescol[:],
                       wqkv[:, ch * 1024 + 512 * fs: ch * 1024 + 512 * fs + 512],
                       start=ch == 0)
            return i
        tk_m = S.op("T", t_cs, deps=[tk_wq, tk_consts] + bdep(7),
                    tag=f"csq{fs}")
        tk_c = S.op("V", lambda e, fs=fs: e.tensor_copy(
            aux_w[32:33, 512 * fs:512 * fs + 512], bank(7, 0, 1)),
            deps=[tk_m, tk_bq], tag=f"csqc{fs}")
        set_bank(7, tk_c)
        tk_csq.append(tk_c)

    tk_q2_dma = []
    tk_qmm = []
    prev_sq = None
    for sub in range(NSUB):
        tk_rows, _ = emit_rows(sub, tk_ln2, tag="q")
        tk_xs2 = emit_xs(sub, tk_rows, xs2, "xs2", tag="q")
        for j in range(4):
            g = sub * 4 + j
            b0 = 0 if g % 2 == 0 else 2
            def t_q(e, j=j, b0=b0):
                for ch in range(DCH):
                    lh = xs2[:, ch * 512 + 128 * j: ch * 512 + 128 * j + 128]
                    mm(e, bank(b0), lh, wqkv[:, ch * 1024:ch * 1024 + 512],
                       start=ch == 0, stop=False)
                    mm(e, bank(b0 + 1), lh,
                       wqkv[:, ch * 1024 + 512:ch * 1024 + 1024],
                       start=ch == 0, stop=False)
                ax = aux4[j][:]
                mm(e, bank(b0), ax, aux_w[:, 0:512], start=False)
                i = mm(e, bank(b0 + 1), ax, aux_w[:, 512:1024], start=False)
                return i
            tk_qm = S.op("T", t_q, deps=[tk_xs2, tk_rows, tk_bq] + tk_csq
                         + bdep(b0, b0 + 1), tag=f"qmm{g}")
            bufs["xs2"].consume(tk_qm)
            bufs["aux4"].consume(tk_qm)
            bufs["aux_w"].consume(tk_qm)
            tk_qmm.append(tk_qm)
            qp = pall[:, 512 * b0:512 * b0 + 1024]
            def v_mq(e, qp=qp):
                return e.tensor_reduce(statc[:, C_G:C_G + 1], qp, AX.X,
                                       ALU.add, negate=True)
            tk_mq = S.op("V", v_mq,
                         deps=[tk_qm] + ([prev_sq] if prev_sq else []),
                         tag=f"mq{g}")
            tk_sq = S.op("S", lambda e, qp=qp: e.activation(
                out=q2st[:], in_=qp, func=ACT.Square,
                bias=statc[:, C_G:C_G + 1], scale=float(D),
                accum_out=statc[:, C_G + 1:C_G + 2]),
                deps=[tk_mq] + bufs["q2st"].war(), tag=f"q2sq{g}")
            prev_sq = tk_sq
            set_bank(b0, tk_sq)
            set_bank(b0 + 1, tk_sq)
            tk_t4 = S.op("S", lambda e: e.activation(
                out=u2st[:], in_=q2st[:], func=ACT.Square,
                accum_out=statc[:, C_G + 2:C_G + 3]),
                deps=[tk_sq] + bufs["u2st"].war(), tag=f"t4{g}")
            tk_rt = S.op("S", lambda e: e.activation(
                out=statc[:, C_G + 3:C_G + 4], in_=statc[:, C_G + 2:C_G + 3],
                func=ACT.Sqrt), deps=[tk_t4], tag=f"rsq{g}")
            def v_rs(e):
                S.vf(e, e.tensor_scalar(statc[:, C_G + 4:C_G + 5],
                                        statc[:, C_G + 1:C_G + 2],
                                        float(D) * D * EPS_SUM, EPS_NORM,
                                        ALU.add, ALU.mult))
                S.vf(e, e.tensor_tensor(statc[:, C_G + 4:C_G + 5],
                                        statc[:, C_G + 3:C_G + 4],
                                        statc[:, C_G + 4:C_G + 5], ALU.max))
                S.vf(e, e.reciprocal(statc[:, C_G + 5:C_G + 6],
                                     statc[:, C_G + 4:C_G + 5]))
                i = e.tensor_scalar(q2st[:], q2st[:],
                                    statc[:, C_G + 5:C_G + 6], None, ALU.mult)
                return i
            tk_rsx = S.op("V", v_rs, deps=[tk_rt], tag=f"rs{g}")
            tk_dma = S.dma("dS", lambda e, g=g: e.dma_start(
                out=q2_dram.ap()[128 * g:128 * (g + 1), :], in_=q2st[:]),
                deps=[tk_rsx], tag=f"q2dma{g}")
            bufs["q2st"].consume(tk_dma)
            bufs["u2st"].consume(tk_t4)
            tk_q2_dma.append(tk_dma)

    tk_wkv = S.dma("dG", lambda e: e.dma_start(
        out=wqkv[:].rearrange("p (c f) -> p c f", c=DCH),
        in_=wqkv_d[:, 1024:3072].rearrange("(c p) f -> p c f", p=128)),
        deps=tk_qmm + tk_csq, tag="ldwkv")
    tk_bkv = S.dma("dG", lambda e: e.dma_start(
        out=aux_w[0:1, :], in_=bqkv_d[:, 1024:3072]),
        deps=tk_qmm + bufs["aux_w"].war(), tag="ldbkv")
    tk_cskv = []
    for fs in range(4):
        def t_cs(e, fs=fs):
            for ch in range(DCH):
                i = mm(e, bank(7), onescol[:],
                       wqkv[:, ch * 2048 + 512 * fs: ch * 2048 + 512 * fs + 512],
                       start=ch == 0)
            return i
        tk_m = S.op("T", t_cs, deps=[tk_wkv, tk_consts] + bdep(7),
                    tag=f"cskv{fs}")
        tk_c = S.op("V", lambda e, fs=fs: e.tensor_copy(
            aux_w[32:33, 512 * fs:512 * fs + 512], bank(7, 0, 1)),
            deps=[tk_m, tk_bkv], tag=f"cskvc{fs}")
        set_bank(7, tk_c)
        tk_cskv.append(tk_c)

    tk_kvmm = []
    tk_uev = []
    tk_vev = []
    prev_k2 = None
    for sub in range(NSUB):
        tk_rows, _ = emit_rows(sub, tk_ln2, tag="kv")
        tk_xs2 = emit_xs(sub, tk_rows, xs2, "xs2", tag="kv")
        for j in range(4):
            g = sub * 4 + j
            b0 = 0 if g % 2 == 0 else 2
            def t_kv(e, j=j, b0=b0):
                for ch in range(DCH):
                    lh = xs2[:, ch * 512 + 128 * j: ch * 512 + 128 * j + 128]
                    mm(e, bank(b0), lh, wqkv[:, ch * 2048:ch * 2048 + 512],
                       start=ch == 0, stop=False)
                    mm(e, bank(b0 + 1), lh,
                       wqkv[:, ch * 2048 + 512:ch * 2048 + 1024],
                       start=ch == 0, stop=False)
                    mm(e, bank(4), lh,
                       wqkv[:, ch * 2048 + 1024:ch * 2048 + 1536],
                       start=ch == 0, stop=False)
                    mm(e, bank(5), lh,
                       wqkv[:, ch * 2048 + 1536:ch * 2048 + 2048],
                       start=ch == 0, stop=False)
                ax = aux4[j][:]
                mm(e, bank(b0), ax, aux_w[:, 0:512], start=False)
                mm(e, bank(b0 + 1), ax, aux_w[:, 512:1024], start=False)
                mm(e, bank(4), ax, aux_w[:, 1024:1536], start=False)
                i = mm(e, bank(5), ax, aux_w[:, 1536:2048], start=False)
                return i
            tk_km = S.op("T", t_kv, deps=[tk_xs2, tk_rows, tk_bkv] + tk_cskv
                         + bdep(b0, b0 + 1, 4, 5), tag=f"kvmm{g}")
            bufs["xs2"].consume(tk_km)
            bufs["aux4"].consume(tk_km)
            bufs["aux_w"].consume(tk_km)
            tk_kvmm.append(tk_km)
            tk_v = S.op("V", lambda e, g=g: e.tensor_copy(
                vtm[:, g * D:(g + 1) * D], pall[:, 512 * 4:512 * 6]),
                deps=[tk_km], tag=f"vev{g}")
            set_bank(4, tk_v)
            set_bank(5, tk_v)
            tk_vev.append(tk_v)
            kp = pall[:, 512 * b0:512 * b0 + 1024]
            def v_mk(e, kp=kp):
                return e.tensor_reduce(statc[:, C_G:C_G + 1], kp, AX.X,
                                       ALU.add, negate=True)
            tk_mk = S.op("V", v_mk,
                         deps=[tk_km] + ([prev_k2] if prev_k2 else []),
                         tag=f"mk{g}")
            tk_k2 = S.op("S", lambda e, kp=kp, g=g: e.activation(
                out=utm[:, g * D:(g + 1) * D], in_=kp, func=ACT.Square,
                bias=statc[:, C_G:C_G + 1], scale=float(D),
                accum_out=statc[:, C_G + 1:C_G + 2]),
                deps=[tk_mk], tag=f"k2{g}")
            prev_k2 = tk_k2
            set_bank(b0, tk_k2)
            set_bank(b0 + 1, tk_k2)
            def v_u(e, g=g):
                S.vf(e, e.tensor_scalar(statc[:, C_G + 2:C_G + 3],
                                        statc[:, C_G + 1:C_G + 2],
                                        float(D) * D * EPS_SUM, None, ALU.add))
                S.vf(e, e.reciprocal(statc[:, C_G + 3:C_G + 4],
                                     statc[:, C_G + 2:C_G + 3]))
                i = e.tensor_scalar(utm[:, g * D:(g + 1) * D],
                                    utm[:, g * D:(g + 1) * D],
                                    statc[:, C_G + 3:C_G + 4], None, ALU.mult)
                return i
            tk_u = S.op("V", v_u, deps=[tk_k2], tag=f"uev{g}")
            tk_uev.append(tk_u)

    if taps:
        tk_tap_u = S.dma("dG", lambda e: e.dma_start(
            out=tap["tap_u"].rearrange("(g p) c -> p g c", p=128),
            in_=utm[:].rearrange("p (g c) -> p g c", g=NG)), deps=tk_uev)
        tk_tap_v = S.dma("dG", lambda e: e.dma_start(
            out=tap["tap_v"].rearrange("(g p) c -> p g c", p=128),
            in_=vtm[:].rearrange("p (g c) -> p g c", g=NG)), deps=tk_vev)

    tk_kvt_ev = []
    kv_mm_toks = []
    for chalf in range(2):
        def t_kvp(e, chalf=chalf):
            for g in range(NG):
                for dch in range(DCH):
                    i = mm(e, bank(dch),
                           vtm[:, g * D + 128 * dch: g * D + 128 * dch + 128],
                           utm[:, g * D + 512 * chalf: g * D + 512 * chalf + 512],
                           start=g == 0, stop=g == NG - 1)
            return i
        tk_kvm = S.op("T", t_kvp, deps=tk_uev + tk_vev + bdep(*range(8)),
                      tag=f"kvp{chalf}")
        kv_mm_toks.append(tk_kvm)
        def v_kvev(e, chalf=chalf):
            for dch in range(DCH):
                i = e.tensor_copy(
                    kvtb[:, dch * 1024 + 512 * chalf:
                         dch * 1024 + 512 * chalf + 512], bank(dch))
            return i
        tk_ev = S.op("V", v_kvev, deps=[tk_kvm] + tk_kvmm + tk_qmm,
                     tag=f"kvtev{chalf}")
        for dch in range(DCH):
            set_bank(dch, tk_ev)
        tk_kvt_ev.append(tk_ev)

    tk_s2mm = []
    tk_u2_last = None
    for g in range(NG):
        tk_u2 = S.op("S", lambda e, g=g: e.activation(
            out=u2st[:], in_=utm[:, g * D:(g + 1) * D], func=ACT.Square),
            deps=[tk_uev[g]] + bufs["u2st"].war()
            + ([tk_s2mm[-1]] if tk_s2mm else []), tag=f"u2.{g}")
        tk_u2_last = tk_u2
        def t_s2(e, g=g):
            mm(e, bank(6), onescol[:], u2st[:, 0:512], start=g == 0,
               stop=g == NG - 1)
            i = mm(e, bank(7), onescol[:], u2st[:, 512:1024], start=g == 0,
                   stop=g == NG - 1)
            return i
        tk_mm2 = S.op("T", t_s2,
                      deps=[tk_u2, tk_consts] + (bdep(6, 7) if g == 0 else []),
                      tag=f"s2mm{g}")
        bufs["u2st"].consume(tk_mm2)
        tk_s2mm.append(tk_mm2)
    tk_s2ev = S.op("V", lambda e: (
        e.tensor_copy(arstage[0:1, 0:512], bank(6, 0, 1)),
        e.tensor_copy(arstage[0:1, 512:1024], bank(7, 0, 1)))[-1],
        deps=[tk_s2mm[-1]] + bufs["arstage"].war(), tag="s2ev")
    set_bank(6, tk_s2ev)
    set_bank(7, tk_s2ev)
    tk_s2dma = S.dma("dS", lambda e: e.dma_start(
        out=ar_in.ap()[D:D + 1, :], in_=arstage[0:1, 0:1024]),
        deps=[tk_s2ev], tag="s2dma")
    bufs["arstage"].consume(tk_s2dma)

    tk_wln = S.dma("dG", lambda e: e.dma_start(
        out=wlns[:].rearrange("p (c f) -> p c f", c=DCH),
        in_=wln_d.rearrange("(c p) f -> p c f", p=128)),
        deps=tk_kvmm + tk_qmm + tk_cskv, tag="ldwln")

    ar_dma = [tk_s2dma]
    tk_kvw_mm = []
    for cch in range(DCH):
        for dhalf in range(2):
            b = 2 + (cch * 2 + dhalf) % 2
            def t_kvw(e, cch=cch, dhalf=dhalf, b=b):
                for dch in range(DCH):
                    i = mm(e, bank(b),
                           kvtb[:, dch * 1024 + 128 * cch:
                                dch * 1024 + 128 * cch + 128],
                           wlns[:, dch * 1024 + 512 * dhalf:
                                dch * 1024 + 512 * dhalf + 512],
                           start=dch == 0, stop=dch == DCH - 1)
                return i
            tk_m = S.op("T", t_kvw, deps=tk_kvt_ev + [tk_wln] + bdep(b),
                        tag=f"kvw{cch}.{dhalf}")
            tk_kvw_mm.append(tk_m)
            tk_st = S.op("V", lambda e, b=b: e.tensor_copy(
                arstage[:, 0:512], bank(b)),
                deps=[tk_m] + bufs["arstage"].war(), tag=f"kvwst{cch}.{dhalf}")
            set_bank(b, tk_st)
            tk_d = S.dma("dS", lambda e, cch=cch, dhalf=dhalf: e.dma_start(
                out=ar_in.ap()[128 * cch:128 * (cch + 1),
                               512 * dhalf:512 * dhalf + 512],
                in_=arstage[:, 0:512]), deps=[tk_st],
                tag=f"kvwdma{cch}.{dhalf}")
            bufs["arstage"].consume(tk_d)
            ar_dma.append(tk_d)

    if taps:
        tk_tap_arin = S.dma("dG", lambda e: e.dma_start(
            out=tap["tap_arin"], in_=ar_in.ap()), deps=ar_dma)
        tk_tap_kvt = S.dma("dG", lambda e: e.dma_start(
            out=tap["tap_kvt"].rearrange("(c p) f -> p c f", p=128),
            in_=kvtb[:].rearrange("p (c f) -> p c f", c=DCH)), deps=tk_kvt_ev)

    tk_cc = S.op("G", lambda e: e.collective_compute(
        "AllReduce", ALU.add, replica_groups=[[0, 1, 2, 3], [4, 5, 6, 7]],
        ins=[ar_in.ap().opt()], outs=[ar_out.ap().opt()]),
        deps=ar_dma + ([tk_tap_arin] if taps else []), tag="allreduce")

    vf_war = kv_mm_toks + [tk_u2_last] + ([tk_tap_u, tk_tap_v] if taps else [])
    tk_vf = []
    for g in range(NG):
        for dch in range(DCH):
            tk_vf.append(S.dma("dY", lambda e, g=g, dch=dch: e.dma_start_transpose(
                out=vF[:, dch * T + 128 * g: dch * T + 128 * g + 128],
                in_=vtm[:, g * D + 128 * dch: g * D + 128 * dch + 128]),
                deps=vf_war, tag=f"vf{g}.{dch}"))

    tk_vw_res = []
    for sub in range(NSUB):
        t0 = sub * 512
        vf_dep = [tk_vf[(sub * 4 + j) * DCH + dch]
                  for j in range(4) for dch in range(DCH)]
        for m in range(DCH):
            b = m % 2
            def t_vw(e, m=m, sub=sub, b=b):
                for dch in range(DCH):
                    i = mm(e, bank(b),
                           wlns[:, dch * 1024 + 128 * m:
                                dch * 1024 + 128 * m + 128],
                           vF[:, dch * T + 512 * sub: dch * T + 512 * sub + 512],
                           start=dch == 0, stop=dch == DCH - 1)
                return i
            tk_m = S.op("T", t_vw, deps=vf_dep + [tk_wln] + bdep(b),
                        tag=f"vw{sub}.{m}")
            def v_r(e, m=m, t0=t0, b=b):
                return e.scalar_tensor_tensor(
                    sx[:, m * T + t0: m * T + t0 + 512], bank(b),
                    blnc[:, m:m + 1], sx[:, m * T + t0: m * T + t0 + 512],
                    ALU.add, ALU.add)
            tk_r = S.op("V", v_r, deps=[tk_m, tk_blnc], tag=f"vwres{sub}.{m}")
            set_bank(b, tk_r)
            tk_vw_res.append(tk_r)

    qf_war = kv_mm_toks + [tk_vf[-1]] + ([tk_tap_v] if taps else [])
    tk_qf = []
    for g in range(NG):
        for cch in range(DCH):
            tk_qf.append(S.dma("dY", lambda e, g=g, cch=cch: e.dma_start_transpose(
                out=q2F[:, cch * T + 128 * g: cch * T + 128 * g + 128],
                in_=q2_dram.ap()[128 * g:128 * (g + 1),
                                 128 * cch:128 * (cch + 1)]),
                deps=qf_war + [tk_q2_dma[g]], tag=f"qf{g}.{cch}"))

    tk_s2l = S.dma("dS", lambda e: e.dma_start(
        out=arstage[0:1, :], in_=ar_out.ap()[D:D + 1, :]),
        deps=[tk_cc] + bufs["arstage"].war(), tag="lds2")
    tk_rcp_rows = []
    for r in range(2):
        tk_rr = S.op("V", lambda e, r=r: (
            S.vf(e, e.tensor_copy(trh[0:1, :],
                                  arstage[0:1, 512 * r:512 * r + 512])),
            e.scalar_tensor_tensor(trl[0:1, :],
                                   arstage[0:1, 512 * r:512 * r + 512], 1.0,
                                   trh[0:1, :], ALU.mult, ALU.subtract)
        )[-1], deps=[tk_s2l] + bufs["rowstage"].war(), tag=f"rcprow{r}")
        def t_rc(e, r=r):
            for j in range(4):
                mm(e, bank(6, c0=4 * r + j, c1=4 * r + j + 1),
                   trh[:, 128 * j:128 * (j + 1)], idenb[:, 0:1],
                   start=True, stop=False)
                i = mm(e, bank(6, c0=4 * r + j, c1=4 * r + j + 1),
                       trl[:, 128 * j:128 * (j + 1)], idenb[:, 0:1],
                       start=False, stop=True)
            return i
        tk_rm = S.op("T", t_rc, deps=[tk_rr, tk_idenb]
                     + (bdep(6) if r == 0 else []), tag=f"rcpmm{r}")
        bufs["rowstage"].consume(tk_rm)
        tk_rcp_rows.append(tk_rm)
    tk_rcpt = tk_rcp_rows[-1]
    bufs["arstage"].consume(tk_rcpt)
    tk_rcpc = S.op("V", lambda e: e.tensor_copy(
        statc[:, C_RCP:C_RCP + 8], bank(6, c0=0, c1=8)), deps=[tk_rcpt],
        tag="rcpcp")
    set_bank(6, tk_rcpc)
    tk_rcps = S.op("S", lambda e: e.activation(
        out=statc[:, C_RCP:C_RCP + 8], in_=statc[:, C_RCP:C_RCP + 8],
        func=ACT.Sqrt, scale=float(SEQ)), deps=[tk_rcpc], tag="rcpsqrt")
    def v_rcp(e):
        S.vf(e, e.tensor_scalar_max(statc[:, C_RCP:C_RCP + 8],
                                    statc[:, C_RCP:C_RCP + 8],
                                    EPS_NORM * math.sqrt(SEQ)))
        S.vf(e, e.reciprocal(statc[:, C_RCP:C_RCP + 8],
                             statc[:, C_RCP:C_RCP + 8]))
        return vtail(e)
    tk_rcp = S.op("V", v_rcp, deps=[tk_rcps], tag="rcp")

    tk_kvwb = []
    for cch in range(DCH):
        tk_l = S.dma("dS", lambda e, cch=cch: e.dma_start(
            out=arstage[:], in_=ar_out.ap()[128 * cch:128 * (cch + 1), :]),
            deps=[tk_s2l] + bufs["arstage"].war(), tag=f"ldkvw{cch}")
        tk_c = S.op("V", lambda e, cch=cch: e.tensor_scalar(
            kvwb[:, cch * 1024:(cch + 1) * 1024], arstage[:],
            statc[:, C_RCP + cch:C_RCP + cch + 1], None, ALU.mult),
            deps=[tk_l, tk_rcp] + (tk_kvw_mm if cch == 0 else []),
            tag=f"kvwb{cch}")
        bufs["arstage"].consume(tk_c)
        tk_kvwb.append(tk_c)

    if taps:
        tk_tap_arout = S.dma("dG", lambda e: e.dma_start(
            out=tap["tap_arout"], in_=ar_out.ap()), deps=[tk_cc])
        tk_tap_q2 = S.dma("dG", lambda e: e.dma_start(
            out=tap["tap_q2"], in_=q2_dram.ap()), deps=tk_q2_dma)

    tk_t2_res = []
    for sub in range(NSUB):
        t0 = sub * 512
        qf_dep = [tk_qf[(sub * 4 + j) * DCH + cch]
                  for j in range(4) for cch in range(DCH)]
        for m in range(DCH):
            b = 2 + m % 2
            def t_t2(e, m=m, sub=sub, b=b):
                for cch in range(DCH):
                    i = mm(e, bank(b),
                           kvwb[:, cch * 1024 + 128 * m:
                                cch * 1024 + 128 * m + 128],
                           q2F[:, cch * T + 512 * sub: cch * T + 512 * sub + 512],
                           start=cch == 0, stop=cch == DCH - 1)
                return i
            tk_m = S.op("T", t_t2, deps=qf_dep + tk_kvwb + bdep(b),
                        tag=f"t2.{sub}.{m}")
            def v_r(e, m=m, t0=t0, b=b):
                return e.scalar_tensor_tensor(
                    sx[:, m * T + t0: m * T + t0 + 512], bank(b), 1.0,
                    sx[:, m * T + t0: m * T + t0 + 512], ALU.mult, ALU.add)
            tk_r = S.op("V", v_r, deps=[tk_m], tag=f"t2res{sub}.{m}")
            set_bank(b, tk_r)
            tk_t2_res.append(tk_r)

    if taps:
        tap_x2 = [S.dma("dY", lambda e, ch=ch: e.dma_start(
            out=tap["tap_x2"][128 * ch:128 * (ch + 1), :],
            in_=sx[:, ch * T:(ch + 1) * T]), deps=tk_t2_res + tk_vw_res)
            for ch in range(DCH)]

    tk_w1t2 = S.dma("dG", lambda e: e.dma_start(
        out=w1t[:].rearrange("p (c r) -> p c r", c=DCH),
        in_=w1t_d.rearrange("(c p) r -> p c r", p=128)),
        deps=tk_t2_res + tk_vw_res + tk_kvw_mm, tag="ldw1t2")
    tk_w2t2 = S.dma("dG", lambda e: e.dma_start(
        out=w2t[:].rearrange("p (c d) -> p c d", c=4),
        in_=w2t_d.rearrange("(c p) d -> p c d", p=128)),
        deps=tk_t2_res + tk_vw_res + tk_kvw_mm, tag="ldw2t2")
    ffn2_w = [tk_w1t2, tk_w2t2, tk_csw1, tk_csbg, tk_consts]
    res_f2 = emit_ffn(tk_t2_res + tk_vw_res + [tk_w1t2, tk_w2t2], ffn2_w,
                      tag="f2")

    tk_out = [S.dma("dY", lambda e, ch=ch: e.dma_start(
        out=out_t[128 * ch:128 * (ch + 1), :],
        in_=sx[:, ch * T:(ch + 1) * T]), deps=res_f2, tag=f"out{ch}")
        for ch in range(DCH)]
    if taps:
        tk_out += [tk_tap_arout, tk_tap_q2, tk_tap_kvt] + tap_f1 + tap_x2
        tk_tap_st = S.dma("dY", lambda e: e.dma_start(
            out=tap["tap_stat"], in_=statc[:]), deps=res_f2)
        tk_tap_gw = S.dma("dY", lambda e: e.dma_start(
            out=tap["tap_gw"], in_=gw[:]), deps=res_f2)
        tk_out += [tk_tap_st, tk_tap_gw]
    S.op("Y", lambda e: e.nop(), deps=tk_out, tag="final")

    S.prepare(nc, serialize=serialize, ser_range=ser_range)
    with nc.Block() as block:
        @block.tensor
        def _(eng):
            S.emit_engine("T", eng)
        @block.vector
        def _(eng):
            S.emit_engine("V", eng)
        @block.scalar
        def _(eng):
            S.emit_engine("S", eng)
        @block.gpsimd
        def _(eng):
            S.emit_engine("G", eng)
        @block.sync
        def _(eng):
            S.emit_engine("Y", eng)
    return nc


def host_inputs(inputs, core):
    x = inputs["L0_attn_x"]
    b, s = core // 4, core % 4
    xt = np.ascontiguousarray(np.asarray(x)[b, s * T:(s + 1) * T, :].T)
    sel = np.zeros((128, 512), dtype=ml_dtypes.bfloat16)
    for r in range(512):
        sel[r // 64, r] = 1.0
    jsel = np.zeros((128, 512), dtype=ml_dtypes.bfloat16)
    for n in range(512):
        jsel[32 * (n // 128), n] = 1.0
    W1 = np.asarray(inputs["L0_ffn_W1"], dtype=np.float32)
    W2 = np.asarray(inputs["L0_ffn_W2"], dtype=np.float32)
    Wg = np.asarray(inputs["L0_ffn_Wg"], dtype=np.float32)
    wgt = np.zeros((D, 128), dtype=np.float32)
    wgt[:, 0:8] = Wg.T
    return {
        "xt": xt.astype(np.float32),
        "wqkv": np.ascontiguousarray(np.asarray(inputs["L0_attn_Wqkv"],
                                                dtype=np.float32)),
        "bqkv": np.asarray(inputs["L0_attn_bqkv"],
                           dtype=np.float32).reshape(1, 3 * D),
        "wln": np.ascontiguousarray(np.asarray(inputs["L0_attn_Wln"],
                                               dtype=np.float32)),
        "blnc": np.ascontiguousarray(
            np.asarray(inputs["L0_attn_bln"], dtype=np.float32)
            .reshape(8, 128).T),
        "w1t": np.ascontiguousarray(W1.reshape(512, D).T),
        "w2t": np.ascontiguousarray(W2.transpose(0, 2, 1).reshape(512, D)),
        "wgt": wgt,
        "bg": np.asarray(inputs["L0_ffn_bg"], dtype=np.float32).reshape(1, 8),
        "iden32": np.eye(128, dtype=np.float32),
        "idenb": np.eye(128, dtype=ml_dtypes.bfloat16),
        "selb": sel,
        "jsel": jsel,
    }


def kernel(**inputs):
    nc = build(taps=False, serialize=True)
    in_maps = [host_inputs(inputs, c) for c in range(8)]
    res = run_bass_kernel_spmd(nc, in_maps, core_ids=list(range(8)))
    out = np.empty((2, SEQ, D), dtype=np.float32)
    for c in range(8):
        b, s = c // 4, c % 4
        out[b, s * T:(s + 1) * T, :] = res.results[c]["out_t"].T
    return out


# version 02
# speedup vs baseline: 1.5066x; 1.5066x over previous
import math
import numpy as np
import ml_dtypes

import concourse.bass as bass
import concourse.mybir as mybir
from concourse.bass_utils import run_bass_kernel_spmd

F32 = mybir.dt.float32
BF16 = mybir.dt.bfloat16
AX = mybir.AxisListType
ALU = mybir.AluOpType
ACT = mybir.ActivationFunctionType

D = 1024
T = 2048
NG = 16
NSUB = 4
SEQ = 8192
DCH = 8
EPS_LN = 1e-5
EPS_SUM = 1e-7
EPS_NORM = 1e-12
BIG = 1e30


class Buf:
    def __init__(self):
        self.cons = []

    def consume(self, tok):
        self.cons.append(tok)

    def war(self):
        c, self.cons = self.cons, []
        return c


class Sched:
    DMAQ = {"dY": "Y", "dS": "S", "dG": "G"}

    def __init__(self):
        self.ops = []

    def op(self, eng, fn, deps=(), tag=""):
        self.ops.append(dict(eng=eng, fn=fn, deps=sorted(set(deps)), q=eng,
                             step=1, tag=tag))
        return len(self.ops) - 1

    def dma(self, q, fn, deps=(), tag=""):
        self.ops.append(dict(eng=self.DMAQ[q], fn=fn, deps=sorted(set(deps)),
                             q=q, step=16, tag=tag))
        return len(self.ops) - 1

    def prepare(self, nc, serialize=False, ser_range=None):
        self.serialize = serialize
        if serialize:
            ser_range = (0, len(self.ops))
        if ser_range is not None:
            lo, hi = ser_range
            for i in range(max(1, lo), min(hi, len(self.ops))):
                self.ops[i]["deps"] = sorted(set(self.ops[i]["deps"] + [i - 1]))
            self.serialize = True

        self.fsem = nc.alloc_semaphore(name="vfence")
        self.fcount = [0]
        n = len(self.ops)
        needed = set()
        for o in self.ops:
            for d in o["deps"]:
                assert 0 <= d < n
                needed.add(d)
        for i, o in enumerate(self.ops):
            if o["step"] == 16:
                needed.add(i)
        self.qs = {}
        for i, o in enumerate(self.ops):
            q = o["q"]
            if q not in self.qs:
                self.qs[q] = dict(count=0, sem=nc.alloc_semaphore(name=f"q{q}"))
            o["inc"] = i in needed
            if o["inc"]:
                self.qs[q]["count"] += o["step"]
            o["after"] = (q, self.qs[q]["count"])

    def vf(self, eng, inst):
        self.fcount[0] += 1
        inst.then_inc(self.fsem, 1)
        eng.wait_ge(self.fsem, self.fcount[0])
        return inst

    def emit_engine(self, ename, eng):
        for i, o in enumerate(self.ops):
            if o["eng"] != ename:
                continue
            waits = {}
            for d in self.ops[i]["deps"]:
                do = self.ops[d]
                if (do["q"] == o["q"] and o["step"] == 1 and do["step"] == 1
                        and not self.serialize):
                    continue
                dq, dv = do["after"]
                waits[dq] = max(waits.get(dq, 0), dv)
            for wq in sorted(waits):
                eng.wait_ge(self.qs[wq]["sem"], waits[wq])
            inst = o["fn"](eng)
            if o["inc"]:
                assert inst is not None, f"op {i} ({o['tag']}) returned None"
                inst.then_inc(self.qs[o["q"]]["sem"], o["step"])


def build(taps=False, serialize=False, ser_range=None):
    nc = bass.Bass("TRN2", target_bir_lowering=False, debug=False)

    xt = nc.dram_tensor("xt", [D, T], F32, kind="ExternalInput").ap()
    wqkv_d = nc.dram_tensor("wqkv", [D, 3 * D], F32, kind="ExternalInput").ap()
    bqkv_d = nc.dram_tensor("bqkv", [1, 3 * D], F32, kind="ExternalInput").ap()
    wln_d = nc.dram_tensor("wln", [D, D], F32, kind="ExternalInput").ap()
    blnc_d = nc.dram_tensor("blnc", [128, 8], F32, kind="ExternalInput").ap()
    w1t_d = nc.dram_tensor("w1t", [D, 512], F32, kind="ExternalInput").ap()
    w2t_d = nc.dram_tensor("w2t", [512, D], F32, kind="ExternalInput").ap()
    wgt_d = nc.dram_tensor("wgt", [D, 128], F32, kind="ExternalInput").ap()
    bg_d = nc.dram_tensor("bg", [1, 8], F32, kind="ExternalInput").ap()
    iden32_d = nc.dram_tensor("iden32", [128, 128], F32, kind="ExternalInput").ap()
    idenb_d = nc.dram_tensor("idenb", [128, 128], BF16, kind="ExternalInput").ap()
    selb_d = nc.dram_tensor("selb", [128, 512], BF16, kind="ExternalInput").ap()
    jsel_d = nc.dram_tensor("jsel", [128, 512], BF16, kind="ExternalInput").ap()
    out_t = nc.dram_tensor("out_t", [D, T], F32, kind="ExternalOutput").ap()

    q2_dram = nc.dram_tensor("q2_dram", [T, D], BF16, kind="Internal")
    ar_in = nc.dram_tensor("ar_in", [D + 1, D], F32, kind="Internal")
    ar_out = nc.dram_tensor("ar_out", [D + 1, D], F32, kind="Internal")

    tap = {}
    if taps:
        for name, shape, dt in [
            ("tap_f1", [D, T], F32), ("tap_x2", [D, T], F32),
            ("tap_u", [T, D], F32), ("tap_v", [T, D], F32),
            ("tap_q2", [T, D], F32), ("tap_arin", [D + 1, D], F32),
            ("tap_arout", [D + 1, D], F32), ("tap_kvt", [D, D], F32),
            ("tap_stat", [128, 112], F32), ("tap_gw", [128, 128], F32),
            ("tap_gl1", [128, 128], F32), ("tap_st1", [128, 48], F32),
            ("tap_gw1", [128, 128], F32),
        ]:
            tap[name] = nc.dram_tensor(name, shape, dt, kind="ExternalOutput").ap()

    arena = nc.alloc_sbuf_tensor("arena", [128, 208500], mybir.dt.uint8)
    base = nc.lookup_mloc(arena).addr
    off = [0]
    def sball(name, shape, dtype):
        nbytes = int(np.prod(shape[1:])) * mybir.dt.size(dtype)
        t = nc.alloc_sbuf_tensor_at(name, shape, dtype, offset=base + off[0])
        off[0] += (nbytes + 31) & ~31
        return t

    sx = sball("sx", [128, DCH * T], F32)
    wgt = sball("wgt", [128, DCH * 128], BF16)
    onescol = sball("onescol", [128, 128], BF16)
    onesrow = sball("onesrow", [128, 128], BF16)
    iden32 = sball("iden32", [128, 128], F32)
    idenb = sball("idenb", [128, 128], BF16)
    selb = sball("selb", [128, 512], BF16)
    aux_w = sball("aux_w", [128, 2048], BF16)
    aux_w1 = sball("aux_w1", [128, 512], BF16)
    bgrow = sball("bgrow", [1, 8], BF16)
    cswgrow = sball("cswgrow", [1, 8], BF16)
    aux4 = [sball(f"aux4_{j}", [128, 128], BF16) for j in range(4)]
    trh = sball("trh", [128, 512], BF16)
    trl = sball("trl", [128, 512], BF16)
    trC = sball("trC", [128, 256], BF16)
    r4pad = sball("r4pad", [128, 128], BF16)
    jsel = sball("jsel", [128, 512], BF16)
    onescolB = sball("onescolB", [128, 128], BF16)
    blnc = sball("blnc", [128, 8], F32)
    czero = sball("czero", [128, 1], F32)
    vdmy = sball("vdmy", [128, 2], F32)
    cepsln = sball("cepsln", [128, 1], F32)
    statc = sball("statc", [128, 112], F32)
    gl = sball("gl", [128, 128], F32)
    gtmp = sball("gtmp", [128, 128], F32)
    gw = sball("gw", [128, 128], F32)
    gwb = sball("gwb", [128, 256], BF16)
    csbg = sball("csbg", [128, 16], F32)
    lrowh = sball("lrowh", [128, 512], BF16)
    lrowl = sball("lrowl", [128, 512], BF16)
    wtg128 = sball("wtg128", [128, 128], BF16)
    arstage = sball("arstage", [128, 1024], F32)
    u2st = sball("u2st", [128, 1024], BF16)
    q2st = sball("q2st", [128, 1024], BF16)
    xs2 = sball("xs2", [128, DCH * 512], BF16)
    shared = sball("shared", [128, 38912], mybir.dt.uint8)
    vtm = sball("vtm", [128, NG * D], BF16)
    utm = sball("utm", [128, NG * D], BF16)
    assert off[0] <= 208500, f"SBUF over budget: {off[0]}"

    shbase = nc.lookup_mloc(shared).addr
    def shv(name, shape, dtype, byte_off):
        nbytes = int(np.prod(shape[1:])) * mybir.dt.size(dtype)
        assert byte_off + nbytes <= 38912
        return nc.alloc_sbuf_tensor_at(name, shape, dtype,
                                       offset=shbase + byte_off)

    w1t = shv("w1t", [128, DCH * 512], BF16, 0)
    w2t = shv("w2t", [128, 4 * 1024], BF16, 8192)
    xbsub = shv("xbsub", [128, DCH * 512], BF16, 16384)
    xqsub = shv("xqsub", [128, DCH * 512], BF16, 24576)
    xssub = shv("xssub", [128, DCH * 512], BF16, 16384)
    hg = shv("hg", [128, 4 * 512], F32, 24576)
    hp = shv("hp", [128, 4 * 512], BF16, 32768)
    wqkv = shv("wqkv", [128, DCH * 2048], BF16, 0)
    wlns = shv("wlns", [128, DCH * 1024], BF16, 0)
    kvtb = shv("kvtb", [128, DCH * 1024], BF16, 16384)
    kvwb = shv("kvwb", [128, DCH * 1024], BF16, 16384)

    vF = nc.alloc_sbuf_tensor_at("vF", [128, DCH * T], BF16,
                                 offset=nc.lookup_mloc(utm).addr)
    q2F = nc.alloc_sbuf_tensor_at("q2F", [128, DCH * T], BF16,
                                  offset=nc.lookup_mloc(vtm).addr)

    pall = nc.alloc_psum_tensor("pall", [128, 4096], F32)
    def bank(b, p0=0, p1=128, c0=0, c1=512):
        return pall[p0:p1, 512 * b + c0: 512 * b + c1]
    def bankb16(b, p0=0, p1=128, c0=0, c1=1024):
        return pall[p0:p1, 512 * b: 512 * (b + 1)].bitcast(BF16)[:, c0:c1]

    C_M, C_RSTD, C_MR = 0, 16, 32
    C_T0, C_T1, C_T2 = 48, 64, 80
    C_RCP, C_G = 96, 104

    S = Sched()
    S.dbg = {}
    bank_tok = {}
    def bdep(*bs):
        return [bank_tok[b] for b in bs if b in bank_tok]
    def set_bank(b, tok):
        bank_tok[b] = tok

    bufs = dict(xbsub=Buf(), xqsub=Buf(), xssub=Buf(), xs2=Buf(),
                trC=Buf(), r4pad=Buf(), aux4=Buf(), rowstage=Buf(),
                lrow=Buf(), wtg=Buf(), hg=Buf(), hp=Buf(), arstage=Buf(),
                aux_w=Buf(), cg=Buf(), q2st=Buf(), u2st=Buf())

    def mm(e, out, lhsT, rhs, start, stop=True):
        return e.matmul(out, lhsT, rhs, start=start, stop=stop)

    def vtail(e):
        return e.tensor_copy(vdmy[:, 0:1], vdmy[:, 1:2])

    tk_x = [S.dma("dY", lambda e, ch=ch: e.dma_start(
        out=sx[:, ch * T:(ch + 1) * T], in_=xt[128 * ch:128 * (ch + 1), :]),
        tag=f"ldx{ch}") for ch in range(DCH)]
    tk_iden32 = S.dma("dY", lambda e: e.dma_start(out=iden32[:], in_=iden32_d))
    tk_idenb = S.dma("dY", lambda e: e.dma_start(out=idenb[:], in_=idenb_d))
    tk_selb = S.dma("dY", lambda e: e.dma_start(out=selb[:], in_=selb_d))
    tk_jsel = S.dma("dY", lambda e: e.dma_start(out=jsel[:], in_=jsel_d))
    tk_blnc = S.dma("dY", lambda e: e.dma_start(out=blnc[:], in_=blnc_d))

    tk_w1t = S.dma("dG", lambda e: e.dma_start(
        out=w1t[:].rearrange("p (c r) -> p c r", c=DCH),
        in_=w1t_d.rearrange("(c p) r -> p c r", p=128)), tag="ldw1t")
    tk_w2t = S.dma("dG", lambda e: e.dma_start(
        out=w2t[:].rearrange("p (c d) -> p c d", c=4),
        in_=w2t_d.rearrange("(c p) d -> p c d", p=128)), tag="ldw2t")
    tk_wgt = S.dma("dG", lambda e: e.dma_start(
        out=wgt[:].rearrange("p (c f) -> p c f", c=DCH),
        in_=wgt_d.rearrange("(c p) f -> p c f", p=128)), tag="ldwgt")
    tk_bg = S.dma("dG", lambda e: e.dma_start(out=bgrow[:], in_=bg_d))

    tk_consts = S.op("V", lambda e: (
        e.memset(onescol[:], 0.0), e.memset(onescol[:, 0:1], 1.0),
        e.memset(onesrow[:], 0.0), e.memset(onesrow[0:1, :], 1.0),
        e.memset(aux4[0][:], 0.0), e.memset(aux4[1][:], 0.0),
        e.memset(aux4[2][:], 0.0), e.memset(aux4[3][:], 0.0),
        e.memset(aux4[0][0:1, :], 1.0), e.memset(aux4[1][0:1, :], 1.0),
        e.memset(aux4[2][0:1, :], 1.0), e.memset(aux4[3][0:1, :], 1.0),
        e.memset(trh[:], 0.0), e.memset(trl[:], 0.0),
        e.memset(trC[:], 0.0), e.memset(r4pad[:], 0.0),
        e.memset(onescolB[:], 0.0), e.memset(onescolB[:, 1:2], 1.0),
        e.memset(lrowh[:], 0.0), e.memset(lrowl[:], 0.0),
        e.memset(wtg128[:], 0.0), e.memset(aux_w[:], 0.0),
        e.memset(aux_w1[:], 0.0), e.memset(czero[:], 0.0),
        e.memset(vdmy[:], 0.0),
        e.memset(cepsln[:], EPS_LN))[-1], tag="consts")
    nc.const_aps.register(F32, 0.0, czero[:])

    def t_csw1(e):
        for ch in range(DCH):
            i = mm(e, bank(0), onescol[:], w1t[:, ch * 512:(ch + 1) * 512],
                   start=ch == 0)
        return i
    tk_csw1m = S.op("T", t_csw1, deps=[tk_w1t, tk_consts], tag="csw1m")
    tk_csw1 = S.op("V", lambda e: e.tensor_copy(aux_w1[32:33, :],
                                                bank(0, 0, 1)),
                   deps=[tk_csw1m], tag="csw1")
    set_bank(0, tk_csw1)

    def t_cswg(e):
        for ch in range(DCH):
            i = mm(e, bank(1, c0=0, c1=128), onescol[:],
                   wgt[:, ch * 128:(ch + 1) * 128], start=ch == 0)
        return i
    tk_cswgm = S.op("T", t_cswg, deps=[tk_wgt, tk_consts], tag="cswgm")
    tk_cswg = S.op("V", lambda e: e.tensor_copy(cswgrow[:],
                                                bank(1, 0, 1, 0, 8)),
                   deps=[tk_cswgm], tag="cswg")
    set_bank(1, tk_cswg)
    tk_csbgm = S.op("T", lambda e: (
        mm(e, bank(2, c0=0, c1=8), onesrow[0:1, :], cswgrow[:], start=True),
        mm(e, bank(2, c0=8, c1=16), onesrow[0:1, :], bgrow[:],
           start=True))[-1], deps=[tk_cswg, tk_bg, tk_consts], tag="csbgm")
    tk_csbg = S.op("V", lambda e: e.tensor_copy(csbg[:], bank(2, c0=0, c1=16)),
                   deps=[tk_csbgm], tag="csbg")
    set_bank(2, tk_csbg)

    def emit_stats(xdeps, extra=None, tag=""):
        tr_toks = []
        for sub in range(NSUB):
            t0 = sub * 512
            def v_xb(e, t0=t0):
                for ch in range(DCH):
                    i = e.tensor_copy(xbsub[:, ch * 512:(ch + 1) * 512],
                                      sx[:, ch * T + t0: ch * T + t0 + 512])
                return i
            tk_xb = S.op("V", v_xb, deps=list(xdeps) + bufs["xbsub"].war(),
                         tag=f"xb.{tag}{sub}")
            def v_xq(e, t0=t0):
                for ch in range(DCH):
                    i = e.tensor_tensor(
                        xqsub[:, ch * 512:(ch + 1) * 512],
                        sx[:, ch * T + t0: ch * T + t0 + 512],
                        sx[:, ch * T + t0: ch * T + t0 + 512], ALU.mult)
                return i
            tk_xq = S.op("V", v_xq, deps=list(xdeps) + bufs["xqsub"].war(),
                         tag=f"xq.{tag}{sub}")
            def t_sx(e):
                for ch in range(DCH):
                    mm(e, bank(7), onescol[:],
                       xbsub[:, ch * 512:(ch + 1) * 512], start=ch == 0,
                       stop=False)
                for ch in range(DCH):
                    i = mm(e, bank(7), onescolB[:],
                           xqsub[:, ch * 512:(ch + 1) * 512], start=False,
                           stop=ch == DCH - 1)
                return i
            tk_sx = S.op("T", t_sx, deps=[tk_xb, tk_xq, tk_consts] + bdep(7),
                         tag=f"Sx.{tag}{sub}")
            bufs["xbsub"].consume(tk_sx)
            bufs["xqsub"].consume(tk_sx)
            tk_rs = S.op("V", lambda e: (
                S.vf(e, e.tensor_copy(trh[0:2, :], bank(7, 0, 2))),
                e.scalar_tensor_tensor(trl[0:2, :], bank(7, 0, 2), 1.0,
                                       trh[0:2, :], ALU.mult, ALU.subtract)
            )[-1], deps=[tk_sx] + bufs["rowstage"].war(),
                tag=f"rows.{tag}{sub}")
            set_bank(7, tk_rs)
            def t_tr(e, sub=sub):
                for j in range(4):
                    g = sub * 4 + j
                    mm(e, bank(5, c0=2 * g, c1=2 * g + 2),
                       trh[:, 128 * j:128 * (j + 1)], idenb[:, 0:2],
                       start=True, stop=False)
                    i = mm(e, bank(5, c0=2 * g, c1=2 * g + 2),
                           trl[:, 128 * j:128 * (j + 1)], idenb[:, 0:2],
                           start=False, stop=True)
                return i
            tk_tr = S.op("T", t_tr,
                         deps=[tk_rs, tk_idenb] + (bdep(5) if sub == 0 else []),
                         tag=f"trS.{tag}{sub}")
            bufs["rowstage"].consume(tk_tr)
            tr_toks.append(tk_tr)
            if extra is not None:
                extra(sub, tk_xb, tk_sx)
        sx_cols = pall[:, 5 * 512:5 * 512 + 32].rearrange(
            "p (g t) -> p g t", t=2)
        def v_cm(e):
            S.vf(e, e.tensor_scalar(statc[:, C_M:C_M + 16],
                                    sx_cols[:, :, 0:1].squeeze(2),
                                    -1.0 / D, None, ALU.mult))
            S.vf(e, e.tensor_scalar(statc[:, C_T0:C_T0 + 16],
                                    sx_cols[:, :, 1:2].squeeze(2),
                                    1.0 / D, None, ALU.mult))
            S.vf(e, e.tensor_tensor(statc[:, C_T1:C_T1 + 16],
                                    statc[:, C_M:C_M + 16],
                                    statc[:, C_M:C_M + 16], ALU.mult))
            i = e.tensor_tensor(statc[:, C_T2:C_T2 + 16],
                                statc[:, C_T0:C_T0 + 16],
                                statc[:, C_T1:C_T1 + 16], ALU.subtract)
            return i
        tk_cm = S.op("V", v_cm, deps=tr_toks, tag=f"colm.{tag}")
        set_bank(5, tk_cm)
        tk_sq = S.op("S", lambda e: e.activation(
            out=statc[:, C_T0:C_T0 + 16], in_=statc[:, C_T2:C_T2 + 16],
            func=ACT.Sqrt, bias=cepsln[:]), deps=[tk_cm], tag=f"sqrt.{tag}")
        def v_cm2(e):
            S.vf(e, e.reciprocal(statc[:, C_RSTD:C_RSTD + 16],
                                 statc[:, C_T0:C_T0 + 16]))
            S.vf(e, e.tensor_tensor(statc[:, C_MR:C_MR + 16],
                                    statc[:, C_M:C_M + 16],
                                    statc[:, C_RSTD:C_RSTD + 16], ALU.mult))
            return vtail(e)
        tk_cm2 = S.op("V", v_cm2, deps=[tk_sq], tag=f"colm2.{tag}")
        return tk_cm2

    def emit_rows(sub, tk_stats, tag=""):
        def v_tc(e, sub=sub):
            e.tensor_copy(trC[:, 0:128:32],
                          statc[:, C_RSTD + 4 * sub:C_RSTD + 4 * sub + 4])
            i = e.tensor_copy(trC[:, 128:256:32],
                              statc[:, C_MR + 4 * sub:C_MR + 4 * sub + 4])
            return i
        tk_tc = S.op("V", v_tc, deps=[tk_stats] + bufs["trC"].war(),
                     tag=f"trC.{tag}{sub}")
        def t_cr(e):
            mm(e, bank(7, c0=0, c1=128), trC[:, 0:128], idenb[:], start=True)
            i = mm(e, bank(7, c0=128, c1=256), trC[:, 128:256], idenb[:],
                   start=True)
            return i
        tk_cr = S.op("T", t_cr, deps=[tk_tc, tk_idenb] + bdep(7),
                     tag=f"crMM.{tag}{sub}")
        bufs["trC"].consume(tk_cr)
        def v_rows(e):
            for j in range(4):
                e.tensor_copy(r4pad[32 * j:32 * j + 1, :],
                              bank(7, p0=32 * j, p1=32 * j + 1, c0=0, c1=128))
            for j in range(4):
                i = e.tensor_copy(
                    aux4[j][32:33, :],
                    bank(7, p0=32 * j, p1=32 * j + 1, c0=128, c1=256))
            return i
        tk_rows = S.op("V", v_rows,
                       deps=[tk_cr] + bufs["r4pad"].war() + bufs["aux4"].war(),
                       tag=f"rows4.{tag}{sub}")
        set_bank(7, tk_rows)
        return tk_rows, tk_rows

    def emit_xs(sub, tk_rp, dst, dstbuf, tag=""):
        def t_bc(e):
            for j in range(4):
                i = mm(e, bank(6, c0=128 * j, c1=128 * (j + 1)),
                       jsel[:, 128 * j:128 * (j + 1)], r4pad[:],
                       start=True)
            return i
        tk_bc = S.op("T", t_bc, deps=[tk_rp, tk_jsel] + bdep(6),
                     tag=f"bc.{tag}{sub}")
        bufs["r4pad"].consume(tk_bc)
        t0 = sub * 512
        def v_xs(e, t0=t0):
            for ch in range(DCH):
                i = e.tensor_tensor(dst[:, ch * 512:(ch + 1) * 512],
                                    sx[:, ch * T + t0: ch * T + t0 + 512],
                                    bank(6), ALU.mult)
            return i
        tk_xs = S.op("V", v_xs, deps=[tk_bc] + bufs[dstbuf].war(),
                     tag=f"xs.{tag}{sub}")
        set_bank(6, tk_xs)
        return tk_xs

    def emit_ffn(xdeps, wdeps, tag):
        logit_tr = []
        def sweep1_extra(sub, tk_xb, tk_sx):
            def t_lg(e):
                for ch in range(DCH):
                    i = mm(e, bank(4), wgt[:, ch * 128:(ch + 1) * 128],
                           xbsub[:, ch * 512:(ch + 1) * 512], start=ch == 0)
                return i
            tk_lgm = S.op("T", t_lg, deps=[tk_xb] + list(wdeps) + bdep(4),
                          tag=f"lgm.{tag}{sub}")
            bufs["xbsub"].consume(tk_lgm)
            tk_lr = S.op("V", lambda e: (
                S.vf(e, e.tensor_copy(lrowh[0:8, :], bank(4, 0, 8))),
                e.scalar_tensor_tensor(lrowl[0:8, :], bank(4, 0, 8), 1.0,
                                       lrowh[0:8, :], ALU.mult, ALU.subtract)
            )[-1], deps=[tk_lgm] + bufs["lrow"].war(),
                tag=f"lrow.{tag}{sub}")
            set_bank(4, tk_lr)
            def t_ltr(e, sub=sub):
                for j in range(4):
                    g = sub * 4 + j
                    mm(e, bank(3, c0=8 * g, c1=8 * g + 8),
                       lrowh[:, 128 * j:128 * (j + 1)], idenb[:, 0:8],
                       start=True, stop=False)
                    i = mm(e, bank(3, c0=8 * g, c1=8 * g + 8),
                           lrowl[:, 128 * j:128 * (j + 1)], idenb[:, 0:8],
                           start=False, stop=True)
                return i
            tk_ltr = S.op("T", t_ltr,
                          deps=[tk_lr, tk_idenb]
                          + (bdep(3) if sub == 0 else []),
                          tag=f"ltr.{tag}{sub}")
            bufs["lrow"].consume(tk_ltr)
            logit_tr.append(tk_ltr)

        tk_stats = emit_stats(xdeps, extra=sweep1_extra, tag=tag)

        l3 = gl[:].rearrange("p (g e) -> p g e", e=8)
        g3 = gtmp[:].rearrange("p (g e) -> p g e", e=8)
        w3 = gw[:].rearrange("p (g e) -> p g e", e=8)
        def v_gfix(e):
            rstd3 = statc[:, C_RSTD:C_RSTD + 16].unsqueeze(2).broadcast_to(
                [128, 16, 8])
            mr3 = statc[:, C_MR:C_MR + 16].unsqueeze(2).broadcast_to(
                [128, 16, 8])
            cs3 = csbg[:, 0:8].unsqueeze(1).broadcast_to([128, 16, 8])
            bg3 = csbg[:, 8:16].unsqueeze(1).broadcast_to([128, 16, 8])
            p3 = bank(3, c0=0, c1=128).rearrange("p (g e) -> p g e", e=8)
            S.vf(e, e.tensor_tensor(l3, p3, rstd3, ALU.mult))
            S.vf(e, e.tensor_tensor(g3, mr3, cs3, ALU.mult))
            S.vf(e, e.tensor_tensor(l3, l3, g3, ALU.add))
            S.vf(e, e.tensor_tensor(l3, l3, bg3, ALU.add))
            return vtail(e)
        tk_gfix = S.op("V", v_gfix, deps=logit_tr + [tk_stats, tk_csbg],
                       tag=f"gfix.{tag}")
        set_bank(3, tk_gfix)
        S.dbg[f"gfix.{tag}"] = tk_gfix
        S.dbg[f"stats.{tag}"] = tk_stats
        set_bank(3, tk_gfix)
        def v_gate(e):
            m1 = statc[:, C_T0:C_T0 + 16]
            m2 = statc[:, C_T1:C_T1 + 16]
            S.vf(e, e.tensor_reduce(m1, l3, AX.X, ALU.max))
            S.vf(e, e.tensor_tensor(
                g3, l3, m1.unsqueeze(2).broadcast_to([128, 16, 8]), ALU.is_ge))
            S.vf(e, e.scalar_tensor_tensor(w3, g3, -BIG, l3, ALU.mult,
                                           ALU.add))
            S.vf(e, e.tensor_reduce(m2, w3, AX.X, ALU.max))
            S.vf(e, e.tensor_tensor(
                w3, w3, m2.unsqueeze(2).broadcast_to([128, 16, 8]), ALU.is_ge))
            S.vf(e, e.tensor_tensor(statc[:, C_T2:C_T2 + 16], m1, m2,
                                    ALU.subtract))
            return vtail(e)
        tk_g1 = S.op("V", v_gate, deps=[tk_gfix], tag=f"gate.{tag}")
        tk_sg = S.op("S", lambda e: (
            e.activation(out=statc[:, C_T0:C_T0 + 16],
                         in_=statc[:, C_T2:C_T2 + 16], func=ACT.Sigmoid),
            e.activation(out=statc[:, C_T1:C_T1 + 16],
                         in_=statc[:, C_T2:C_T2 + 16], func=ACT.Sigmoid,
                         scale=-1.0))[-1], deps=[tk_g1], tag=f"sig.{tag}")
        def v_comb(e):
            S.vf(e, e.tensor_tensor(w3, w3, statc[:, C_T1:C_T1 + 16]
                                    .unsqueeze(2).broadcast_to([128, 16, 8]),
                                    ALU.mult))
            S.vf(e, e.tensor_tensor(g3, g3, statc[:, C_T0:C_T0 + 16]
                                    .unsqueeze(2).broadcast_to([128, 16, 8]),
                                    ALU.mult))
            S.vf(e, e.tensor_tensor(w3, w3, g3, ALU.add))
            i = e.tensor_copy(gwb[:, 0:128], gw[:])
            return i
        tk_w = S.op("V", v_comb, deps=[tk_sg], tag=f"gw.{tag}")
        S.dbg[f"gw.{tag}"] = tk_w

        res_toks = []
        for sub in range(NSUB):
            t0 = sub * 512
            tk_rows, _ = emit_rows(sub, tk_stats, tag=tag)
            tk_xs = emit_xs(sub, tk_rows, xssub, "xssub", tag=tag)
            def t_w1(e):
                for rch in range(4):
                    for ch in range(DCH):
                        mm(e, bank(rch),
                           w1t[:, ch * 512 + 128 * rch: ch * 512 + 128 * rch + 128],
                           xssub[:, ch * 512:(ch + 1) * 512],
                           start=ch == 0, stop=False)
                    for j in range(4):
                        i = mm(e, bank(rch, c0=128 * j, c1=128 * (j + 1)),
                               aux_w1[:, 128 * rch:128 * (rch + 1)],
                               aux4[j][:], start=False, stop=j == 3)
                return i
            tk_w1m = S.op("T", t_w1,
                          deps=[tk_xs, tk_rows, tk_csw1] + list(wdeps)
                          + bdep(0, 1, 2, 3), tag=f"w1m.{tag}{sub}")
            bufs["xssub"].consume(tk_w1m)
            bufs["aux4"].consume(tk_w1m)
            tk_gel = []
            for rch in range(4):
                tk_g = S.op("S", lambda e, rch=rch: e.activation(
                    out=hg[:, rch * 512:(rch + 1) * 512], in_=bank(rch),
                    func=ACT.Gelu),
                    deps=[tk_w1m] + (bufs["hg"].war() if rch == 0 else []),
                    tag=f"gelu.{tag}{sub}{rch}")
                set_bank(rch, tk_g)
                tk_gel.append(tk_g)
            tk_hp = []
            for j in range(4):
                g = sub * 4 + j
                tk_gt = S.op("T", lambda e, g=g: mm(
                    e, bank(4, c0=0, c1=128), gwb[:, 8 * g:8 * g + 128],
                    idenb[:], start=True), deps=[tk_w, tk_idenb] + bdep(4),
                    tag=f"gwt.{tag}{sub}{j}")
                tk_wc = S.op("V", lambda e: e.tensor_copy(
                    wtg128[0:8, :], bank(4, 0, 8, 0, 128)),
                    deps=[tk_gt] + bufs["wtg"].war(), tag=f"wtg.{tag}{sub}{j}")
                set_bank(4, tk_wc)
                def t_sel(e):
                    for rch in range(4):
                        i = mm(e, bank(5, c0=128 * rch, c1=128 * (rch + 1)),
                               selb[:, 128 * rch:128 * (rch + 1)], wtg128[:],
                               start=True)
                    return i
                tk_sel = S.op("T", t_sel, deps=[tk_wc, tk_selb] + bdep(5),
                              tag=f"sel.{tag}{sub}{j}")
                bufs["wtg"].consume(tk_sel)
                def v_h(e, j=j):
                    for rch in range(4):
                        i = e.tensor_tensor(
                            hp[:, rch * 512 + 128 * j: rch * 512 + 128 * j + 128],
                            hg[:, rch * 512 + 128 * j: rch * 512 + 128 * j + 128],
                            bank(5, c0=128 * rch, c1=128 * (rch + 1)), ALU.mult)
                    return i
                tk_h = S.op("V", v_h,
                            deps=[tk_sel] + tk_gel
                            + (bufs["hp"].war() if j == 0 else []),
                            tag=f"hprime.{tag}{sub}{j}")
                bufs["hg"].consume(tk_h)
                set_bank(5, tk_h)
                tk_hp.append(tk_h)
            for m in range(DCH):
                fb = 6 + (m % 2)
                def t_w2(e, m=m, fb=fb):
                    for rch in range(4):
                        i = mm(e, bank(fb),
                               w2t[:, rch * 1024 + 128 * m:
                                   rch * 1024 + 128 * m + 128],
                               hp[:, rch * 512:(rch + 1) * 512],
                               start=rch == 0, stop=rch == 3)
                    return i
                tk_w2m = S.op("T", t_w2, deps=tk_hp + list(wdeps) + bdep(fb),
                              tag=f"w2m.{tag}{sub}{m}")
                bufs["hp"].consume(tk_w2m)
                def v_res(e, m=m, fb=fb, t0=t0):
                    return e.scalar_tensor_tensor(
                        sx[:, m * T + t0: m * T + t0 + 512], bank(fb), 1.0,
                        sx[:, m * T + t0: m * T + t0 + 512], ALU.mult, ALU.add)
                tk_res = S.op("V", v_res, deps=[tk_w2m],
                              tag=f"res.{tag}{sub}{m}")
                set_bank(fb, tk_res)
                res_toks.append(tk_res)
        return res_toks

    ffn1_w = [tk_w1t, tk_w2t, tk_csw1, tk_csbg, tk_consts]
    res_f1 = emit_ffn(tk_x + ffn1_w, ffn1_w, tag="f1")

    if taps:
        tap_f1 = [S.dma("dY", lambda e, ch=ch: e.dma_start(
            out=tap["tap_f1"][128 * ch:128 * (ch + 1), :],
            in_=sx[:, ch * T:(ch + 1) * T]), deps=res_f1) for ch in range(DCH)]
        tk_tap_gl1 = S.dma("dY", lambda e: e.dma_start(
            out=tap["tap_gl1"], in_=gl[:]), deps=[S.dbg["gfix.f1"]])
        tk_tap_st1 = S.dma("dY", lambda e: e.dma_start(
            out=tap["tap_st1"], in_=statc[:, 0:48]), deps=[S.dbg["stats.f1"]])
        tk_tap_gw1 = S.dma("dY", lambda e: e.dma_start(
            out=tap["tap_gw1"], in_=gw[:]), deps=[S.dbg["gw.f1"]])
        res_f1 = res_f1 + [tk_tap_gl1, tk_tap_st1, tk_tap_gw1]

    tk_ln2 = emit_stats(res_f1, tag="ln2")

    tk_wq = S.dma("dG", lambda e: e.dma_start(
        out=wqkv[:, 0:DCH * 1024].rearrange("p (c f) -> p c f", c=DCH),
        in_=wqkv_d[:, 0:1024].rearrange("(c p) f -> p c f", p=128)),
        deps=res_f1, tag="ldwq")
    tk_bq = S.dma("dG", lambda e: e.dma_start(
        out=aux_w[0:1, 0:1024], in_=bqkv_d[:, 0:1024]),
        deps=[tk_consts] + bufs["aux_w"].war(), tag="ldbq")

    tk_csq = []
    for fs in range(2):
        def t_cs(e, fs=fs):
            for ch in range(DCH):
                i = mm(e, bank(7), onescol[:],
                       wqkv[:, ch * 1024 + 512 * fs: ch * 1024 + 512 * fs + 512],
                       start=ch == 0)
            return i
        tk_m = S.op("T", t_cs, deps=[tk_wq, tk_consts] + bdep(7),
                    tag=f"csq{fs}")
        tk_c = S.op("V", lambda e, fs=fs: e.tensor_copy(
            aux_w[32:33, 512 * fs:512 * fs + 512], bank(7, 0, 1)),
            deps=[tk_m, tk_bq], tag=f"csqc{fs}")
        set_bank(7, tk_c)
        tk_csq.append(tk_c)

    tk_q2_dma = []
    tk_qmm = []
    prev_sq = None
    for sub in range(NSUB):
        tk_rows, _ = emit_rows(sub, tk_ln2, tag="q")
        tk_xs2 = emit_xs(sub, tk_rows, xs2, "xs2", tag="q")
        for j in range(4):
            g = sub * 4 + j
            b0 = 0 if g % 2 == 0 else 2
            def t_q(e, j=j, b0=b0):
                for ch in range(DCH):
                    lh = xs2[:, ch * 512 + 128 * j: ch * 512 + 128 * j + 128]
                    mm(e, bank(b0), lh, wqkv[:, ch * 1024:ch * 1024 + 512],
                       start=ch == 0, stop=False)
                    mm(e, bank(b0 + 1), lh,
                       wqkv[:, ch * 1024 + 512:ch * 1024 + 1024],
                       start=ch == 0, stop=False)
                ax = aux4[j][:]
                mm(e, bank(b0), ax, aux_w[:, 0:512], start=False)
                i = mm(e, bank(b0 + 1), ax, aux_w[:, 512:1024], start=False)
                return i
            tk_qm = S.op("T", t_q, deps=[tk_xs2, tk_rows, tk_bq] + tk_csq
                         + bdep(b0, b0 + 1), tag=f"qmm{g}")
            bufs["xs2"].consume(tk_qm)
            bufs["aux4"].consume(tk_qm)
            bufs["aux_w"].consume(tk_qm)
            tk_qmm.append(tk_qm)
            qp = pall[:, 512 * b0:512 * b0 + 1024]
            def v_mq(e, qp=qp):
                return e.tensor_reduce(statc[:, C_G:C_G + 1], qp, AX.X,
                                       ALU.add, negate=True)
            tk_mq = S.op("V", v_mq,
                         deps=[tk_qm] + ([prev_sq] if prev_sq else []),
                         tag=f"mq{g}")
            tk_sq = S.op("S", lambda e, qp=qp: e.activation(
                out=q2st[:], in_=qp, func=ACT.Square,
                bias=statc[:, C_G:C_G + 1], scale=float(D),
                accum_out=statc[:, C_G + 1:C_G + 2]),
                deps=[tk_mq] + bufs["q2st"].war(), tag=f"q2sq{g}")
            prev_sq = tk_sq
            set_bank(b0, tk_sq)
            set_bank(b0 + 1, tk_sq)
            tk_t4 = S.op("S", lambda e: e.activation(
                out=u2st[:], in_=q2st[:], func=ACT.Square,
                accum_out=statc[:, C_G + 2:C_G + 3]),
                deps=[tk_sq] + bufs["u2st"].war(), tag=f"t4{g}")
            tk_rt = S.op("S", lambda e: e.activation(
                out=statc[:, C_G + 3:C_G + 4], in_=statc[:, C_G + 2:C_G + 3],
                func=ACT.Sqrt), deps=[tk_t4], tag=f"rsq{g}")
            def v_rs(e):
                S.vf(e, e.tensor_scalar(statc[:, C_G + 4:C_G + 5],
                                        statc[:, C_G + 1:C_G + 2],
                                        float(D) * D * EPS_SUM, EPS_NORM,
                                        ALU.add, ALU.mult))
                S.vf(e, e.tensor_tensor(statc[:, C_G + 4:C_G + 5],
                                        statc[:, C_G + 3:C_G + 4],
                                        statc[:, C_G + 4:C_G + 5], ALU.max))
                S.vf(e, e.reciprocal(statc[:, C_G + 5:C_G + 6],
                                     statc[:, C_G + 4:C_G + 5]))
                i = e.tensor_scalar(q2st[:], q2st[:],
                                    statc[:, C_G + 5:C_G + 6], None, ALU.mult)
                return i
            tk_rsx = S.op("V", v_rs, deps=[tk_rt], tag=f"rs{g}")
            tk_dma = S.dma("dS", lambda e, g=g: e.dma_start(
                out=q2_dram.ap()[128 * g:128 * (g + 1), :], in_=q2st[:]),
                deps=[tk_rsx], tag=f"q2dma{g}")
            bufs["q2st"].consume(tk_dma)
            bufs["u2st"].consume(tk_t4)
            tk_q2_dma.append(tk_dma)

    tk_wkv = S.dma("dG", lambda e: e.dma_start(
        out=wqkv[:].rearrange("p (c f) -> p c f", c=DCH),
        in_=wqkv_d[:, 1024:3072].rearrange("(c p) f -> p c f", p=128)),
        deps=tk_qmm + tk_csq, tag="ldwkv")
    tk_bkv = S.dma("dG", lambda e: e.dma_start(
        out=aux_w[0:1, :], in_=bqkv_d[:, 1024:3072]),
        deps=tk_qmm + bufs["aux_w"].war(), tag="ldbkv")
    tk_cskv = []
    for fs in range(4):
        def t_cs(e, fs=fs):
            for ch in range(DCH):
                i = mm(e, bank(7), onescol[:],
                       wqkv[:, ch * 2048 + 512 * fs: ch * 2048 + 512 * fs + 512],
                       start=ch == 0)
            return i
        tk_m = S.op("T", t_cs, deps=[tk_wkv, tk_consts] + bdep(7),
                    tag=f"cskv{fs}")
        tk_c = S.op("V", lambda e, fs=fs: e.tensor_copy(
            aux_w[32:33, 512 * fs:512 * fs + 512], bank(7, 0, 1)),
            deps=[tk_m, tk_bkv], tag=f"cskvc{fs}")
        set_bank(7, tk_c)
        tk_cskv.append(tk_c)

    tk_kvmm = []
    tk_uev = []
    tk_vev = []
    prev_k2 = None
    for sub in range(NSUB):
        tk_rows, _ = emit_rows(sub, tk_ln2, tag="kv")
        tk_xs2 = emit_xs(sub, tk_rows, xs2, "xs2", tag="kv")
        for j in range(4):
            g = sub * 4 + j
            b0 = 0 if g % 2 == 0 else 2
            def t_kv(e, j=j, b0=b0):
                for ch in range(DCH):
                    lh = xs2[:, ch * 512 + 128 * j: ch * 512 + 128 * j + 128]
                    mm(e, bank(b0), lh, wqkv[:, ch * 2048:ch * 2048 + 512],
                       start=ch == 0, stop=False)
                    mm(e, bank(b0 + 1), lh,
                       wqkv[:, ch * 2048 + 512:ch * 2048 + 1024],
                       start=ch == 0, stop=False)
                    mm(e, bank(4), lh,
                       wqkv[:, ch * 2048 + 1024:ch * 2048 + 1536],
                       start=ch == 0, stop=False)
                    mm(e, bank(5), lh,
                       wqkv[:, ch * 2048 + 1536:ch * 2048 + 2048],
                       start=ch == 0, stop=False)
                ax = aux4[j][:]
                mm(e, bank(b0), ax, aux_w[:, 0:512], start=False)
                mm(e, bank(b0 + 1), ax, aux_w[:, 512:1024], start=False)
                mm(e, bank(4), ax, aux_w[:, 1024:1536], start=False)
                i = mm(e, bank(5), ax, aux_w[:, 1536:2048], start=False)
                return i
            tk_km = S.op("T", t_kv, deps=[tk_xs2, tk_rows, tk_bkv] + tk_cskv
                         + bdep(b0, b0 + 1, 4, 5), tag=f"kvmm{g}")
            bufs["xs2"].consume(tk_km)
            bufs["aux4"].consume(tk_km)
            bufs["aux_w"].consume(tk_km)
            tk_kvmm.append(tk_km)
            tk_v = S.op("V", lambda e, g=g: e.tensor_copy(
                vtm[:, g * D:(g + 1) * D], pall[:, 512 * 4:512 * 6]),
                deps=[tk_km], tag=f"vev{g}")
            set_bank(4, tk_v)
            set_bank(5, tk_v)
            tk_vev.append(tk_v)
            kp = pall[:, 512 * b0:512 * b0 + 1024]
            def v_mk(e, kp=kp):
                return e.tensor_reduce(statc[:, C_G:C_G + 1], kp, AX.X,
                                       ALU.add, negate=True)
            tk_mk = S.op("V", v_mk,
                         deps=[tk_km] + ([prev_k2] if prev_k2 else []),
                         tag=f"mk{g}")
            tk_k2 = S.op("S", lambda e, kp=kp, g=g: e.activation(
                out=utm[:, g * D:(g + 1) * D], in_=kp, func=ACT.Square,
                bias=statc[:, C_G:C_G + 1], scale=float(D),
                accum_out=statc[:, C_G + 1:C_G + 2]),
                deps=[tk_mk], tag=f"k2{g}")
            prev_k2 = tk_k2
            set_bank(b0, tk_k2)
            set_bank(b0 + 1, tk_k2)
            def v_u(e, g=g):
                S.vf(e, e.tensor_scalar(statc[:, C_G + 2:C_G + 3],
                                        statc[:, C_G + 1:C_G + 2],
                                        float(D) * D * EPS_SUM, None, ALU.add))
                S.vf(e, e.reciprocal(statc[:, C_G + 3:C_G + 4],
                                     statc[:, C_G + 2:C_G + 3]))
                i = e.tensor_scalar(utm[:, g * D:(g + 1) * D],
                                    utm[:, g * D:(g + 1) * D],
                                    statc[:, C_G + 3:C_G + 4], None, ALU.mult)
                return i
            tk_u = S.op("V", v_u, deps=[tk_k2], tag=f"uev{g}")
            tk_uev.append(tk_u)

    if taps:
        tk_tap_u = S.dma("dG", lambda e: e.dma_start(
            out=tap["tap_u"].rearrange("(g p) c -> p g c", p=128),
            in_=utm[:].rearrange("p (g c) -> p g c", g=NG)), deps=tk_uev)
        tk_tap_v = S.dma("dG", lambda e: e.dma_start(
            out=tap["tap_v"].rearrange("(g p) c -> p g c", p=128),
            in_=vtm[:].rearrange("p (g c) -> p g c", g=NG)), deps=tk_vev)

    tk_kvt_ev = []
    kv_mm_toks = []
    for chalf in range(2):
        def t_kvp(e, chalf=chalf):
            for g in range(NG):
                for dch in range(DCH):
                    i = mm(e, bank(dch),
                           vtm[:, g * D + 128 * dch: g * D + 128 * dch + 128],
                           utm[:, g * D + 512 * chalf: g * D + 512 * chalf + 512],
                           start=g == 0, stop=g == NG - 1)
            return i
        tk_kvm = S.op("T", t_kvp, deps=tk_uev + tk_vev + bdep(*range(8)),
                      tag=f"kvp{chalf}")
        kv_mm_toks.append(tk_kvm)
        def v_kvev(e, chalf=chalf):
            for dch in range(DCH):
                i = e.tensor_copy(
                    kvtb[:, dch * 1024 + 512 * chalf:
                         dch * 1024 + 512 * chalf + 512], bank(dch))
            return i
        tk_ev = S.op("V", v_kvev, deps=[tk_kvm] + tk_kvmm + tk_qmm,
                     tag=f"kvtev{chalf}")
        for dch in range(DCH):
            set_bank(dch, tk_ev)
        tk_kvt_ev.append(tk_ev)

    tk_s2mm = []
    tk_u2_last = None
    for g in range(NG):
        tk_u2 = S.op("S", lambda e, g=g: e.activation(
            out=u2st[:], in_=utm[:, g * D:(g + 1) * D], func=ACT.Square),
            deps=[tk_uev[g]] + bufs["u2st"].war()
            + ([tk_s2mm[-1]] if tk_s2mm else []), tag=f"u2.{g}")
        tk_u2_last = tk_u2
        def t_s2(e, g=g):
            mm(e, bank(6), onescol[:], u2st[:, 0:512], start=g == 0,
               stop=g == NG - 1)
            i = mm(e, bank(7), onescol[:], u2st[:, 512:1024], start=g == 0,
                   stop=g == NG - 1)
            return i
        tk_mm2 = S.op("T", t_s2,
                      deps=[tk_u2, tk_consts] + (bdep(6, 7) if g == 0 else []),
                      tag=f"s2mm{g}")
        bufs["u2st"].consume(tk_mm2)
        tk_s2mm.append(tk_mm2)
    tk_s2ev = S.op("V", lambda e: (
        e.tensor_copy(arstage[0:1, 0:512], bank(6, 0, 1)),
        e.tensor_copy(arstage[0:1, 512:1024], bank(7, 0, 1)))[-1],
        deps=[tk_s2mm[-1]] + bufs["arstage"].war(), tag="s2ev")
    set_bank(6, tk_s2ev)
    set_bank(7, tk_s2ev)
    tk_s2dma = S.dma("dS", lambda e: e.dma_start(
        out=ar_in.ap()[D:D + 1, :], in_=arstage[0:1, 0:1024]),
        deps=[tk_s2ev], tag="s2dma")
    bufs["arstage"].consume(tk_s2dma)

    tk_wln = S.dma("dG", lambda e: e.dma_start(
        out=wlns[:].rearrange("p (c f) -> p c f", c=DCH),
        in_=wln_d.rearrange("(c p) f -> p c f", p=128)),
        deps=tk_kvmm + tk_qmm + tk_cskv, tag="ldwln")

    ar_dma = [tk_s2dma]
    tk_kvw_mm = []
    for cch in range(DCH):
        for dhalf in range(2):
            b = 2 + (cch * 2 + dhalf) % 2
            def t_kvw(e, cch=cch, dhalf=dhalf, b=b):
                for dch in range(DCH):
                    i = mm(e, bank(b),
                           kvtb[:, dch * 1024 + 128 * cch:
                                dch * 1024 + 128 * cch + 128],
                           wlns[:, dch * 1024 + 512 * dhalf:
                                dch * 1024 + 512 * dhalf + 512],
                           start=dch == 0, stop=dch == DCH - 1)
                return i
            tk_m = S.op("T", t_kvw, deps=tk_kvt_ev + [tk_wln] + bdep(b),
                        tag=f"kvw{cch}.{dhalf}")
            tk_kvw_mm.append(tk_m)
            tk_st = S.op("V", lambda e, b=b: e.tensor_copy(
                arstage[:, 0:512], bank(b)),
                deps=[tk_m] + bufs["arstage"].war(), tag=f"kvwst{cch}.{dhalf}")
            set_bank(b, tk_st)
            tk_d = S.dma("dS", lambda e, cch=cch, dhalf=dhalf: e.dma_start(
                out=ar_in.ap()[128 * cch:128 * (cch + 1),
                               512 * dhalf:512 * dhalf + 512],
                in_=arstage[:, 0:512]), deps=[tk_st],
                tag=f"kvwdma{cch}.{dhalf}")
            bufs["arstage"].consume(tk_d)
            ar_dma.append(tk_d)

    if taps:
        tk_tap_arin = S.dma("dG", lambda e: e.dma_start(
            out=tap["tap_arin"], in_=ar_in.ap()), deps=ar_dma)
        tk_tap_kvt = S.dma("dG", lambda e: e.dma_start(
            out=tap["tap_kvt"].rearrange("(c p) f -> p c f", p=128),
            in_=kvtb[:].rearrange("p (c f) -> p c f", c=DCH)), deps=tk_kvt_ev)

    tk_cc = S.op("G", lambda e: e.collective_compute(
        "AllReduce", ALU.add, replica_groups=[[0, 1, 2, 3], [4, 5, 6, 7]],
        ins=[ar_in.ap().opt()], outs=[ar_out.ap().opt()]),
        deps=ar_dma + ([tk_tap_arin] if taps else []), tag="allreduce")

    vf_war = kv_mm_toks + [tk_u2_last] + ([tk_tap_u, tk_tap_v] if taps else [])
    tk_vf = []
    for g in range(NG):
        for dch in range(DCH):
            tk_vf.append(S.dma("dY", lambda e, g=g, dch=dch: e.dma_start_transpose(
                out=vF[:, dch * T + 128 * g: dch * T + 128 * g + 128],
                in_=vtm[:, g * D + 128 * dch: g * D + 128 * dch + 128]),
                deps=vf_war, tag=f"vf{g}.{dch}"))

    tk_vw_res = []
    for sub in range(NSUB):
        t0 = sub * 512
        vf_dep = [tk_vf[(sub * 4 + j) * DCH + dch]
                  for j in range(4) for dch in range(DCH)]
        for m in range(DCH):
            b = m % 2
            def t_vw(e, m=m, sub=sub, b=b):
                for dch in range(DCH):
                    i = mm(e, bank(b),
                           wlns[:, dch * 1024 + 128 * m:
                                dch * 1024 + 128 * m + 128],
                           vF[:, dch * T + 512 * sub: dch * T + 512 * sub + 512],
                           start=dch == 0, stop=dch == DCH - 1)
                return i
            tk_m = S.op("T", t_vw, deps=vf_dep + [tk_wln] + bdep(b),
                        tag=f"vw{sub}.{m}")
            def v_r(e, m=m, t0=t0, b=b):
                return e.scalar_tensor_tensor(
                    sx[:, m * T + t0: m * T + t0 + 512], bank(b),
                    blnc[:, m:m + 1], sx[:, m * T + t0: m * T + t0 + 512],
                    ALU.add, ALU.add)
            tk_r = S.op("V", v_r, deps=[tk_m, tk_blnc], tag=f"vwres{sub}.{m}")
            set_bank(b, tk_r)
            tk_vw_res.append(tk_r)

    qf_war = kv_mm_toks + [tk_vf[-1]] + ([tk_tap_v] if taps else [])
    tk_qf = []
    for g in range(NG):
        for cch in range(DCH):
            tk_qf.append(S.dma("dY", lambda e, g=g, cch=cch: e.dma_start_transpose(
                out=q2F[:, cch * T + 128 * g: cch * T + 128 * g + 128],
                in_=q2_dram.ap()[128 * g:128 * (g + 1),
                                 128 * cch:128 * (cch + 1)]),
                deps=qf_war + [tk_q2_dma[g]], tag=f"qf{g}.{cch}"))

    tk_s2l = S.dma("dS", lambda e: e.dma_start(
        out=arstage[0:1, :], in_=ar_out.ap()[D:D + 1, :]),
        deps=[tk_cc] + bufs["arstage"].war(), tag="lds2")
    tk_rcp_rows = []
    for r in range(2):
        tk_rr = S.op("V", lambda e, r=r: (
            S.vf(e, e.tensor_copy(trh[0:1, :],
                                  arstage[0:1, 512 * r:512 * r + 512])),
            e.scalar_tensor_tensor(trl[0:1, :],
                                   arstage[0:1, 512 * r:512 * r + 512], 1.0,
                                   trh[0:1, :], ALU.mult, ALU.subtract)
        )[-1], deps=[tk_s2l] + bufs["rowstage"].war(), tag=f"rcprow{r}")
        def t_rc(e, r=r):
            for j in range(4):
                mm(e, bank(6, c0=4 * r + j, c1=4 * r + j + 1),
                   trh[:, 128 * j:128 * (j + 1)], idenb[:, 0:1],
                   start=True, stop=False)
                i = mm(e, bank(6, c0=4 * r + j, c1=4 * r + j + 1),
                       trl[:, 128 * j:128 * (j + 1)], idenb[:, 0:1],
                       start=False, stop=True)
            return i
        tk_rm = S.op("T", t_rc, deps=[tk_rr, tk_idenb]
                     + (bdep(6) if r == 0 else []), tag=f"rcpmm{r}")
        bufs["rowstage"].consume(tk_rm)
        tk_rcp_rows.append(tk_rm)
    tk_rcpt = tk_rcp_rows[-1]
    bufs["arstage"].consume(tk_rcpt)
    tk_rcpc = S.op("V", lambda e: e.tensor_copy(
        statc[:, C_RCP:C_RCP + 8], bank(6, c0=0, c1=8)), deps=[tk_rcpt],
        tag="rcpcp")
    set_bank(6, tk_rcpc)
    tk_rcps = S.op("S", lambda e: e.activation(
        out=statc[:, C_RCP:C_RCP + 8], in_=statc[:, C_RCP:C_RCP + 8],
        func=ACT.Sqrt, scale=float(SEQ)), deps=[tk_rcpc], tag="rcpsqrt")
    def v_rcp(e):
        S.vf(e, e.tensor_scalar_max(statc[:, C_RCP:C_RCP + 8],
                                    statc[:, C_RCP:C_RCP + 8],
                                    EPS_NORM * math.sqrt(SEQ)))
        S.vf(e, e.reciprocal(statc[:, C_RCP:C_RCP + 8],
                             statc[:, C_RCP:C_RCP + 8]))
        return vtail(e)
    tk_rcp = S.op("V", v_rcp, deps=[tk_rcps], tag="rcp")

    tk_kvwb = []
    for cch in range(DCH):
        tk_l = S.dma("dS", lambda e, cch=cch: e.dma_start(
            out=arstage[:], in_=ar_out.ap()[128 * cch:128 * (cch + 1), :]),
            deps=[tk_s2l] + bufs["arstage"].war(), tag=f"ldkvw{cch}")
        tk_c = S.op("V", lambda e, cch=cch: e.tensor_scalar(
            kvwb[:, cch * 1024:(cch + 1) * 1024], arstage[:],
            statc[:, C_RCP + cch:C_RCP + cch + 1], None, ALU.mult),
            deps=[tk_l, tk_rcp] + (tk_kvw_mm if cch == 0 else []),
            tag=f"kvwb{cch}")
        bufs["arstage"].consume(tk_c)
        tk_kvwb.append(tk_c)

    if taps:
        tk_tap_arout = S.dma("dG", lambda e: e.dma_start(
            out=tap["tap_arout"], in_=ar_out.ap()), deps=[tk_cc])
        tk_tap_q2 = S.dma("dG", lambda e: e.dma_start(
            out=tap["tap_q2"], in_=q2_dram.ap()), deps=tk_q2_dma)

    tk_t2_res = []
    for sub in range(NSUB):
        t0 = sub * 512
        qf_dep = [tk_qf[(sub * 4 + j) * DCH + cch]
                  for j in range(4) for cch in range(DCH)]
        for m in range(DCH):
            b = 2 + m % 2
            def t_t2(e, m=m, sub=sub, b=b):
                for cch in range(DCH):
                    i = mm(e, bank(b),
                           kvwb[:, cch * 1024 + 128 * m:
                                cch * 1024 + 128 * m + 128],
                           q2F[:, cch * T + 512 * sub: cch * T + 512 * sub + 512],
                           start=cch == 0, stop=cch == DCH - 1)
                return i
            tk_m = S.op("T", t_t2, deps=qf_dep + tk_kvwb + bdep(b),
                        tag=f"t2.{sub}.{m}")
            def v_r(e, m=m, t0=t0, b=b):
                return e.scalar_tensor_tensor(
                    sx[:, m * T + t0: m * T + t0 + 512], bank(b), 1.0,
                    sx[:, m * T + t0: m * T + t0 + 512], ALU.mult, ALU.add)
            tk_r = S.op("V", v_r, deps=[tk_m], tag=f"t2res{sub}.{m}")
            set_bank(b, tk_r)
            tk_t2_res.append(tk_r)

    if taps:
        tap_x2 = [S.dma("dY", lambda e, ch=ch: e.dma_start(
            out=tap["tap_x2"][128 * ch:128 * (ch + 1), :],
            in_=sx[:, ch * T:(ch + 1) * T]), deps=tk_t2_res + tk_vw_res)
            for ch in range(DCH)]

    tk_w1t2 = S.dma("dG", lambda e: e.dma_start(
        out=w1t[:].rearrange("p (c r) -> p c r", c=DCH),
        in_=w1t_d.rearrange("(c p) r -> p c r", p=128)),
        deps=tk_t2_res + tk_vw_res + tk_kvw_mm, tag="ldw1t2")
    tk_w2t2 = S.dma("dG", lambda e: e.dma_start(
        out=w2t[:].rearrange("p (c d) -> p c d", c=4),
        in_=w2t_d.rearrange("(c p) d -> p c d", p=128)),
        deps=tk_t2_res + tk_vw_res + tk_kvw_mm, tag="ldw2t2")
    ffn2_w = [tk_w1t2, tk_w2t2, tk_csw1, tk_csbg, tk_consts]
    res_f2 = emit_ffn(tk_t2_res + tk_vw_res + [tk_w1t2, tk_w2t2], ffn2_w,
                      tag="f2")

    tk_out = [S.dma("dY", lambda e, ch=ch: e.dma_start(
        out=out_t[128 * ch:128 * (ch + 1), :],
        in_=sx[:, ch * T:(ch + 1) * T]), deps=res_f2, tag=f"out{ch}")
        for ch in range(DCH)]
    if taps:
        tk_out += [tk_tap_arout, tk_tap_q2, tk_tap_kvt] + tap_f1 + tap_x2
        tk_tap_st = S.dma("dY", lambda e: e.dma_start(
            out=tap["tap_stat"], in_=statc[:]), deps=res_f2)
        tk_tap_gw = S.dma("dY", lambda e: e.dma_start(
            out=tap["tap_gw"], in_=gw[:]), deps=res_f2)
        tk_out += [tk_tap_st, tk_tap_gw]
    S.op("Y", lambda e: e.nop(), deps=tk_out, tag="final")

    S.prepare(nc, serialize=serialize, ser_range=ser_range)
    with nc.Block() as block:
        @block.tensor
        def _(eng):
            S.emit_engine("T", eng)
        @block.vector
        def _(eng):
            S.emit_engine("V", eng)
        @block.scalar
        def _(eng):
            S.emit_engine("S", eng)
        @block.gpsimd
        def _(eng):
            S.emit_engine("G", eng)
        @block.sync
        def _(eng):
            S.emit_engine("Y", eng)
    return nc


def host_inputs(inputs, core):
    x = inputs["L0_attn_x"]
    b, s = core // 4, core % 4
    xt = np.ascontiguousarray(np.asarray(x)[b, s * T:(s + 1) * T, :].T)
    sel = np.zeros((128, 512), dtype=ml_dtypes.bfloat16)
    for r in range(512):
        sel[r // 64, r] = 1.0
    jsel = np.zeros((128, 512), dtype=ml_dtypes.bfloat16)
    for n in range(512):
        jsel[32 * (n // 128), n] = 1.0
    W1 = np.asarray(inputs["L0_ffn_W1"], dtype=np.float32)
    W2 = np.asarray(inputs["L0_ffn_W2"], dtype=np.float32)
    Wg = np.asarray(inputs["L0_ffn_Wg"], dtype=np.float32)
    wgt = np.zeros((D, 128), dtype=np.float32)
    wgt[:, 0:8] = Wg.T
    return {
        "xt": xt.astype(np.float32),
        "wqkv": np.ascontiguousarray(np.asarray(inputs["L0_attn_Wqkv"],
                                                dtype=np.float32)),
        "bqkv": np.asarray(inputs["L0_attn_bqkv"],
                           dtype=np.float32).reshape(1, 3 * D),
        "wln": np.ascontiguousarray(np.asarray(inputs["L0_attn_Wln"],
                                               dtype=np.float32)),
        "blnc": np.ascontiguousarray(
            np.asarray(inputs["L0_attn_bln"], dtype=np.float32)
            .reshape(8, 128).T),
        "w1t": np.ascontiguousarray(W1.reshape(512, D).T),
        "w2t": np.ascontiguousarray(W2.transpose(0, 2, 1).reshape(512, D)),
        "wgt": wgt,
        "bg": np.asarray(inputs["L0_ffn_bg"], dtype=np.float32).reshape(1, 8),
        "iden32": np.eye(128, dtype=np.float32),
        "idenb": np.eye(128, dtype=ml_dtypes.bfloat16),
        "selb": sel,
        "jsel": jsel,
    }


def kernel(**inputs):
    nc = build(taps=False, ser_range=(0, 300))
    in_maps = [host_inputs(inputs, c) for c in range(8)]
    res = run_bass_kernel_spmd(nc, in_maps, core_ids=list(range(8)))
    out = np.empty((2, SEQ, D), dtype=np.float32)
    for c in range(8):
        b, s = c // 4, c % 4
        out[b, s * T:(s + 1) * T, :] = res.results[c]["out_t"].T
    return out


# version 03 (final)
# speedup vs baseline: 1.6028x; 1.0638x over previous
import math
import numpy as np
import ml_dtypes

import concourse.bass as bass
import concourse.mybir as mybir
from concourse.bass_utils import run_bass_kernel_spmd

F32 = mybir.dt.float32
BF16 = mybir.dt.bfloat16
AX = mybir.AxisListType
ALU = mybir.AluOpType
ACT = mybir.ActivationFunctionType

D = 1024
T = 2048
NG = 16
NSUB = 4
SEQ = 8192
DCH = 8
EPS_LN = 1e-5
EPS_SUM = 1e-7
EPS_NORM = 1e-12
BIG = 1e30


class Buf:
    def __init__(self):
        self.cons = []

    def consume(self, tok):
        self.cons.append(tok)

    def war(self):
        c, self.cons = self.cons, []
        return c


class Sched:
    DMAQ = {"dY": "Y", "dS": "S", "dG": "G"}

    def __init__(self):
        self.ops = []

    def op(self, eng, fn, deps=(), tag=""):
        self.ops.append(dict(eng=eng, fn=fn, deps=sorted(set(deps)), q=eng,
                             step=1, tag=tag))
        return len(self.ops) - 1

    def dma(self, q, fn, deps=(), tag=""):
        self.ops.append(dict(eng=self.DMAQ[q], fn=fn, deps=sorted(set(deps)),
                             q=q, step=16, tag=tag))
        return len(self.ops) - 1

    def prepare(self, nc, serialize=False, ser_range=None):
        self.serialize = serialize
        if serialize:
            ser_range = (0, len(self.ops))
        if ser_range is not None:
            lo, hi = ser_range
            for i in range(max(1, lo), min(hi, len(self.ops))):
                self.ops[i]["deps"] = sorted(set(self.ops[i]["deps"] + [i - 1]))
            self.serialize = True

        self.fsem = nc.alloc_semaphore(name="vfence")
        self.fcount = [0]
        n = len(self.ops)
        needed = set()
        for o in self.ops:
            for d in o["deps"]:
                assert 0 <= d < n
                needed.add(d)
        for i, o in enumerate(self.ops):
            if o["step"] == 16:
                needed.add(i)
        self.qs = {}
        for i, o in enumerate(self.ops):
            q = o["q"]
            if q not in self.qs:
                self.qs[q] = dict(count=0, sem=nc.alloc_semaphore(name=f"q{q}"))
            o["inc"] = i in needed
            if o["inc"]:
                self.qs[q]["count"] += o["step"]
            o["after"] = (q, self.qs[q]["count"])

    def vf(self, eng, inst):
        self.fcount[0] += 1
        inst.then_inc(self.fsem, 1)
        eng.wait_ge(self.fsem, self.fcount[0])
        return inst

    def emit_engine(self, ename, eng):
        for i, o in enumerate(self.ops):
            if o["eng"] != ename:
                continue
            waits = {}
            for d in self.ops[i]["deps"]:
                do = self.ops[d]
                if (do["q"] == o["q"] and o["step"] == 1 and do["step"] == 1
                        and not self.serialize):
                    continue
                dq, dv = do["after"]
                waits[dq] = max(waits.get(dq, 0), dv)
            for wq in sorted(waits):
                eng.wait_ge(self.qs[wq]["sem"], waits[wq])
            inst = o["fn"](eng)
            if o["inc"]:
                assert inst is not None, f"op {i} ({o['tag']}) returned None"
                inst.then_inc(self.qs[o["q"]]["sem"], o["step"])


def build(taps=False, serialize=False, ser_range=None):
    nc = bass.Bass("TRN2", target_bir_lowering=False, debug=False)

    xt = nc.dram_tensor("xt", [D, T], F32, kind="ExternalInput").ap()
    wqkv_d = nc.dram_tensor("wqkv", [D, 3 * D], F32, kind="ExternalInput").ap()
    bqkv_d = nc.dram_tensor("bqkv", [1, 3 * D], F32, kind="ExternalInput").ap()
    wln_d = nc.dram_tensor("wln", [D, D], F32, kind="ExternalInput").ap()
    blnc_d = nc.dram_tensor("blnc", [128, 8], F32, kind="ExternalInput").ap()
    w1t_d = nc.dram_tensor("w1t", [D, 512], F32, kind="ExternalInput").ap()
    w2t_d = nc.dram_tensor("w2t", [512, D], F32, kind="ExternalInput").ap()
    wgt_d = nc.dram_tensor("wgt", [D, 128], F32, kind="ExternalInput").ap()
    bg_d = nc.dram_tensor("bg", [1, 8], F32, kind="ExternalInput").ap()
    iden32_d = nc.dram_tensor("iden32", [128, 128], F32, kind="ExternalInput").ap()
    idenb_d = nc.dram_tensor("idenb", [128, 128], BF16, kind="ExternalInput").ap()
    selb_d = nc.dram_tensor("selb", [128, 512], BF16, kind="ExternalInput").ap()
    jsel_d = nc.dram_tensor("jsel", [128, 512], BF16, kind="ExternalInput").ap()
    out_t = nc.dram_tensor("out_t", [D, T], F32, kind="ExternalOutput").ap()

    q2_dram = nc.dram_tensor("q2_dram", [T, D], BF16, kind="Internal")
    ar_in = nc.dram_tensor("ar_in", [D + 1, D], F32, kind="Internal")
    ar_out = nc.dram_tensor("ar_out", [D + 1, D], F32, kind="Internal")

    tap = {}
    if taps:
        for name, shape, dt in [
            ("tap_f1", [D, T], F32), ("tap_x2", [D, T], F32),
            ("tap_u", [T, D], F32), ("tap_v", [T, D], F32),
            ("tap_q2", [T, D], F32), ("tap_arin", [D + 1, D], F32),
            ("tap_arout", [D + 1, D], F32), ("tap_kvt", [D, D], F32),
            ("tap_stat", [128, 112], F32), ("tap_gw", [128, 128], F32),
            ("tap_gl1", [128, 128], F32), ("tap_st1", [128, 48], F32),
            ("tap_gw1", [128, 128], F32),
        ]:
            tap[name] = nc.dram_tensor(name, shape, dt, kind="ExternalOutput").ap()

    arena = nc.alloc_sbuf_tensor("arena", [128, 208500], mybir.dt.uint8)
    base = nc.lookup_mloc(arena).addr
    off = [0]
    def sball(name, shape, dtype):
        nbytes = int(np.prod(shape[1:])) * mybir.dt.size(dtype)
        t = nc.alloc_sbuf_tensor_at(name, shape, dtype, offset=base + off[0])
        off[0] += (nbytes + 31) & ~31
        return t

    sx = sball("sx", [128, DCH * T], F32)
    wgt = sball("wgt", [128, DCH * 128], BF16)
    onescol = sball("onescol", [128, 128], BF16)
    onesrow = sball("onesrow", [128, 128], BF16)
    iden32 = sball("iden32", [128, 128], F32)
    idenb = sball("idenb", [128, 128], BF16)
    selb = sball("selb", [128, 512], BF16)
    aux_w = sball("aux_w", [128, 2048], BF16)
    aux_w1 = sball("aux_w1", [128, 512], BF16)
    bgrow = sball("bgrow", [1, 8], BF16)
    cswgrow = sball("cswgrow", [1, 8], BF16)
    aux4 = [sball(f"aux4_{j}", [128, 128], BF16) for j in range(4)]
    trh = sball("trh", [128, 512], BF16)
    trl = sball("trl", [128, 512], BF16)
    trC = sball("trC", [128, 256], BF16)
    r4pad = sball("r4pad", [128, 128], BF16)
    jsel = sball("jsel", [128, 512], BF16)
    onescolB = sball("onescolB", [128, 128], BF16)
    blnc = sball("blnc", [128, 8], F32)
    czero = sball("czero", [128, 1], F32)
    vdmy = sball("vdmy", [128, 2], F32)
    cepsln = sball("cepsln", [128, 1], F32)
    statc = sball("statc", [128, 112], F32)
    gl = sball("gl", [128, 128], F32)
    gtmp = sball("gtmp", [128, 128], F32)
    gw = sball("gw", [128, 128], F32)
    gwb = sball("gwb", [128, 256], BF16)
    csbg = sball("csbg", [128, 16], F32)
    lrowh = sball("lrowh", [128, 512], BF16)
    lrowl = sball("lrowl", [128, 512], BF16)
    wtg128 = sball("wtg128", [128, 128], BF16)
    arstage = sball("arstage", [128, 1024], F32)
    u2st = sball("u2st", [128, 1024], BF16)
    q2st = sball("q2st", [128, 1024], BF16)
    xs2 = sball("xs2", [128, DCH * 512], BF16)
    shared = sball("shared", [128, 38912], mybir.dt.uint8)
    vtm = sball("vtm", [128, NG * D], BF16)
    utm = sball("utm", [128, NG * D], BF16)
    assert off[0] <= 208500, f"SBUF over budget: {off[0]}"

    shbase = nc.lookup_mloc(shared).addr
    def shv(name, shape, dtype, byte_off):
        nbytes = int(np.prod(shape[1:])) * mybir.dt.size(dtype)
        assert byte_off + nbytes <= 38912
        return nc.alloc_sbuf_tensor_at(name, shape, dtype,
                                       offset=shbase + byte_off)

    w1t = shv("w1t", [128, DCH * 512], BF16, 0)
    w2t = shv("w2t", [128, 4 * 1024], BF16, 8192)
    xbsub = shv("xbsub", [128, DCH * 512], BF16, 16384)
    xqsub = shv("xqsub", [128, DCH * 512], BF16, 24576)
    xssub = shv("xssub", [128, DCH * 512], BF16, 16384)
    hg = shv("hg", [128, 4 * 512], F32, 24576)
    hp = shv("hp", [128, 4 * 512], BF16, 32768)
    wqkv = shv("wqkv", [128, DCH * 2048], BF16, 0)
    wlns = shv("wlns", [128, DCH * 1024], BF16, 0)
    kvtb = shv("kvtb", [128, DCH * 1024], BF16, 16384)
    kvwb = shv("kvwb", [128, DCH * 1024], BF16, 16384)

    vF = nc.alloc_sbuf_tensor_at("vF", [128, DCH * T], BF16,
                                 offset=nc.lookup_mloc(utm).addr)
    q2F = nc.alloc_sbuf_tensor_at("q2F", [128, DCH * T], BF16,
                                  offset=nc.lookup_mloc(vtm).addr)

    pall = nc.alloc_psum_tensor("pall", [128, 4096], F32)
    def bank(b, p0=0, p1=128, c0=0, c1=512):
        return pall[p0:p1, 512 * b + c0: 512 * b + c1]
    def bankb16(b, p0=0, p1=128, c0=0, c1=1024):
        return pall[p0:p1, 512 * b: 512 * (b + 1)].bitcast(BF16)[:, c0:c1]

    C_M, C_RSTD, C_MR = 0, 16, 32
    C_T0, C_T1, C_T2 = 48, 64, 80
    C_RCP, C_G = 96, 104

    S = Sched()
    S.dbg = {}
    bank_tok = {}
    def bdep(*bs):
        return [bank_tok[b] for b in bs if b in bank_tok]
    def set_bank(b, tok):
        bank_tok[b] = tok

    bufs = dict(xbsub=Buf(), xqsub=Buf(), xssub=Buf(), xs2=Buf(),
                trC=Buf(), r4pad=Buf(), aux4=Buf(), rowstage=Buf(),
                lrow=Buf(), wtg=Buf(), hg=Buf(), hp=Buf(), arstage=Buf(),
                aux_w=Buf(), cg=Buf(), q2st=Buf(), u2st=Buf())

    def mm(e, out, lhsT, rhs, start, stop=True):
        return e.matmul(out, lhsT, rhs, start=start, stop=stop)

    def vtail(e):
        return e.tensor_copy(vdmy[:, 0:1], vdmy[:, 1:2])

    tk_x = [S.dma("dY", lambda e, ch=ch: e.dma_start(
        out=sx[:, ch * T:(ch + 1) * T], in_=xt[128 * ch:128 * (ch + 1), :]),
        tag=f"ldx{ch}") for ch in range(DCH)]
    tk_iden32 = S.dma("dY", lambda e: e.dma_start(out=iden32[:], in_=iden32_d))
    tk_idenb = S.dma("dY", lambda e: e.dma_start(out=idenb[:], in_=idenb_d))
    tk_selb = S.dma("dY", lambda e: e.dma_start(out=selb[:], in_=selb_d))
    tk_jsel = S.dma("dY", lambda e: e.dma_start(out=jsel[:], in_=jsel_d))
    tk_blnc = S.dma("dY", lambda e: e.dma_start(out=blnc[:], in_=blnc_d))

    tk_w1t = S.dma("dG", lambda e: e.dma_start(
        out=w1t[:].rearrange("p (c r) -> p c r", c=DCH),
        in_=w1t_d.rearrange("(c p) r -> p c r", p=128)), tag="ldw1t")
    tk_w2t = S.dma("dG", lambda e: e.dma_start(
        out=w2t[:].rearrange("p (c d) -> p c d", c=4),
        in_=w2t_d.rearrange("(c p) d -> p c d", p=128)), tag="ldw2t")
    tk_wgt = S.dma("dG", lambda e: e.dma_start(
        out=wgt[:].rearrange("p (c f) -> p c f", c=DCH),
        in_=wgt_d.rearrange("(c p) f -> p c f", p=128)), tag="ldwgt")
    tk_bg = S.dma("dG", lambda e: e.dma_start(out=bgrow[:], in_=bg_d))

    tk_consts = S.op("V", lambda e: (
        e.memset(onescol[:], 0.0), e.memset(onescol[:, 0:1], 1.0),
        e.memset(onesrow[:], 0.0), e.memset(onesrow[0:1, :], 1.0),
        e.memset(aux4[0][:], 0.0), e.memset(aux4[1][:], 0.0),
        e.memset(aux4[2][:], 0.0), e.memset(aux4[3][:], 0.0),
        e.memset(aux4[0][0:1, :], 1.0), e.memset(aux4[1][0:1, :], 1.0),
        e.memset(aux4[2][0:1, :], 1.0), e.memset(aux4[3][0:1, :], 1.0),
        e.memset(trh[:], 0.0), e.memset(trl[:], 0.0),
        e.memset(trC[:], 0.0), e.memset(r4pad[:], 0.0),
        e.memset(onescolB[:], 0.0), e.memset(onescolB[:, 1:2], 1.0),
        e.memset(lrowh[:], 0.0), e.memset(lrowl[:], 0.0),
        e.memset(wtg128[:], 0.0), e.memset(aux_w[:], 0.0),
        e.memset(aux_w1[:], 0.0), e.memset(czero[:], 0.0),
        e.memset(vdmy[:], 0.0),
        e.memset(cepsln[:], EPS_LN))[-1], tag="consts")
    nc.const_aps.register(F32, 0.0, czero[:])

    def t_csw1(e):
        for ch in range(DCH):
            i = mm(e, bank(0), onescol[:], w1t[:, ch * 512:(ch + 1) * 512],
                   start=ch == 0)
        return i
    tk_csw1m = S.op("T", t_csw1, deps=[tk_w1t, tk_consts], tag="csw1m")
    tk_csw1 = S.op("V", lambda e: e.tensor_copy(aux_w1[32:33, :],
                                                bank(0, 0, 1)),
                   deps=[tk_csw1m], tag="csw1")
    set_bank(0, tk_csw1)

    def t_cswg(e):
        for ch in range(DCH):
            i = mm(e, bank(1, c0=0, c1=128), onescol[:],
                   wgt[:, ch * 128:(ch + 1) * 128], start=ch == 0)
        return i
    tk_cswgm = S.op("T", t_cswg, deps=[tk_wgt, tk_consts], tag="cswgm")
    tk_cswg = S.op("V", lambda e: e.tensor_copy(cswgrow[:],
                                                bank(1, 0, 1, 0, 8)),
                   deps=[tk_cswgm], tag="cswg")
    set_bank(1, tk_cswg)
    tk_csbgm = S.op("T", lambda e: (
        mm(e, bank(2, c0=0, c1=8), onesrow[0:1, :], cswgrow[:], start=True),
        mm(e, bank(2, c0=8, c1=16), onesrow[0:1, :], bgrow[:],
           start=True))[-1], deps=[tk_cswg, tk_bg, tk_consts], tag="csbgm")
    tk_csbg = S.op("V", lambda e: e.tensor_copy(csbg[:], bank(2, c0=0, c1=16)),
                   deps=[tk_csbgm], tag="csbg")
    set_bank(2, tk_csbg)

    def emit_stats(xdeps, extra=None, tag=""):
        tr_toks = []
        for sub in range(NSUB):
            t0 = sub * 512
            def v_xb(e, t0=t0):
                for ch in range(DCH):
                    i = e.tensor_copy(xbsub[:, ch * 512:(ch + 1) * 512],
                                      sx[:, ch * T + t0: ch * T + t0 + 512])
                return i
            tk_xb = S.op("V", v_xb, deps=list(xdeps) + bufs["xbsub"].war(),
                         tag=f"xb.{tag}{sub}")
            def v_xq(e, t0=t0):
                for ch in range(DCH):
                    i = e.tensor_tensor(
                        xqsub[:, ch * 512:(ch + 1) * 512],
                        sx[:, ch * T + t0: ch * T + t0 + 512],
                        sx[:, ch * T + t0: ch * T + t0 + 512], ALU.mult)
                return i
            tk_xq = S.op("V", v_xq, deps=list(xdeps) + bufs["xqsub"].war(),
                         tag=f"xq.{tag}{sub}")
            def t_sx(e):
                for ch in range(DCH):
                    mm(e, bank(7), onescol[:],
                       xbsub[:, ch * 512:(ch + 1) * 512], start=ch == 0,
                       stop=False)
                for ch in range(DCH):
                    i = mm(e, bank(7), onescolB[:],
                           xqsub[:, ch * 512:(ch + 1) * 512], start=False,
                           stop=ch == DCH - 1)
                return i
            tk_sx = S.op("T", t_sx, deps=[tk_xb, tk_xq, tk_consts] + bdep(7),
                         tag=f"Sx.{tag}{sub}")
            bufs["xbsub"].consume(tk_sx)
            bufs["xqsub"].consume(tk_sx)
            tk_rs = S.op("V", lambda e: (
                S.vf(e, e.tensor_copy(trh[0:2, :], bank(7, 0, 2))),
                e.scalar_tensor_tensor(trl[0:2, :], bank(7, 0, 2), 1.0,
                                       trh[0:2, :], ALU.mult, ALU.subtract)
            )[-1], deps=[tk_sx] + bufs["rowstage"].war(),
                tag=f"rows.{tag}{sub}")
            set_bank(7, tk_rs)
            def t_tr(e, sub=sub):
                for j in range(4):
                    g = sub * 4 + j
                    mm(e, bank(5, c0=2 * g, c1=2 * g + 2),
                       trh[:, 128 * j:128 * (j + 1)], idenb[:, 0:2],
                       start=True, stop=False)
                    i = mm(e, bank(5, c0=2 * g, c1=2 * g + 2),
                           trl[:, 128 * j:128 * (j + 1)], idenb[:, 0:2],
                           start=False, stop=True)
                return i
            tk_tr = S.op("T", t_tr,
                         deps=[tk_rs, tk_idenb] + (bdep(5) if sub == 0 else []),
                         tag=f"trS.{tag}{sub}")
            bufs["rowstage"].consume(tk_tr)
            tr_toks.append(tk_tr)
            if extra is not None:
                extra(sub, tk_xb, tk_sx)
        sx_cols = pall[:, 5 * 512:5 * 512 + 32].rearrange(
            "p (g t) -> p g t", t=2)
        def v_cm(e):
            S.vf(e, e.tensor_scalar(statc[:, C_M:C_M + 16],
                                    sx_cols[:, :, 0:1].squeeze(2),
                                    -1.0 / D, None, ALU.mult))
            S.vf(e, e.tensor_scalar(statc[:, C_T0:C_T0 + 16],
                                    sx_cols[:, :, 1:2].squeeze(2),
                                    1.0 / D, None, ALU.mult))
            S.vf(e, e.tensor_tensor(statc[:, C_T1:C_T1 + 16],
                                    statc[:, C_M:C_M + 16],
                                    statc[:, C_M:C_M + 16], ALU.mult))
            i = e.tensor_tensor(statc[:, C_T2:C_T2 + 16],
                                statc[:, C_T0:C_T0 + 16],
                                statc[:, C_T1:C_T1 + 16], ALU.subtract)
            return i
        tk_cm = S.op("V", v_cm, deps=tr_toks, tag=f"colm.{tag}")
        set_bank(5, tk_cm)
        tk_sq = S.op("S", lambda e: e.activation(
            out=statc[:, C_T0:C_T0 + 16], in_=statc[:, C_T2:C_T2 + 16],
            func=ACT.Sqrt, bias=cepsln[:]), deps=[tk_cm], tag=f"sqrt.{tag}")
        def v_cm2(e):
            S.vf(e, e.reciprocal(statc[:, C_RSTD:C_RSTD + 16],
                                 statc[:, C_T0:C_T0 + 16]))
            S.vf(e, e.tensor_tensor(statc[:, C_MR:C_MR + 16],
                                    statc[:, C_M:C_M + 16],
                                    statc[:, C_RSTD:C_RSTD + 16], ALU.mult))
            return vtail(e)
        tk_cm2 = S.op("V", v_cm2, deps=[tk_sq], tag=f"colm2.{tag}")
        return tk_cm2

    def emit_rows(sub, tk_stats, tag=""):
        def v_tc(e, sub=sub):
            e.tensor_copy(trC[:, 0:128:32],
                          statc[:, C_RSTD + 4 * sub:C_RSTD + 4 * sub + 4])
            i = e.tensor_copy(trC[:, 128:256:32],
                              statc[:, C_MR + 4 * sub:C_MR + 4 * sub + 4])
            return i
        tk_tc = S.op("V", v_tc, deps=[tk_stats] + bufs["trC"].war(),
                     tag=f"trC.{tag}{sub}")
        def t_cr(e):
            mm(e, bank(7, c0=0, c1=128), trC[:, 0:128], idenb[:], start=True)
            i = mm(e, bank(7, c0=128, c1=256), trC[:, 128:256], idenb[:],
                   start=True)
            return i
        tk_cr = S.op("T", t_cr, deps=[tk_tc, tk_idenb] + bdep(7),
                     tag=f"crMM.{tag}{sub}")
        bufs["trC"].consume(tk_cr)
        def v_rows(e):
            for j in range(4):
                e.tensor_copy(r4pad[32 * j:32 * j + 1, :],
                              bank(7, p0=32 * j, p1=32 * j + 1, c0=0, c1=128))
            for j in range(4):
                i = e.tensor_copy(
                    aux4[j][32:33, :],
                    bank(7, p0=32 * j, p1=32 * j + 1, c0=128, c1=256))
            return i
        tk_rows = S.op("V", v_rows,
                       deps=[tk_cr] + bufs["r4pad"].war() + bufs["aux4"].war(),
                       tag=f"rows4.{tag}{sub}")
        set_bank(7, tk_rows)
        return tk_rows, tk_rows

    def emit_xs(sub, tk_rp, dst, dstbuf, tag=""):
        def t_bc(e):
            for j in range(4):
                i = mm(e, bank(6, c0=128 * j, c1=128 * (j + 1)),
                       jsel[:, 128 * j:128 * (j + 1)], r4pad[:],
                       start=True)
            return i
        tk_bc = S.op("T", t_bc, deps=[tk_rp, tk_jsel] + bdep(6),
                     tag=f"bc.{tag}{sub}")
        bufs["r4pad"].consume(tk_bc)
        t0 = sub * 512
        def v_xs(e, t0=t0):
            for ch in range(DCH):
                i = e.tensor_tensor(dst[:, ch * 512:(ch + 1) * 512],
                                    sx[:, ch * T + t0: ch * T + t0 + 512],
                                    bank(6), ALU.mult)
            return i
        tk_xs = S.op("V", v_xs, deps=[tk_bc] + bufs[dstbuf].war(),
                     tag=f"xs.{tag}{sub}")
        set_bank(6, tk_xs)
        return tk_xs

    def emit_ffn(xdeps, wdeps, tag):
        logit_tr = []
        def sweep1_extra(sub, tk_xb, tk_sx):
            def t_lg(e):
                for ch in range(DCH):
                    i = mm(e, bank(4), wgt[:, ch * 128:(ch + 1) * 128],
                           xbsub[:, ch * 512:(ch + 1) * 512], start=ch == 0)
                return i
            tk_lgm = S.op("T", t_lg, deps=[tk_xb] + list(wdeps) + bdep(4),
                          tag=f"lgm.{tag}{sub}")
            bufs["xbsub"].consume(tk_lgm)
            tk_lr = S.op("V", lambda e: (
                S.vf(e, e.tensor_copy(lrowh[0:8, :], bank(4, 0, 8))),
                e.scalar_tensor_tensor(lrowl[0:8, :], bank(4, 0, 8), 1.0,
                                       lrowh[0:8, :], ALU.mult, ALU.subtract)
            )[-1], deps=[tk_lgm] + bufs["lrow"].war(),
                tag=f"lrow.{tag}{sub}")
            set_bank(4, tk_lr)
            def t_ltr(e, sub=sub):
                for j in range(4):
                    g = sub * 4 + j
                    mm(e, bank(3, c0=8 * g, c1=8 * g + 8),
                       lrowh[:, 128 * j:128 * (j + 1)], idenb[:, 0:8],
                       start=True, stop=False)
                    i = mm(e, bank(3, c0=8 * g, c1=8 * g + 8),
                           lrowl[:, 128 * j:128 * (j + 1)], idenb[:, 0:8],
                           start=False, stop=True)
                return i
            tk_ltr = S.op("T", t_ltr,
                          deps=[tk_lr, tk_idenb]
                          + (bdep(3) if sub == 0 else []),
                          tag=f"ltr.{tag}{sub}")
            bufs["lrow"].consume(tk_ltr)
            logit_tr.append(tk_ltr)

        tk_stats = emit_stats(xdeps, extra=sweep1_extra, tag=tag)

        l3 = gl[:].rearrange("p (g e) -> p g e", e=8)
        g3 = gtmp[:].rearrange("p (g e) -> p g e", e=8)
        w3 = gw[:].rearrange("p (g e) -> p g e", e=8)
        def v_gfix(e):
            rstd3 = statc[:, C_RSTD:C_RSTD + 16].unsqueeze(2).broadcast_to(
                [128, 16, 8])
            mr3 = statc[:, C_MR:C_MR + 16].unsqueeze(2).broadcast_to(
                [128, 16, 8])
            cs3 = csbg[:, 0:8].unsqueeze(1).broadcast_to([128, 16, 8])
            bg3 = csbg[:, 8:16].unsqueeze(1).broadcast_to([128, 16, 8])
            p3 = bank(3, c0=0, c1=128).rearrange("p (g e) -> p g e", e=8)
            S.vf(e, e.tensor_tensor(l3, p3, rstd3, ALU.mult))
            S.vf(e, e.tensor_tensor(g3, mr3, cs3, ALU.mult))
            S.vf(e, e.tensor_tensor(l3, l3, g3, ALU.add))
            S.vf(e, e.tensor_tensor(l3, l3, bg3, ALU.add))
            return vtail(e)
        tk_gfix = S.op("V", v_gfix, deps=logit_tr + [tk_stats, tk_csbg],
                       tag=f"gfix.{tag}")
        set_bank(3, tk_gfix)
        S.dbg[f"gfix.{tag}"] = tk_gfix
        S.dbg[f"stats.{tag}"] = tk_stats
        set_bank(3, tk_gfix)
        def v_gate(e):
            m1 = statc[:, C_T0:C_T0 + 16]
            m2 = statc[:, C_T1:C_T1 + 16]
            S.vf(e, e.tensor_reduce(m1, l3, AX.X, ALU.max))
            S.vf(e, e.tensor_tensor(
                g3, l3, m1.unsqueeze(2).broadcast_to([128, 16, 8]), ALU.is_ge))
            S.vf(e, e.scalar_tensor_tensor(w3, g3, -BIG, l3, ALU.mult,
                                           ALU.add))
            S.vf(e, e.tensor_reduce(m2, w3, AX.X, ALU.max))
            S.vf(e, e.tensor_tensor(
                w3, w3, m2.unsqueeze(2).broadcast_to([128, 16, 8]), ALU.is_ge))
            S.vf(e, e.tensor_tensor(statc[:, C_T2:C_T2 + 16], m1, m2,
                                    ALU.subtract))
            return vtail(e)
        tk_g1 = S.op("V", v_gate, deps=[tk_gfix], tag=f"gate.{tag}")
        tk_sg = S.op("S", lambda e: (
            e.activation(out=statc[:, C_T0:C_T0 + 16],
                         in_=statc[:, C_T2:C_T2 + 16], func=ACT.Sigmoid),
            e.activation(out=statc[:, C_T1:C_T1 + 16],
                         in_=statc[:, C_T2:C_T2 + 16], func=ACT.Sigmoid,
                         scale=-1.0))[-1], deps=[tk_g1], tag=f"sig.{tag}")
        def v_comb(e):
            S.vf(e, e.tensor_tensor(w3, w3, statc[:, C_T1:C_T1 + 16]
                                    .unsqueeze(2).broadcast_to([128, 16, 8]),
                                    ALU.mult))
            S.vf(e, e.tensor_tensor(g3, g3, statc[:, C_T0:C_T0 + 16]
                                    .unsqueeze(2).broadcast_to([128, 16, 8]),
                                    ALU.mult))
            S.vf(e, e.tensor_tensor(w3, w3, g3, ALU.add))
            i = e.tensor_copy(gwb[:, 0:128], gw[:])
            return i
        tk_w = S.op("V", v_comb, deps=[tk_sg], tag=f"gw.{tag}")
        S.dbg[f"gw.{tag}"] = tk_w

        res_toks = []
        for sub in range(NSUB):
            t0 = sub * 512
            tk_rows, _ = emit_rows(sub, tk_stats, tag=tag)
            tk_xs = emit_xs(sub, tk_rows, xssub, "xssub", tag=tag)
            def t_w1(e):
                for rch in range(4):
                    for ch in range(DCH):
                        mm(e, bank(rch),
                           w1t[:, ch * 512 + 128 * rch: ch * 512 + 128 * rch + 128],
                           xssub[:, ch * 512:(ch + 1) * 512],
                           start=ch == 0, stop=False)
                    for j in range(4):
                        i = mm(e, bank(rch, c0=128 * j, c1=128 * (j + 1)),
                               aux_w1[:, 128 * rch:128 * (rch + 1)],
                               aux4[j][:], start=False, stop=j == 3)
                return i
            tk_w1m = S.op("T", t_w1,
                          deps=[tk_xs, tk_rows, tk_csw1] + list(wdeps)
                          + bdep(0, 1, 2, 3), tag=f"w1m.{tag}{sub}")
            bufs["xssub"].consume(tk_w1m)
            bufs["aux4"].consume(tk_w1m)
            tk_gel = []
            for rch in range(4):
                tk_g = S.op("S", lambda e, rch=rch: e.activation(
                    out=hg[:, rch * 512:(rch + 1) * 512], in_=bank(rch),
                    func=ACT.Gelu),
                    deps=[tk_w1m] + (bufs["hg"].war() if rch == 0 else []),
                    tag=f"gelu.{tag}{sub}{rch}")
                set_bank(rch, tk_g)
                tk_gel.append(tk_g)
            tk_hp = []
            for j in range(4):
                g = sub * 4 + j
                tk_gt = S.op("T", lambda e, g=g: mm(
                    e, bank(4, c0=0, c1=128), gwb[:, 8 * g:8 * g + 128],
                    idenb[:], start=True), deps=[tk_w, tk_idenb] + bdep(4),
                    tag=f"gwt.{tag}{sub}{j}")
                tk_wc = S.op("V", lambda e: e.tensor_copy(
                    wtg128[0:8, :], bank(4, 0, 8, 0, 128)),
                    deps=[tk_gt] + bufs["wtg"].war(), tag=f"wtg.{tag}{sub}{j}")
                set_bank(4, tk_wc)
                def t_sel(e):
                    for rch in range(4):
                        i = mm(e, bank(5, c0=128 * rch, c1=128 * (rch + 1)),
                               selb[:, 128 * rch:128 * (rch + 1)], wtg128[:],
                               start=True)
                    return i
                tk_sel = S.op("T", t_sel, deps=[tk_wc, tk_selb] + bdep(5),
                              tag=f"sel.{tag}{sub}{j}")
                bufs["wtg"].consume(tk_sel)
                def v_h(e, j=j):
                    for rch in range(4):
                        i = e.tensor_tensor(
                            hp[:, rch * 512 + 128 * j: rch * 512 + 128 * j + 128],
                            hg[:, rch * 512 + 128 * j: rch * 512 + 128 * j + 128],
                            bank(5, c0=128 * rch, c1=128 * (rch + 1)), ALU.mult)
                    return i
                tk_h = S.op("V", v_h,
                            deps=[tk_sel] + tk_gel
                            + (bufs["hp"].war() if j == 0 else []),
                            tag=f"hprime.{tag}{sub}{j}")
                bufs["hg"].consume(tk_h)
                set_bank(5, tk_h)
                tk_hp.append(tk_h)
            for m in range(DCH):
                fb = 6 + (m % 2)
                def t_w2(e, m=m, fb=fb):
                    for rch in range(4):
                        i = mm(e, bank(fb),
                               w2t[:, rch * 1024 + 128 * m:
                                   rch * 1024 + 128 * m + 128],
                               hp[:, rch * 512:(rch + 1) * 512],
                               start=rch == 0, stop=rch == 3)
                    return i
                tk_w2m = S.op("T", t_w2, deps=tk_hp + list(wdeps) + bdep(fb),
                              tag=f"w2m.{tag}{sub}{m}")
                bufs["hp"].consume(tk_w2m)
                def v_res(e, m=m, fb=fb, t0=t0):
                    return e.scalar_tensor_tensor(
                        sx[:, m * T + t0: m * T + t0 + 512], bank(fb), 1.0,
                        sx[:, m * T + t0: m * T + t0 + 512], ALU.mult, ALU.add)
                tk_res = S.op("V", v_res, deps=[tk_w2m],
                              tag=f"res.{tag}{sub}{m}")
                set_bank(fb, tk_res)
                res_toks.append(tk_res)
        return res_toks

    ffn1_w = [tk_w1t, tk_w2t, tk_csw1, tk_csbg, tk_consts]
    res_f1 = emit_ffn(tk_x + ffn1_w, ffn1_w, tag="f1")

    if taps:
        tap_f1 = [S.dma("dY", lambda e, ch=ch: e.dma_start(
            out=tap["tap_f1"][128 * ch:128 * (ch + 1), :],
            in_=sx[:, ch * T:(ch + 1) * T]), deps=res_f1) for ch in range(DCH)]
        tk_tap_gl1 = S.dma("dY", lambda e: e.dma_start(
            out=tap["tap_gl1"], in_=gl[:]), deps=[S.dbg["gfix.f1"]])
        tk_tap_st1 = S.dma("dY", lambda e: e.dma_start(
            out=tap["tap_st1"], in_=statc[:, 0:48]), deps=[S.dbg["stats.f1"]])
        tk_tap_gw1 = S.dma("dY", lambda e: e.dma_start(
            out=tap["tap_gw1"], in_=gw[:]), deps=[S.dbg["gw.f1"]])
        res_f1 = res_f1 + [tk_tap_gl1, tk_tap_st1, tk_tap_gw1]

    tk_ln2 = emit_stats(res_f1, tag="ln2")

    tk_wq = S.dma("dG", lambda e: e.dma_start(
        out=wqkv[:, 0:DCH * 1024].rearrange("p (c f) -> p c f", c=DCH),
        in_=wqkv_d[:, 0:1024].rearrange("(c p) f -> p c f", p=128)),
        deps=res_f1, tag="ldwq")
    tk_bq = S.dma("dG", lambda e: e.dma_start(
        out=aux_w[0:1, 0:1024], in_=bqkv_d[:, 0:1024]),
        deps=[tk_consts] + bufs["aux_w"].war(), tag="ldbq")

    tk_csq = []
    for fs in range(2):
        def t_cs(e, fs=fs):
            for ch in range(DCH):
                i = mm(e, bank(7), onescol[:],
                       wqkv[:, ch * 1024 + 512 * fs: ch * 1024 + 512 * fs + 512],
                       start=ch == 0)
            return i
        tk_m = S.op("T", t_cs, deps=[tk_wq, tk_consts] + bdep(7),
                    tag=f"csq{fs}")
        tk_c = S.op("V", lambda e, fs=fs: e.tensor_copy(
            aux_w[32:33, 512 * fs:512 * fs + 512], bank(7, 0, 1)),
            deps=[tk_m, tk_bq], tag=f"csqc{fs}")
        set_bank(7, tk_c)
        tk_csq.append(tk_c)

    tk_q2_dma = []
    tk_qmm = []
    prev_sq = None
    for sub in range(NSUB):
        tk_rows, _ = emit_rows(sub, tk_ln2, tag="q")
        tk_xs2 = emit_xs(sub, tk_rows, xs2, "xs2", tag="q")
        for j in range(4):
            g = sub * 4 + j
            b0 = 0 if g % 2 == 0 else 2
            def t_q(e, j=j, b0=b0):
                for ch in range(DCH):
                    lh = xs2[:, ch * 512 + 128 * j: ch * 512 + 128 * j + 128]
                    mm(e, bank(b0), lh, wqkv[:, ch * 1024:ch * 1024 + 512],
                       start=ch == 0, stop=False)
                    mm(e, bank(b0 + 1), lh,
                       wqkv[:, ch * 1024 + 512:ch * 1024 + 1024],
                       start=ch == 0, stop=False)
                ax = aux4[j][:]
                mm(e, bank(b0), ax, aux_w[:, 0:512], start=False)
                i = mm(e, bank(b0 + 1), ax, aux_w[:, 512:1024], start=False)
                return i
            tk_qm = S.op("T", t_q, deps=[tk_xs2, tk_rows, tk_bq] + tk_csq
                         + bdep(b0, b0 + 1), tag=f"qmm{g}")
            bufs["xs2"].consume(tk_qm)
            bufs["aux4"].consume(tk_qm)
            bufs["aux_w"].consume(tk_qm)
            tk_qmm.append(tk_qm)
            qp = pall[:, 512 * b0:512 * b0 + 1024]
            def v_mq(e, qp=qp):
                return e.tensor_reduce(statc[:, C_G:C_G + 1], qp, AX.X,
                                       ALU.add, negate=True)
            tk_mq = S.op("V", v_mq,
                         deps=[tk_qm] + ([prev_sq] if prev_sq else []),
                         tag=f"mq{g}")
            tk_sq = S.op("S", lambda e, qp=qp: e.activation(
                out=q2st[:], in_=qp, func=ACT.Square,
                bias=statc[:, C_G:C_G + 1], scale=float(D),
                accum_out=statc[:, C_G + 1:C_G + 2]),
                deps=[tk_mq] + bufs["q2st"].war(), tag=f"q2sq{g}")
            prev_sq = tk_sq
            set_bank(b0, tk_sq)
            set_bank(b0 + 1, tk_sq)
            tk_t4 = S.op("S", lambda e: e.activation(
                out=u2st[:], in_=q2st[:], func=ACT.Square,
                accum_out=statc[:, C_G + 2:C_G + 3]),
                deps=[tk_sq] + bufs["u2st"].war(), tag=f"t4{g}")
            tk_rt = S.op("S", lambda e: e.activation(
                out=statc[:, C_G + 3:C_G + 4], in_=statc[:, C_G + 2:C_G + 3],
                func=ACT.Sqrt), deps=[tk_t4], tag=f"rsq{g}")
            def v_rs(e):
                S.vf(e, e.tensor_scalar(statc[:, C_G + 4:C_G + 5],
                                        statc[:, C_G + 1:C_G + 2],
                                        float(D) * D * EPS_SUM, EPS_NORM,
                                        ALU.add, ALU.mult))
                S.vf(e, e.tensor_tensor(statc[:, C_G + 4:C_G + 5],
                                        statc[:, C_G + 3:C_G + 4],
                                        statc[:, C_G + 4:C_G + 5], ALU.max))
                S.vf(e, e.reciprocal(statc[:, C_G + 5:C_G + 6],
                                     statc[:, C_G + 4:C_G + 5]))
                i = e.tensor_scalar(q2st[:], q2st[:],
                                    statc[:, C_G + 5:C_G + 6], None, ALU.mult)
                return i
            tk_rsx = S.op("V", v_rs, deps=[tk_rt], tag=f"rs{g}")
            tk_dma = S.dma("dS", lambda e, g=g: e.dma_start(
                out=q2_dram.ap()[128 * g:128 * (g + 1), :], in_=q2st[:]),
                deps=[tk_rsx], tag=f"q2dma{g}")
            bufs["q2st"].consume(tk_dma)
            bufs["u2st"].consume(tk_t4)
            tk_q2_dma.append(tk_dma)

    tk_wkv = S.dma("dG", lambda e: e.dma_start(
        out=wqkv[:].rearrange("p (c f) -> p c f", c=DCH),
        in_=wqkv_d[:, 1024:3072].rearrange("(c p) f -> p c f", p=128)),
        deps=tk_qmm + tk_csq, tag="ldwkv")
    tk_bkv = S.dma("dG", lambda e: e.dma_start(
        out=aux_w[0:1, :], in_=bqkv_d[:, 1024:3072]),
        deps=tk_qmm + bufs["aux_w"].war(), tag="ldbkv")
    tk_cskv = []
    for fs in range(4):
        def t_cs(e, fs=fs):
            for ch in range(DCH):
                i = mm(e, bank(7), onescol[:],
                       wqkv[:, ch * 2048 + 512 * fs: ch * 2048 + 512 * fs + 512],
                       start=ch == 0)
            return i
        tk_m = S.op("T", t_cs, deps=[tk_wkv, tk_consts] + bdep(7),
                    tag=f"cskv{fs}")
        tk_c = S.op("V", lambda e, fs=fs: e.tensor_copy(
            aux_w[32:33, 512 * fs:512 * fs + 512], bank(7, 0, 1)),
            deps=[tk_m, tk_bkv], tag=f"cskvc{fs}")
        set_bank(7, tk_c)
        tk_cskv.append(tk_c)

    tk_kvmm = []
    tk_uev = []
    tk_vev = []
    prev_k2 = None
    for sub in range(NSUB):
        tk_rows, _ = emit_rows(sub, tk_ln2, tag="kv")
        tk_xs2 = emit_xs(sub, tk_rows, xs2, "xs2", tag="kv")
        for j in range(4):
            g = sub * 4 + j
            b0 = 0 if g % 2 == 0 else 2
            def t_kv(e, j=j, b0=b0):
                for ch in range(DCH):
                    lh = xs2[:, ch * 512 + 128 * j: ch * 512 + 128 * j + 128]
                    mm(e, bank(b0), lh, wqkv[:, ch * 2048:ch * 2048 + 512],
                       start=ch == 0, stop=False)
                    mm(e, bank(b0 + 1), lh,
                       wqkv[:, ch * 2048 + 512:ch * 2048 + 1024],
                       start=ch == 0, stop=False)
                    mm(e, bank(4), lh,
                       wqkv[:, ch * 2048 + 1024:ch * 2048 + 1536],
                       start=ch == 0, stop=False)
                    mm(e, bank(5), lh,
                       wqkv[:, ch * 2048 + 1536:ch * 2048 + 2048],
                       start=ch == 0, stop=False)
                ax = aux4[j][:]
                mm(e, bank(b0), ax, aux_w[:, 0:512], start=False)
                mm(e, bank(b0 + 1), ax, aux_w[:, 512:1024], start=False)
                mm(e, bank(4), ax, aux_w[:, 1024:1536], start=False)
                i = mm(e, bank(5), ax, aux_w[:, 1536:2048], start=False)
                return i
            tk_km = S.op("T", t_kv, deps=[tk_xs2, tk_rows, tk_bkv] + tk_cskv
                         + bdep(b0, b0 + 1, 4, 5), tag=f"kvmm{g}")
            bufs["xs2"].consume(tk_km)
            bufs["aux4"].consume(tk_km)
            bufs["aux_w"].consume(tk_km)
            tk_kvmm.append(tk_km)
            tk_v = S.op("V", lambda e, g=g: e.tensor_copy(
                vtm[:, g * D:(g + 1) * D], pall[:, 512 * 4:512 * 6]),
                deps=[tk_km], tag=f"vev{g}")
            set_bank(4, tk_v)
            set_bank(5, tk_v)
            tk_vev.append(tk_v)
            kp = pall[:, 512 * b0:512 * b0 + 1024]
            def v_mk(e, kp=kp):
                return e.tensor_reduce(statc[:, C_G:C_G + 1], kp, AX.X,
                                       ALU.add, negate=True)
            tk_mk = S.op("V", v_mk,
                         deps=[tk_km] + ([prev_k2] if prev_k2 else []),
                         tag=f"mk{g}")
            tk_k2 = S.op("S", lambda e, kp=kp, g=g: e.activation(
                out=utm[:, g * D:(g + 1) * D], in_=kp, func=ACT.Square,
                bias=statc[:, C_G:C_G + 1], scale=float(D),
                accum_out=statc[:, C_G + 1:C_G + 2]),
                deps=[tk_mk], tag=f"k2{g}")
            prev_k2 = tk_k2
            set_bank(b0, tk_k2)
            set_bank(b0 + 1, tk_k2)
            def v_u(e, g=g):
                S.vf(e, e.tensor_scalar(statc[:, C_G + 2:C_G + 3],
                                        statc[:, C_G + 1:C_G + 2],
                                        float(D) * D * EPS_SUM, None, ALU.add))
                S.vf(e, e.reciprocal(statc[:, C_G + 3:C_G + 4],
                                     statc[:, C_G + 2:C_G + 3]))
                i = e.tensor_scalar(utm[:, g * D:(g + 1) * D],
                                    utm[:, g * D:(g + 1) * D],
                                    statc[:, C_G + 3:C_G + 4], None, ALU.mult)
                return i
            tk_u = S.op("V", v_u, deps=[tk_k2], tag=f"uev{g}")
            tk_uev.append(tk_u)

    if taps:
        tk_tap_u = S.dma("dG", lambda e: e.dma_start(
            out=tap["tap_u"].rearrange("(g p) c -> p g c", p=128),
            in_=utm[:].rearrange("p (g c) -> p g c", g=NG)), deps=tk_uev)
        tk_tap_v = S.dma("dG", lambda e: e.dma_start(
            out=tap["tap_v"].rearrange("(g p) c -> p g c", p=128),
            in_=vtm[:].rearrange("p (g c) -> p g c", g=NG)), deps=tk_vev)

    tk_kvt_ev = []
    kv_mm_toks = []
    for chalf in range(2):
        def t_kvp(e, chalf=chalf):
            for g in range(NG):
                for dch in range(DCH):
                    i = mm(e, bank(dch),
                           vtm[:, g * D + 128 * dch: g * D + 128 * dch + 128],
                           utm[:, g * D + 512 * chalf: g * D + 512 * chalf + 512],
                           start=g == 0, stop=g == NG - 1)
            return i
        tk_kvm = S.op("T", t_kvp, deps=tk_uev + tk_vev + bdep(*range(8)),
                      tag=f"kvp{chalf}")
        kv_mm_toks.append(tk_kvm)
        def v_kvev(e, chalf=chalf):
            for dch in range(DCH):
                i = e.tensor_copy(
                    kvtb[:, dch * 1024 + 512 * chalf:
                         dch * 1024 + 512 * chalf + 512], bank(dch))
            return i
        tk_ev = S.op("V", v_kvev, deps=[tk_kvm] + tk_kvmm + tk_qmm,
                     tag=f"kvtev{chalf}")
        for dch in range(DCH):
            set_bank(dch, tk_ev)
        tk_kvt_ev.append(tk_ev)

    tk_s2mm = []
    tk_u2_last = None
    for g in range(NG):
        tk_u2 = S.op("S", lambda e, g=g: e.activation(
            out=u2st[:], in_=utm[:, g * D:(g + 1) * D], func=ACT.Square),
            deps=[tk_uev[g]] + bufs["u2st"].war()
            + ([tk_s2mm[-1]] if tk_s2mm else []), tag=f"u2.{g}")
        tk_u2_last = tk_u2
        def t_s2(e, g=g):
            mm(e, bank(6), onescol[:], u2st[:, 0:512], start=g == 0,
               stop=g == NG - 1)
            i = mm(e, bank(7), onescol[:], u2st[:, 512:1024], start=g == 0,
                   stop=g == NG - 1)
            return i
        tk_mm2 = S.op("T", t_s2,
                      deps=[tk_u2, tk_consts] + (bdep(6, 7) if g == 0 else []),
                      tag=f"s2mm{g}")
        bufs["u2st"].consume(tk_mm2)
        tk_s2mm.append(tk_mm2)
    tk_s2ev = S.op("V", lambda e: (
        e.tensor_copy(arstage[0:1, 0:512], bank(6, 0, 1)),
        e.tensor_copy(arstage[0:1, 512:1024], bank(7, 0, 1)))[-1],
        deps=[tk_s2mm[-1]] + bufs["arstage"].war(), tag="s2ev")
    set_bank(6, tk_s2ev)
    set_bank(7, tk_s2ev)
    tk_s2dma = S.dma("dS", lambda e: e.dma_start(
        out=ar_in.ap()[D:D + 1, :], in_=arstage[0:1, 0:1024]),
        deps=[tk_s2ev], tag="s2dma")
    bufs["arstage"].consume(tk_s2dma)

    tk_wln = S.dma("dG", lambda e: e.dma_start(
        out=wlns[:].rearrange("p (c f) -> p c f", c=DCH),
        in_=wln_d.rearrange("(c p) f -> p c f", p=128)),
        deps=tk_kvmm + tk_qmm + tk_cskv, tag="ldwln")

    ar_dma = [tk_s2dma]
    tk_kvw_mm = []
    for cch in range(DCH):
        for dhalf in range(2):
            b = 2 + (cch * 2 + dhalf) % 2
            def t_kvw(e, cch=cch, dhalf=dhalf, b=b):
                for dch in range(DCH):
                    i = mm(e, bank(b),
                           kvtb[:, dch * 1024 + 128 * cch:
                                dch * 1024 + 128 * cch + 128],
                           wlns[:, dch * 1024 + 512 * dhalf:
                                dch * 1024 + 512 * dhalf + 512],
                           start=dch == 0, stop=dch == DCH - 1)
                return i
            tk_m = S.op("T", t_kvw, deps=tk_kvt_ev + [tk_wln] + bdep(b),
                        tag=f"kvw{cch}.{dhalf}")
            tk_kvw_mm.append(tk_m)
            tk_st = S.op("V", lambda e, b=b: e.tensor_copy(
                arstage[:, 0:512], bank(b)),
                deps=[tk_m] + bufs["arstage"].war(), tag=f"kvwst{cch}.{dhalf}")
            set_bank(b, tk_st)
            tk_d = S.dma("dS", lambda e, cch=cch, dhalf=dhalf: e.dma_start(
                out=ar_in.ap()[128 * cch:128 * (cch + 1),
                               512 * dhalf:512 * dhalf + 512],
                in_=arstage[:, 0:512]), deps=[tk_st],
                tag=f"kvwdma{cch}.{dhalf}")
            bufs["arstage"].consume(tk_d)
            ar_dma.append(tk_d)

    if taps:
        tk_tap_arin = S.dma("dG", lambda e: e.dma_start(
            out=tap["tap_arin"], in_=ar_in.ap()), deps=ar_dma)
        tk_tap_kvt = S.dma("dG", lambda e: e.dma_start(
            out=tap["tap_kvt"].rearrange("(c p) f -> p c f", p=128),
            in_=kvtb[:].rearrange("p (c f) -> p c f", c=DCH)), deps=tk_kvt_ev)

    tk_cc = S.op("G", lambda e: e.collective_compute(
        "AllReduce", ALU.add, replica_groups=[[0, 1, 2, 3], [4, 5, 6, 7]],
        ins=[ar_in.ap().opt()], outs=[ar_out.ap().opt()]),
        deps=ar_dma + ([tk_tap_arin] if taps else []), tag="allreduce")

    vf_war = kv_mm_toks + [tk_u2_last] + ([tk_tap_u, tk_tap_v] if taps else [])
    tk_vf = []
    for g in range(NG):
        for dch in range(DCH):
            tk_vf.append(S.dma("dY", lambda e, g=g, dch=dch: e.dma_start_transpose(
                out=vF[:, dch * T + 128 * g: dch * T + 128 * g + 128],
                in_=vtm[:, g * D + 128 * dch: g * D + 128 * dch + 128]),
                deps=vf_war, tag=f"vf{g}.{dch}"))

    tk_vw_res = []
    for sub in range(NSUB):
        t0 = sub * 512
        vf_dep = [tk_vf[(sub * 4 + j) * DCH + dch]
                  for j in range(4) for dch in range(DCH)]
        for m in range(DCH):
            b = m % 2
            def t_vw(e, m=m, sub=sub, b=b):
                for dch in range(DCH):
                    i = mm(e, bank(b),
                           wlns[:, dch * 1024 + 128 * m:
                                dch * 1024 + 128 * m + 128],
                           vF[:, dch * T + 512 * sub: dch * T + 512 * sub + 512],
                           start=dch == 0, stop=dch == DCH - 1)
                return i
            tk_m = S.op("T", t_vw, deps=vf_dep + [tk_wln] + bdep(b),
                        tag=f"vw{sub}.{m}")
            def v_r(e, m=m, t0=t0, b=b):
                return e.scalar_tensor_tensor(
                    sx[:, m * T + t0: m * T + t0 + 512], bank(b),
                    blnc[:, m:m + 1], sx[:, m * T + t0: m * T + t0 + 512],
                    ALU.add, ALU.add)
            tk_r = S.op("V", v_r, deps=[tk_m, tk_blnc], tag=f"vwres{sub}.{m}")
            set_bank(b, tk_r)
            tk_vw_res.append(tk_r)

    qf_war = kv_mm_toks + [tk_vf[-1]] + ([tk_tap_v] if taps else [])
    tk_qf = []
    for g in range(NG):
        for cch in range(DCH):
            tk_qf.append(S.dma("dY", lambda e, g=g, cch=cch: e.dma_start_transpose(
                out=q2F[:, cch * T + 128 * g: cch * T + 128 * g + 128],
                in_=q2_dram.ap()[128 * g:128 * (g + 1),
                                 128 * cch:128 * (cch + 1)]),
                deps=qf_war + [tk_q2_dma[g]], tag=f"qf{g}.{cch}"))

    tk_s2l = S.dma("dS", lambda e: e.dma_start(
        out=arstage[0:1, :], in_=ar_out.ap()[D:D + 1, :]),
        deps=[tk_cc] + bufs["arstage"].war(), tag="lds2")
    tk_rcp_rows = []
    for r in range(2):
        tk_rr = S.op("V", lambda e, r=r: (
            S.vf(e, e.tensor_copy(trh[0:1, :],
                                  arstage[0:1, 512 * r:512 * r + 512])),
            e.scalar_tensor_tensor(trl[0:1, :],
                                   arstage[0:1, 512 * r:512 * r + 512], 1.0,
                                   trh[0:1, :], ALU.mult, ALU.subtract)
        )[-1], deps=[tk_s2l] + bufs["rowstage"].war(), tag=f"rcprow{r}")
        def t_rc(e, r=r):
            for j in range(4):
                mm(e, bank(6, c0=4 * r + j, c1=4 * r + j + 1),
                   trh[:, 128 * j:128 * (j + 1)], idenb[:, 0:1],
                   start=True, stop=False)
                i = mm(e, bank(6, c0=4 * r + j, c1=4 * r + j + 1),
                       trl[:, 128 * j:128 * (j + 1)], idenb[:, 0:1],
                       start=False, stop=True)
            return i
        tk_rm = S.op("T", t_rc, deps=[tk_rr, tk_idenb]
                     + (bdep(6) if r == 0 else []), tag=f"rcpmm{r}")
        bufs["rowstage"].consume(tk_rm)
        tk_rcp_rows.append(tk_rm)
    tk_rcpt = tk_rcp_rows[-1]
    bufs["arstage"].consume(tk_rcpt)
    tk_rcpc = S.op("V", lambda e: e.tensor_copy(
        statc[:, C_RCP:C_RCP + 8], bank(6, c0=0, c1=8)), deps=[tk_rcpt],
        tag="rcpcp")
    set_bank(6, tk_rcpc)
    tk_rcps = S.op("S", lambda e: e.activation(
        out=statc[:, C_RCP:C_RCP + 8], in_=statc[:, C_RCP:C_RCP + 8],
        func=ACT.Sqrt, scale=float(SEQ)), deps=[tk_rcpc], tag="rcpsqrt")
    def v_rcp(e):
        S.vf(e, e.tensor_scalar_max(statc[:, C_RCP:C_RCP + 8],
                                    statc[:, C_RCP:C_RCP + 8],
                                    EPS_NORM * math.sqrt(SEQ)))
        S.vf(e, e.reciprocal(statc[:, C_RCP:C_RCP + 8],
                             statc[:, C_RCP:C_RCP + 8]))
        return vtail(e)
    tk_rcp = S.op("V", v_rcp, deps=[tk_rcps], tag="rcp")

    tk_kvwb = []
    for cch in range(DCH):
        tk_l = S.dma("dS", lambda e, cch=cch: e.dma_start(
            out=arstage[:], in_=ar_out.ap()[128 * cch:128 * (cch + 1), :]),
            deps=[tk_s2l] + bufs["arstage"].war(), tag=f"ldkvw{cch}")
        tk_c = S.op("V", lambda e, cch=cch: e.tensor_scalar(
            kvwb[:, cch * 1024:(cch + 1) * 1024], arstage[:],
            statc[:, C_RCP + cch:C_RCP + cch + 1], None, ALU.mult),
            deps=[tk_l, tk_rcp] + (tk_kvw_mm if cch == 0 else []),
            tag=f"kvwb{cch}")
        bufs["arstage"].consume(tk_c)
        tk_kvwb.append(tk_c)

    if taps:
        tk_tap_arout = S.dma("dG", lambda e: e.dma_start(
            out=tap["tap_arout"], in_=ar_out.ap()), deps=[tk_cc])
        tk_tap_q2 = S.dma("dG", lambda e: e.dma_start(
            out=tap["tap_q2"], in_=q2_dram.ap()), deps=tk_q2_dma)

    tk_t2_res = []
    for sub in range(NSUB):
        t0 = sub * 512
        qf_dep = [tk_qf[(sub * 4 + j) * DCH + cch]
                  for j in range(4) for cch in range(DCH)]
        for m in range(DCH):
            b = 2 + m % 2
            def t_t2(e, m=m, sub=sub, b=b):
                for cch in range(DCH):
                    i = mm(e, bank(b),
                           kvwb[:, cch * 1024 + 128 * m:
                                cch * 1024 + 128 * m + 128],
                           q2F[:, cch * T + 512 * sub: cch * T + 512 * sub + 512],
                           start=cch == 0, stop=cch == DCH - 1)
                return i
            tk_m = S.op("T", t_t2, deps=qf_dep + tk_kvwb + bdep(b),
                        tag=f"t2.{sub}.{m}")
            def v_r(e, m=m, t0=t0, b=b):
                return e.scalar_tensor_tensor(
                    sx[:, m * T + t0: m * T + t0 + 512], bank(b), 1.0,
                    sx[:, m * T + t0: m * T + t0 + 512], ALU.mult, ALU.add)
            tk_r = S.op("V", v_r, deps=[tk_m], tag=f"t2res{sub}.{m}")
            set_bank(b, tk_r)
            tk_t2_res.append(tk_r)

    if taps:
        tap_x2 = [S.dma("dY", lambda e, ch=ch: e.dma_start(
            out=tap["tap_x2"][128 * ch:128 * (ch + 1), :],
            in_=sx[:, ch * T:(ch + 1) * T]), deps=tk_t2_res + tk_vw_res)
            for ch in range(DCH)]

    tk_w1t2 = S.dma("dG", lambda e: e.dma_start(
        out=w1t[:].rearrange("p (c r) -> p c r", c=DCH),
        in_=w1t_d.rearrange("(c p) r -> p c r", p=128)),
        deps=tk_t2_res + tk_vw_res + tk_kvw_mm, tag="ldw1t2")
    tk_w2t2 = S.dma("dG", lambda e: e.dma_start(
        out=w2t[:].rearrange("p (c d) -> p c d", c=4),
        in_=w2t_d.rearrange("(c p) d -> p c d", p=128)),
        deps=tk_t2_res + tk_vw_res + tk_kvw_mm, tag="ldw2t2")
    ffn2_w = [tk_w1t2, tk_w2t2, tk_csw1, tk_csbg, tk_consts]
    res_f2 = emit_ffn(tk_t2_res + tk_vw_res + [tk_w1t2, tk_w2t2], ffn2_w,
                      tag="f2")

    tk_out = [S.dma("dY", lambda e, ch=ch: e.dma_start(
        out=out_t[128 * ch:128 * (ch + 1), :],
        in_=sx[:, ch * T:(ch + 1) * T]), deps=res_f2, tag=f"out{ch}")
        for ch in range(DCH)]
    if taps:
        tk_out += [tk_tap_arout, tk_tap_q2, tk_tap_kvt] + tap_f1 + tap_x2
        tk_tap_st = S.dma("dY", lambda e: e.dma_start(
            out=tap["tap_stat"], in_=statc[:]), deps=res_f2)
        tk_tap_gw = S.dma("dY", lambda e: e.dma_start(
            out=tap["tap_gw"], in_=gw[:]), deps=res_f2)
        tk_out += [tk_tap_st, tk_tap_gw]
    S.op("Y", lambda e: e.nop(), deps=tk_out, tag="final")

    S.prepare(nc, serialize=serialize, ser_range=ser_range)
    with nc.Block() as block:
        @block.tensor
        def _(eng):
            S.emit_engine("T", eng)
        @block.vector
        def _(eng):
            S.emit_engine("V", eng)
        @block.scalar
        def _(eng):
            S.emit_engine("S", eng)
        @block.gpsimd
        def _(eng):
            S.emit_engine("G", eng)
        @block.sync
        def _(eng):
            S.emit_engine("Y", eng)
    return nc


def host_inputs(inputs, core):
    x = inputs["L0_attn_x"]
    b, s = core // 4, core % 4
    xt = np.ascontiguousarray(np.asarray(x)[b, s * T:(s + 1) * T, :].T)
    sel = np.zeros((128, 512), dtype=ml_dtypes.bfloat16)
    for r in range(512):
        sel[r // 64, r] = 1.0
    jsel = np.zeros((128, 512), dtype=ml_dtypes.bfloat16)
    for n in range(512):
        jsel[32 * (n // 128), n] = 1.0
    W1 = np.asarray(inputs["L0_ffn_W1"], dtype=np.float32)
    W2 = np.asarray(inputs["L0_ffn_W2"], dtype=np.float32)
    Wg = np.asarray(inputs["L0_ffn_Wg"], dtype=np.float32)
    wgt = np.zeros((D, 128), dtype=np.float32)
    wgt[:, 0:8] = Wg.T
    return {
        "xt": xt.astype(np.float32),
        "wqkv": np.ascontiguousarray(np.asarray(inputs["L0_attn_Wqkv"],
                                                dtype=np.float32)),
        "bqkv": np.asarray(inputs["L0_attn_bqkv"],
                           dtype=np.float32).reshape(1, 3 * D),
        "wln": np.ascontiguousarray(np.asarray(inputs["L0_attn_Wln"],
                                               dtype=np.float32)),
        "blnc": np.ascontiguousarray(
            np.asarray(inputs["L0_attn_bln"], dtype=np.float32)
            .reshape(8, 128).T),
        "w1t": np.ascontiguousarray(W1.reshape(512, D).T),
        "w2t": np.ascontiguousarray(W2.transpose(0, 2, 1).reshape(512, D)),
        "wgt": wgt,
        "bg": np.asarray(inputs["L0_ffn_bg"], dtype=np.float32).reshape(1, 8),
        "iden32": np.eye(128, dtype=np.float32),
        "idenb": np.eye(128, dtype=ml_dtypes.bfloat16),
        "selb": sel,
        "jsel": jsel,
    }


def kernel(**inputs):
    nc = build(taps=False, ser_range=(0, 100))
    in_maps = [host_inputs(inputs, c) for c in range(8)]
    res = run_bass_kernel_spmd(nc, in_maps, core_ids=list(range(8)))
    out = np.empty((2, SEQ, D), dtype=np.float32)
    for c in range(8):
        b, s = c // 4, c % 4
        out[b, s * T:(s + 1) * T, :] = res.results[c]["out_t"].T
    return out


# version 04
# speedup vs baseline: 1.6269x; 1.0151x over previous
import math
import numpy as np
import ml_dtypes

import concourse.bass as bass
import concourse.mybir as mybir
from concourse.bass_utils import run_bass_kernel_spmd

F32 = mybir.dt.float32
BF16 = mybir.dt.bfloat16
AX = mybir.AxisListType
ALU = mybir.AluOpType
ACT = mybir.ActivationFunctionType

D = 1024
T = 2048
NG = 16
NSUB = 4
SEQ = 8192
DCH = 8
EPS_LN = 1e-5
EPS_SUM = 1e-7
EPS_NORM = 1e-12
BIG = 1e30


class Buf:
    def __init__(self):
        self.cons = []

    def consume(self, tok):
        self.cons.append(tok)

    def war(self):
        c, self.cons = self.cons, []
        return c


class Sched:
    DMAQ = {"dY": "Y", "dS": "S", "dG": "G"}

    def __init__(self):
        self.ops = []

    def op(self, eng, fn, deps=(), tag=""):
        self.ops.append(dict(eng=eng, fn=fn, deps=sorted(set(deps)), q=eng,
                             step=1, tag=tag))
        return len(self.ops) - 1

    def dma(self, q, fn, deps=(), tag=""):
        self.ops.append(dict(eng=self.DMAQ[q], fn=fn, deps=sorted(set(deps)),
                             q=q, step=16, tag=tag))
        return len(self.ops) - 1

    def prepare(self, nc, serialize=False, ser_range=None):
        self.serialize = serialize
        if serialize:
            ser_range = (0, len(self.ops))
        if ser_range is not None:
            lo, hi = ser_range
            for i in range(max(1, lo), min(hi, len(self.ops))):
                self.ops[i]["deps"] = sorted(set(self.ops[i]["deps"] + [i - 1]))
            self.serialize = True

        self.fsem = nc.alloc_semaphore(name="vfence")
        self.fcount = [0]
        n = len(self.ops)
        needed = set()
        for o in self.ops:
            for d in o["deps"]:
                assert 0 <= d < n
                needed.add(d)
        for i, o in enumerate(self.ops):
            if o["step"] == 16:
                needed.add(i)
        self.qs = {}
        for i, o in enumerate(self.ops):
            q = o["q"]
            if q not in self.qs:
                self.qs[q] = dict(count=0, sem=nc.alloc_semaphore(name=f"q{q}"))
            o["inc"] = i in needed
            if o["inc"]:
                self.qs[q]["count"] += o["step"]
            o["after"] = (q, self.qs[q]["count"])

    def vf(self, eng, inst):
        self.fcount[0] += 1
        inst.then_inc(self.fsem, 1)
        eng.wait_ge(self.fsem, self.fcount[0])
        return inst

    def emit_engine(self, ename, eng):
        for i, o in enumerate(self.ops):
            if o["eng"] != ename:
                continue
            waits = {}
            for d in self.ops[i]["deps"]:
                do = self.ops[d]
                if (do["q"] == o["q"] and o["step"] == 1 and do["step"] == 1
                        and not self.serialize):
                    continue
                dq, dv = do["after"]
                waits[dq] = max(waits.get(dq, 0), dv)
            for wq in sorted(waits):
                eng.wait_ge(self.qs[wq]["sem"], waits[wq])
            inst = o["fn"](eng)
            if o["inc"]:
                assert inst is not None, f"op {i} ({o['tag']}) returned None"
                inst.then_inc(self.qs[o["q"]]["sem"], o["step"])


def build(taps=False, serialize=False, ser_range=None):
    nc = bass.Bass("TRN2", target_bir_lowering=False, debug=False)

    xt = nc.dram_tensor("xt", [D, T], F32, kind="ExternalInput").ap()
    wqkv_d = nc.dram_tensor("wqkv", [D, 3 * D], F32, kind="ExternalInput").ap()
    bqkv_d = nc.dram_tensor("bqkv", [1, 3 * D], F32, kind="ExternalInput").ap()
    wln_d = nc.dram_tensor("wln", [D, D], F32, kind="ExternalInput").ap()
    blnc_d = nc.dram_tensor("blnc", [128, 8], F32, kind="ExternalInput").ap()
    w1t_d = nc.dram_tensor("w1t", [D, 512], F32, kind="ExternalInput").ap()
    w2t_d = nc.dram_tensor("w2t", [512, D], F32, kind="ExternalInput").ap()
    wgt_d = nc.dram_tensor("wgt", [D, 128], F32, kind="ExternalInput").ap()
    bg_d = nc.dram_tensor("bg", [1, 8], F32, kind="ExternalInput").ap()
    iden32_d = nc.dram_tensor("iden32", [128, 128], F32, kind="ExternalInput").ap()
    idenb_d = nc.dram_tensor("idenb", [128, 128], BF16, kind="ExternalInput").ap()
    selb_d = nc.dram_tensor("selb", [128, 512], BF16, kind="ExternalInput").ap()
    jsel_d = nc.dram_tensor("jsel", [128, 512], BF16, kind="ExternalInput").ap()
    out_t = nc.dram_tensor("out_t", [D, T], F32, kind="ExternalOutput").ap()

    q2_dram = nc.dram_tensor("q2_dram", [T, D], BF16, kind="Internal")
    ar_in = nc.dram_tensor("ar_in", [D + 1, D], F32, kind="Internal")
    ar_out = nc.dram_tensor("ar_out", [D + 1, D], F32, kind="Internal")

    tap = {}
    if taps:
        for name, shape, dt in [
            ("tap_f1", [D, T], F32), ("tap_x2", [D, T], F32),
            ("tap_u", [T, D], F32), ("tap_v", [T, D], F32),
            ("tap_q2", [T, D], F32), ("tap_arin", [D + 1, D], F32),
            ("tap_arout", [D + 1, D], F32), ("tap_kvt", [D, D], F32),
            ("tap_stat", [128, 112], F32), ("tap_gw", [128, 128], F32),
            ("tap_gl1", [128, 128], F32), ("tap_st1", [128, 48], F32),
            ("tap_gw1", [128, 128], F32),
        ]:
            tap[name] = nc.dram_tensor(name, shape, dt, kind="ExternalOutput").ap()

    arena = nc.alloc_sbuf_tensor("arena", [128, 208500], mybir.dt.uint8)
    base = nc.lookup_mloc(arena).addr
    off = [0]
    def sball(name, shape, dtype):
        nbytes = int(np.prod(shape[1:])) * mybir.dt.size(dtype)
        t = nc.alloc_sbuf_tensor_at(name, shape, dtype, offset=base + off[0])
        off[0] += (nbytes + 31) & ~31
        return t

    sx = sball("sx", [128, DCH * T], F32)
    wgt = sball("wgt", [128, DCH * 128], BF16)
    onescol = sball("onescol", [128, 128], BF16)
    onesrow = sball("onesrow", [128, 128], BF16)
    iden32 = sball("iden32", [128, 128], F32)
    idenb = sball("idenb", [128, 128], BF16)
    selb = sball("selb", [128, 512], BF16)
    aux_w = sball("aux_w", [128, 2048], BF16)
    aux_w1 = sball("aux_w1", [128, 512], BF16)
    bgrow = sball("bgrow", [1, 8], BF16)
    cswgrow = sball("cswgrow", [1, 8], BF16)
    aux4 = [sball(f"aux4_{j}", [128, 128], BF16) for j in range(4)]
    trh = sball("trh", [128, 512], BF16)
    trl = sball("trl", [128, 512], BF16)
    trC = sball("trC", [128, 256], BF16)
    r4pad = sball("r4pad", [128, 128], BF16)
    jsel = sball("jsel", [128, 512], BF16)
    onescolB = sball("onescolB", [128, 128], BF16)
    blnc = sball("blnc", [128, 8], F32)
    czero = sball("czero", [128, 1], F32)
    vdmy = sball("vdmy", [128, 2], F32)
    cepsln = sball("cepsln", [128, 1], F32)
    statc = sball("statc", [128, 112], F32)
    gl = sball("gl", [128, 128], F32)
    gtmp = sball("gtmp", [128, 128], F32)
    gw = sball("gw", [128, 128], F32)
    gwb = sball("gwb", [128, 256], BF16)
    csbg = sball("csbg", [128, 16], F32)
    lrowh = sball("lrowh", [128, 512], BF16)
    lrowl = sball("lrowl", [128, 512], BF16)
    wtg128 = sball("wtg128", [128, 128], BF16)
    arstage = sball("arstage", [128, 1024], F32)
    u2st = sball("u2st", [128, 1024], BF16)
    q2st = sball("q2st", [128, 1024], BF16)
    xs2 = sball("xs2", [128, DCH * 512], BF16)
    shared = sball("shared", [128, 38912], mybir.dt.uint8)
    vtm = sball("vtm", [128, NG * D], BF16)
    utm = sball("utm", [128, NG * D], BF16)
    assert off[0] <= 208500, f"SBUF over budget: {off[0]}"

    shbase = nc.lookup_mloc(shared).addr
    def shv(name, shape, dtype, byte_off):
        nbytes = int(np.prod(shape[1:])) * mybir.dt.size(dtype)
        assert byte_off + nbytes <= 38912
        return nc.alloc_sbuf_tensor_at(name, shape, dtype,
                                       offset=shbase + byte_off)

    w1t = shv("w1t", [128, DCH * 512], BF16, 0)
    w2t = shv("w2t", [128, 4 * 1024], BF16, 8192)
    xbsub = shv("xbsub", [128, DCH * 512], BF16, 16384)
    xqsub = shv("xqsub", [128, DCH * 512], BF16, 24576)
    xssub = shv("xssub", [128, DCH * 512], BF16, 16384)
    hg = shv("hg", [128, 4 * 512], F32, 24576)
    hp = shv("hp", [128, 4 * 512], BF16, 32768)
    wqkv = shv("wqkv", [128, DCH * 2048], BF16, 0)
    wlns = shv("wlns", [128, DCH * 1024], BF16, 0)
    kvtb = shv("kvtb", [128, DCH * 1024], BF16, 16384)
    kvwb = shv("kvwb", [128, DCH * 1024], BF16, 16384)

    vF = nc.alloc_sbuf_tensor_at("vF", [128, DCH * T], BF16,
                                 offset=nc.lookup_mloc(utm).addr)
    q2F = nc.alloc_sbuf_tensor_at("q2F", [128, DCH * T], BF16,
                                  offset=nc.lookup_mloc(vtm).addr)

    pall = nc.alloc_psum_tensor("pall", [128, 4096], F32)
    def bank(b, p0=0, p1=128, c0=0, c1=512):
        return pall[p0:p1, 512 * b + c0: 512 * b + c1]
    def bankb16(b, p0=0, p1=128, c0=0, c1=1024):
        return pall[p0:p1, 512 * b: 512 * (b + 1)].bitcast(BF16)[:, c0:c1]

    C_M, C_RSTD, C_MR = 0, 16, 32
    C_T0, C_T1, C_T2 = 48, 64, 80
    C_RCP, C_G = 96, 104

    S = Sched()
    S.dbg = {}
    bank_tok = {}
    def bdep(*bs):
        return [bank_tok[b] for b in bs if b in bank_tok]
    def set_bank(b, tok):
        bank_tok[b] = tok

    bufs = dict(xbsub=Buf(), xqsub=Buf(), xssub=Buf(), xs2=Buf(),
                trC=Buf(), r4pad=Buf(), aux4=Buf(), rowstage=Buf(),
                lrow=Buf(), wtg=Buf(), hg=Buf(), hp=Buf(), arstage=Buf(),
                aux_w=Buf(), cg=Buf(), q2st=Buf(), u2st=Buf())

    def mm(e, out, lhsT, rhs, start, stop=True):
        return e.matmul(out, lhsT, rhs, start=start, stop=stop)

    def vtail(e):
        return e.tensor_copy(vdmy[:, 0:1], vdmy[:, 1:2])

    tk_x = [S.dma("dY", lambda e, ch=ch: e.dma_start(
        out=sx[:, ch * T:(ch + 1) * T], in_=xt[128 * ch:128 * (ch + 1), :]),
        tag=f"ldx{ch}") for ch in range(DCH)]
    tk_iden32 = S.dma("dY", lambda e: e.dma_start(out=iden32[:], in_=iden32_d))
    tk_idenb = S.dma("dY", lambda e: e.dma_start(out=idenb[:], in_=idenb_d))
    tk_selb = S.dma("dY", lambda e: e.dma_start(out=selb[:], in_=selb_d))
    tk_jsel = S.dma("dY", lambda e: e.dma_start(out=jsel[:], in_=jsel_d))
    tk_blnc = S.dma("dY", lambda e: e.dma_start(out=blnc[:], in_=blnc_d))

    tk_w1t = S.dma("dG", lambda e: e.dma_start(
        out=w1t[:].rearrange("p (c r) -> p c r", c=DCH),
        in_=w1t_d.rearrange("(c p) r -> p c r", p=128)), tag="ldw1t")
    tk_w2t = S.dma("dG", lambda e: e.dma_start(
        out=w2t[:].rearrange("p (c d) -> p c d", c=4),
        in_=w2t_d.rearrange("(c p) d -> p c d", p=128)), tag="ldw2t")
    tk_wgt = S.dma("dG", lambda e: e.dma_start(
        out=wgt[:].rearrange("p (c f) -> p c f", c=DCH),
        in_=wgt_d.rearrange("(c p) f -> p c f", p=128)), tag="ldwgt")
    tk_bg = S.dma("dG", lambda e: e.dma_start(out=bgrow[:], in_=bg_d))

    tk_consts = S.op("V", lambda e: (
        e.memset(onescol[:], 0.0), e.memset(onescol[:, 0:1], 1.0),
        e.memset(onesrow[:], 0.0), e.memset(onesrow[0:1, :], 1.0),
        e.memset(aux4[0][:], 0.0), e.memset(aux4[1][:], 0.0),
        e.memset(aux4[2][:], 0.0), e.memset(aux4[3][:], 0.0),
        e.memset(aux4[0][0:1, :], 1.0), e.memset(aux4[1][0:1, :], 1.0),
        e.memset(aux4[2][0:1, :], 1.0), e.memset(aux4[3][0:1, :], 1.0),
        e.memset(trh[:], 0.0), e.memset(trl[:], 0.0),
        e.memset(trC[:], 0.0), e.memset(r4pad[:], 0.0),
        e.memset(onescolB[:], 0.0), e.memset(onescolB[:, 1:2], 1.0),
        e.memset(lrowh[:], 0.0), e.memset(lrowl[:], 0.0),
        e.memset(wtg128[:], 0.0), e.memset(aux_w[:], 0.0),
        e.memset(aux_w1[:], 0.0), e.memset(czero[:], 0.0),
        e.memset(vdmy[:], 0.0),
        e.memset(cepsln[:], EPS_LN))[-1], tag="consts")
    nc.const_aps.register(F32, 0.0, czero[:])

    def t_csw1(e):
        for ch in range(DCH):
            i = mm(e, bank(0), onescol[:], w1t[:, ch * 512:(ch + 1) * 512],
                   start=ch == 0)
        return i
    tk_csw1m = S.op("T", t_csw1, deps=[tk_w1t, tk_consts], tag="csw1m")
    tk_csw1 = S.op("V", lambda e: e.tensor_copy(aux_w1[32:33, :],
                                                bank(0, 0, 1)),
                   deps=[tk_csw1m], tag="csw1")
    set_bank(0, tk_csw1)

    def t_cswg(e):
        for ch in range(DCH):
            i = mm(e, bank(1, c0=0, c1=128), onescol[:],
                   wgt[:, ch * 128:(ch + 1) * 128], start=ch == 0)
        return i
    tk_cswgm = S.op("T", t_cswg, deps=[tk_wgt, tk_consts], tag="cswgm")
    tk_cswg = S.op("V", lambda e: e.tensor_copy(cswgrow[:],
                                                bank(1, 0, 1, 0, 8)),
                   deps=[tk_cswgm], tag="cswg")
    set_bank(1, tk_cswg)
    tk_csbgm = S.op("T", lambda e: (
        mm(e, bank(2, c0=0, c1=8), onesrow[0:1, :], cswgrow[:], start=True),
        mm(e, bank(2, c0=8, c1=16), onesrow[0:1, :], bgrow[:],
           start=True))[-1], deps=[tk_cswg, tk_bg, tk_consts], tag="csbgm")
    tk_csbg = S.op("V", lambda e: e.tensor_copy(csbg[:], bank(2, c0=0, c1=16)),
                   deps=[tk_csbgm], tag="csbg")
    set_bank(2, tk_csbg)

    def emit_stats(xdeps, extra=None, tag=""):
        tr_toks = []
        for sub in range(NSUB):
            t0 = sub * 512
            def v_xb(e, t0=t0):
                for ch in range(DCH):
                    i = e.tensor_copy(xbsub[:, ch * 512:(ch + 1) * 512],
                                      sx[:, ch * T + t0: ch * T + t0 + 512])
                return i
            tk_xb = S.op("V", v_xb, deps=list(xdeps) + bufs["xbsub"].war(),
                         tag=f"xb.{tag}{sub}")
            def v_xq(e, t0=t0):
                for ch in range(DCH):
                    i = e.tensor_tensor(
                        xqsub[:, ch * 512:(ch + 1) * 512],
                        sx[:, ch * T + t0: ch * T + t0 + 512],
                        sx[:, ch * T + t0: ch * T + t0 + 512], ALU.mult)
                return i
            tk_xq = S.op("V", v_xq, deps=list(xdeps) + bufs["xqsub"].war(),
                         tag=f"xq.{tag}{sub}")
            def t_sx(e):
                for ch in range(DCH):
                    mm(e, bank(7), onescol[:],
                       xbsub[:, ch * 512:(ch + 1) * 512], start=ch == 0,
                       stop=False)
                for ch in range(DCH):
                    i = mm(e, bank(7), onescolB[:],
                           xqsub[:, ch * 512:(ch + 1) * 512], start=False,
                           stop=ch == DCH - 1)
                return i
            tk_sx = S.op("T", t_sx, deps=[tk_xb, tk_xq, tk_consts] + bdep(7),
                         tag=f"Sx.{tag}{sub}")
            bufs["xbsub"].consume(tk_sx)
            bufs["xqsub"].consume(tk_sx)
            tk_rs = S.op("V", lambda e: (
                S.vf(e, e.tensor_copy(trh[0:2, :], bank(7, 0, 2))),
                e.scalar_tensor_tensor(trl[0:2, :], bank(7, 0, 2), 1.0,
                                       trh[0:2, :], ALU.mult, ALU.subtract)
            )[-1], deps=[tk_sx] + bufs["rowstage"].war(),
                tag=f"rows.{tag}{sub}")
            set_bank(7, tk_rs)
            def t_tr(e, sub=sub):
                for j in range(4):
                    g = sub * 4 + j
                    mm(e, bank(5, c0=2 * g, c1=2 * g + 2),
                       trh[:, 128 * j:128 * (j + 1)], idenb[:, 0:2],
                       start=True, stop=False)
                    i = mm(e, bank(5, c0=2 * g, c1=2 * g + 2),
                           trl[:, 128 * j:128 * (j + 1)], idenb[:, 0:2],
                           start=False, stop=True)
                return i
            tk_tr = S.op("T", t_tr,
                         deps=[tk_rs, tk_idenb] + (bdep(5) if sub == 0 else []),
                         tag=f"trS.{tag}{sub}")
            bufs["rowstage"].consume(tk_tr)
            tr_toks.append(tk_tr)
            if extra is not None:
                extra(sub, tk_xb, tk_sx)
        sx_cols = pall[:, 5 * 512:5 * 512 + 32].rearrange(
            "p (g t) -> p g t", t=2)
        def v_cm(e):
            S.vf(e, e.tensor_scalar(statc[:, C_M:C_M + 16],
                                    sx_cols[:, :, 0:1].squeeze(2),
                                    -1.0 / D, None, ALU.mult))
            S.vf(e, e.tensor_scalar(statc[:, C_T0:C_T0 + 16],
                                    sx_cols[:, :, 1:2].squeeze(2),
                                    1.0 / D, None, ALU.mult))
            S.vf(e, e.tensor_tensor(statc[:, C_T1:C_T1 + 16],
                                    statc[:, C_M:C_M + 16],
                                    statc[:, C_M:C_M + 16], ALU.mult))
            i = e.tensor_tensor(statc[:, C_T2:C_T2 + 16],
                                statc[:, C_T0:C_T0 + 16],
                                statc[:, C_T1:C_T1 + 16], ALU.subtract)
            return i
        tk_cm = S.op("V", v_cm, deps=tr_toks, tag=f"colm.{tag}")
        set_bank(5, tk_cm)
        tk_sq = S.op("S", lambda e: e.activation(
            out=statc[:, C_T0:C_T0 + 16], in_=statc[:, C_T2:C_T2 + 16],
            func=ACT.Sqrt, bias=cepsln[:]), deps=[tk_cm], tag=f"sqrt.{tag}")
        def v_cm2(e):
            S.vf(e, e.reciprocal(statc[:, C_RSTD:C_RSTD + 16],
                                 statc[:, C_T0:C_T0 + 16]))
            S.vf(e, e.tensor_tensor(statc[:, C_MR:C_MR + 16],
                                    statc[:, C_M:C_M + 16],
                                    statc[:, C_RSTD:C_RSTD + 16], ALU.mult))
            return vtail(e)
        tk_cm2 = S.op("V", v_cm2, deps=[tk_sq], tag=f"colm2.{tag}")
        return tk_cm2

    def emit_rows(sub, tk_stats, tag=""):
        def v_tc(e, sub=sub):
            e.tensor_copy(trC[:, 0:128:32],
                          statc[:, C_RSTD + 4 * sub:C_RSTD + 4 * sub + 4])
            i = e.tensor_copy(trC[:, 128:256:32],
                              statc[:, C_MR + 4 * sub:C_MR + 4 * sub + 4])
            return i
        tk_tc = S.op("V", v_tc, deps=[tk_stats] + bufs["trC"].war(),
                     tag=f"trC.{tag}{sub}")
        def t_cr(e):
            mm(e, bank(7, c0=0, c1=128), trC[:, 0:128], idenb[:], start=True)
            i = mm(e, bank(7, c0=128, c1=256), trC[:, 128:256], idenb[:],
                   start=True)
            return i
        tk_cr = S.op("T", t_cr, deps=[tk_tc, tk_idenb] + bdep(7),
                     tag=f"crMM.{tag}{sub}")
        bufs["trC"].consume(tk_cr)
        def v_rows(e):
            for j in range(4):
                e.tensor_copy(r4pad[32 * j:32 * j + 1, :],
                              bank(7, p0=32 * j, p1=32 * j + 1, c0=0, c1=128))
            for j in range(4):
                i = e.tensor_copy(
                    aux4[j][32:33, :],
                    bank(7, p0=32 * j, p1=32 * j + 1, c0=128, c1=256))
            return i
        tk_rows = S.op("V", v_rows,
                       deps=[tk_cr] + bufs["r4pad"].war() + bufs["aux4"].war(),
                       tag=f"rows4.{tag}{sub}")
        set_bank(7, tk_rows)
        return tk_rows, tk_rows

    def emit_xs(sub, tk_rp, dst, dstbuf, tag=""):
        def t_bc(e):
            for j in range(4):
                i = mm(e, bank(6, c0=128 * j, c1=128 * (j + 1)),
                       jsel[:, 128 * j:128 * (j + 1)], r4pad[:],
                       start=True)
            return i
        tk_bc = S.op("T", t_bc, deps=[tk_rp, tk_jsel] + bdep(6),
                     tag=f"bc.{tag}{sub}")
        bufs["r4pad"].consume(tk_bc)
        t0 = sub * 512
        def v_xs(e, t0=t0):
            for ch in range(DCH):
                i = e.tensor_tensor(dst[:, ch * 512:(ch + 1) * 512],
                                    sx[:, ch * T + t0: ch * T + t0 + 512],
                                    bank(6), ALU.mult)
            return i
        tk_xs = S.op("V", v_xs, deps=[tk_bc] + bufs[dstbuf].war(),
                     tag=f"xs.{tag}{sub}")
        set_bank(6, tk_xs)
        return tk_xs

    def emit_ffn(xdeps, wdeps, tag):
        logit_tr = []
        def sweep1_extra(sub, tk_xb, tk_sx):
            def t_lg(e):
                for ch in range(DCH):
                    i = mm(e, bank(4), wgt[:, ch * 128:(ch + 1) * 128],
                           xbsub[:, ch * 512:(ch + 1) * 512], start=ch == 0)
                return i
            tk_lgm = S.op("T", t_lg, deps=[tk_xb] + list(wdeps) + bdep(4),
                          tag=f"lgm.{tag}{sub}")
            bufs["xbsub"].consume(tk_lgm)
            tk_lr = S.op("V", lambda e: (
                S.vf(e, e.tensor_copy(lrowh[0:8, :], bank(4, 0, 8))),
                e.scalar_tensor_tensor(lrowl[0:8, :], bank(4, 0, 8), 1.0,
                                       lrowh[0:8, :], ALU.mult, ALU.subtract)
            )[-1], deps=[tk_lgm] + bufs["lrow"].war(),
                tag=f"lrow.{tag}{sub}")
            set_bank(4, tk_lr)
            def t_ltr(e, sub=sub):
                for j in range(4):
                    g = sub * 4 + j
                    mm(e, bank(3, c0=8 * g, c1=8 * g + 8),
                       lrowh[:, 128 * j:128 * (j + 1)], idenb[:, 0:8],
                       start=True, stop=False)
                    i = mm(e, bank(3, c0=8 * g, c1=8 * g + 8),
                           lrowl[:, 128 * j:128 * (j + 1)], idenb[:, 0:8],
                           start=False, stop=True)
                return i
            tk_ltr = S.op("T", t_ltr,
                          deps=[tk_lr, tk_idenb]
                          + (bdep(3) if sub == 0 else []),
                          tag=f"ltr.{tag}{sub}")
            bufs["lrow"].consume(tk_ltr)
            logit_tr.append(tk_ltr)

        tk_stats = emit_stats(xdeps, extra=sweep1_extra, tag=tag)

        l3 = gl[:].rearrange("p (g e) -> p g e", e=8)
        g3 = gtmp[:].rearrange("p (g e) -> p g e", e=8)
        w3 = gw[:].rearrange("p (g e) -> p g e", e=8)
        def v_gfix(e):
            rstd3 = statc[:, C_RSTD:C_RSTD + 16].unsqueeze(2).broadcast_to(
                [128, 16, 8])
            mr3 = statc[:, C_MR:C_MR + 16].unsqueeze(2).broadcast_to(
                [128, 16, 8])
            cs3 = csbg[:, 0:8].unsqueeze(1).broadcast_to([128, 16, 8])
            bg3 = csbg[:, 8:16].unsqueeze(1).broadcast_to([128, 16, 8])
            p3 = bank(3, c0=0, c1=128).rearrange("p (g e) -> p g e", e=8)
            S.vf(e, e.tensor_tensor(l3, p3, rstd3, ALU.mult))
            S.vf(e, e.tensor_tensor(g3, mr3, cs3, ALU.mult))
            S.vf(e, e.tensor_tensor(l3, l3, g3, ALU.add))
            S.vf(e, e.tensor_tensor(l3, l3, bg3, ALU.add))
            return vtail(e)
        tk_gfix = S.op("V", v_gfix, deps=logit_tr + [tk_stats, tk_csbg],
                       tag=f"gfix.{tag}")
        set_bank(3, tk_gfix)
        S.dbg[f"gfix.{tag}"] = tk_gfix
        S.dbg[f"stats.{tag}"] = tk_stats
        set_bank(3, tk_gfix)
        def v_gate(e):
            m1 = statc[:, C_T0:C_T0 + 16]
            m2 = statc[:, C_T1:C_T1 + 16]
            S.vf(e, e.tensor_reduce(m1, l3, AX.X, ALU.max))
            S.vf(e, e.tensor_tensor(
                g3, l3, m1.unsqueeze(2).broadcast_to([128, 16, 8]), ALU.is_ge))
            S.vf(e, e.scalar_tensor_tensor(w3, g3, -BIG, l3, ALU.mult,
                                           ALU.add))
            S.vf(e, e.tensor_reduce(m2, w3, AX.X, ALU.max))
            S.vf(e, e.tensor_tensor(
                w3, w3, m2.unsqueeze(2).broadcast_to([128, 16, 8]), ALU.is_ge))
            S.vf(e, e.tensor_tensor(statc[:, C_T2:C_T2 + 16], m1, m2,
                                    ALU.subtract))
            return vtail(e)
        tk_g1 = S.op("V", v_gate, deps=[tk_gfix], tag=f"gate.{tag}")
        tk_sg = S.op("S", lambda e: (
            e.activation(out=statc[:, C_T0:C_T0 + 16],
                         in_=statc[:, C_T2:C_T2 + 16], func=ACT.Sigmoid),
            e.activation(out=statc[:, C_T1:C_T1 + 16],
                         in_=statc[:, C_T2:C_T2 + 16], func=ACT.Sigmoid,
                         scale=-1.0))[-1], deps=[tk_g1], tag=f"sig.{tag}")
        def v_comb(e):
            S.vf(e, e.tensor_tensor(w3, w3, statc[:, C_T1:C_T1 + 16]
                                    .unsqueeze(2).broadcast_to([128, 16, 8]),
                                    ALU.mult))
            S.vf(e, e.tensor_tensor(g3, g3, statc[:, C_T0:C_T0 + 16]
                                    .unsqueeze(2).broadcast_to([128, 16, 8]),
                                    ALU.mult))
            S.vf(e, e.tensor_tensor(w3, w3, g3, ALU.add))
            i = e.tensor_copy(gwb[:, 0:128], gw[:])
            return i
        tk_w = S.op("V", v_comb, deps=[tk_sg], tag=f"gw.{tag}")
        S.dbg[f"gw.{tag}"] = tk_w

        res_toks = []
        for sub in range(NSUB):
            t0 = sub * 512
            tk_rows, _ = emit_rows(sub, tk_stats, tag=tag)
            tk_xs = emit_xs(sub, tk_rows, xssub, "xssub", tag=tag)
            def t_w1(e):
                for rch in range(4):
                    for ch in range(DCH):
                        mm(e, bank(rch),
                           w1t[:, ch * 512 + 128 * rch: ch * 512 + 128 * rch + 128],
                           xssub[:, ch * 512:(ch + 1) * 512],
                           start=ch == 0, stop=False)
                    for j in range(4):
                        i = mm(e, bank(rch, c0=128 * j, c1=128 * (j + 1)),
                               aux_w1[:, 128 * rch:128 * (rch + 1)],
                               aux4[j][:], start=False, stop=j == 3)
                return i
            tk_w1m = S.op("T", t_w1,
                          deps=[tk_xs, tk_rows, tk_csw1] + list(wdeps)
                          + bdep(0, 1, 2, 3), tag=f"w1m.{tag}{sub}")
            bufs["xssub"].consume(tk_w1m)
            bufs["aux4"].consume(tk_w1m)
            tk_gel = []
            for rch in range(4):
                tk_g = S.op("S", lambda e, rch=rch: e.activation(
                    out=hg[:, rch * 512:(rch + 1) * 512], in_=bank(rch),
                    func=ACT.Gelu),
                    deps=[tk_w1m] + (bufs["hg"].war() if rch == 0 else []),
                    tag=f"gelu.{tag}{sub}{rch}")
                set_bank(rch, tk_g)
                tk_gel.append(tk_g)
            tk_hp = []
            for j in range(4):
                g = sub * 4 + j
                tk_gt = S.op("T", lambda e, g=g: mm(
                    e, bank(4, c0=0, c1=128), gwb[:, 8 * g:8 * g + 128],
                    idenb[:], start=True), deps=[tk_w, tk_idenb] + bdep(4),
                    tag=f"gwt.{tag}{sub}{j}")
                tk_wc = S.op("V", lambda e: e.tensor_copy(
                    wtg128[0:8, :], bank(4, 0, 8, 0, 128)),
                    deps=[tk_gt] + bufs["wtg"].war(), tag=f"wtg.{tag}{sub}{j}")
                set_bank(4, tk_wc)
                def t_sel(e):
                    for rch in range(4):
                        i = mm(e, bank(5, c0=128 * rch, c1=128 * (rch + 1)),
                               selb[:, 128 * rch:128 * (rch + 1)], wtg128[:],
                               start=True)
                    return i
                tk_sel = S.op("T", t_sel, deps=[tk_wc, tk_selb] + bdep(5),
                              tag=f"sel.{tag}{sub}{j}")
                bufs["wtg"].consume(tk_sel)
                def v_h(e, j=j):
                    for rch in range(4):
                        i = e.tensor_tensor(
                            hp[:, rch * 512 + 128 * j: rch * 512 + 128 * j + 128],
                            hg[:, rch * 512 + 128 * j: rch * 512 + 128 * j + 128],
                            bank(5, c0=128 * rch, c1=128 * (rch + 1)), ALU.mult)
                    return i
                tk_h = S.op("V", v_h,
                            deps=[tk_sel] + tk_gel
                            + (bufs["hp"].war() if j == 0 else []),
                            tag=f"hprime.{tag}{sub}{j}")
                bufs["hg"].consume(tk_h)
                set_bank(5, tk_h)
                tk_hp.append(tk_h)
            for m in range(DCH):
                fb = 6 + (m % 2)
                def t_w2(e, m=m, fb=fb):
                    for rch in range(4):
                        i = mm(e, bank(fb),
                               w2t[:, rch * 1024 + 128 * m:
                                   rch * 1024 + 128 * m + 128],
                               hp[:, rch * 512:(rch + 1) * 512],
                               start=rch == 0, stop=rch == 3)
                    return i
                tk_w2m = S.op("T", t_w2, deps=tk_hp + list(wdeps) + bdep(fb),
                              tag=f"w2m.{tag}{sub}{m}")
                bufs["hp"].consume(tk_w2m)
                def v_res(e, m=m, fb=fb, t0=t0):
                    return e.scalar_tensor_tensor(
                        sx[:, m * T + t0: m * T + t0 + 512], bank(fb), 1.0,
                        sx[:, m * T + t0: m * T + t0 + 512], ALU.mult, ALU.add)
                tk_res = S.op("V", v_res, deps=[tk_w2m],
                              tag=f"res.{tag}{sub}{m}")
                set_bank(fb, tk_res)
                res_toks.append(tk_res)
        return res_toks

    ffn1_w = [tk_w1t, tk_w2t, tk_csw1, tk_csbg, tk_consts]
    res_f1 = emit_ffn(tk_x + ffn1_w, ffn1_w, tag="f1")

    if taps:
        tap_f1 = [S.dma("dY", lambda e, ch=ch: e.dma_start(
            out=tap["tap_f1"][128 * ch:128 * (ch + 1), :],
            in_=sx[:, ch * T:(ch + 1) * T]), deps=res_f1) for ch in range(DCH)]
        tk_tap_gl1 = S.dma("dY", lambda e: e.dma_start(
            out=tap["tap_gl1"], in_=gl[:]), deps=[S.dbg["gfix.f1"]])
        tk_tap_st1 = S.dma("dY", lambda e: e.dma_start(
            out=tap["tap_st1"], in_=statc[:, 0:48]), deps=[S.dbg["stats.f1"]])
        tk_tap_gw1 = S.dma("dY", lambda e: e.dma_start(
            out=tap["tap_gw1"], in_=gw[:]), deps=[S.dbg["gw.f1"]])
        res_f1 = res_f1 + [tk_tap_gl1, tk_tap_st1, tk_tap_gw1]

    tk_ln2 = emit_stats(res_f1, tag="ln2")

    tk_wq = S.dma("dG", lambda e: e.dma_start(
        out=wqkv[:, 0:DCH * 1024].rearrange("p (c f) -> p c f", c=DCH),
        in_=wqkv_d[:, 0:1024].rearrange("(c p) f -> p c f", p=128)),
        deps=res_f1, tag="ldwq")
    tk_bq = S.dma("dG", lambda e: e.dma_start(
        out=aux_w[0:1, 0:1024], in_=bqkv_d[:, 0:1024]),
        deps=[tk_consts] + bufs["aux_w"].war(), tag="ldbq")

    tk_csq = []
    for fs in range(2):
        def t_cs(e, fs=fs):
            for ch in range(DCH):
                i = mm(e, bank(7), onescol[:],
                       wqkv[:, ch * 1024 + 512 * fs: ch * 1024 + 512 * fs + 512],
                       start=ch == 0)
            return i
        tk_m = S.op("T", t_cs, deps=[tk_wq, tk_consts] + bdep(7),
                    tag=f"csq{fs}")
        tk_c = S.op("V", lambda e, fs=fs: e.tensor_copy(
            aux_w[32:33, 512 * fs:512 * fs + 512], bank(7, 0, 1)),
            deps=[tk_m, tk_bq], tag=f"csqc{fs}")
        set_bank(7, tk_c)
        tk_csq.append(tk_c)

    tk_q2_dma = []
    tk_qmm = []
    prev_sq = None
    for sub in range(NSUB):
        tk_rows, _ = emit_rows(sub, tk_ln2, tag="q")
        tk_xs2 = emit_xs(sub, tk_rows, xs2, "xs2", tag="q")
        for j in range(4):
            g = sub * 4 + j
            b0 = 0 if g % 2 == 0 else 2
            def t_q(e, j=j, b0=b0):
                for ch in range(DCH):
                    lh = xs2[:, ch * 512 + 128 * j: ch * 512 + 128 * j + 128]
                    mm(e, bank(b0), lh, wqkv[:, ch * 1024:ch * 1024 + 512],
                       start=ch == 0, stop=False)
                    mm(e, bank(b0 + 1), lh,
                       wqkv[:, ch * 1024 + 512:ch * 1024 + 1024],
                       start=ch == 0, stop=False)
                ax = aux4[j][:]
                mm(e, bank(b0), ax, aux_w[:, 0:512], start=False)
                i = mm(e, bank(b0 + 1), ax, aux_w[:, 512:1024], start=False)
                return i
            tk_qm = S.op("T", t_q, deps=[tk_xs2, tk_rows, tk_bq] + tk_csq
                         + bdep(b0, b0 + 1), tag=f"qmm{g}")
            bufs["xs2"].consume(tk_qm)
            bufs["aux4"].consume(tk_qm)
            bufs["aux_w"].consume(tk_qm)
            tk_qmm.append(tk_qm)
            qp = pall[:, 512 * b0:512 * b0 + 1024]
            def v_mq(e, qp=qp):
                return e.tensor_reduce(statc[:, C_G:C_G + 1], qp, AX.X,
                                       ALU.add, negate=True)
            tk_mq = S.op("V", v_mq,
                         deps=[tk_qm] + ([prev_sq] if prev_sq else []),
                         tag=f"mq{g}")
            tk_sq = S.op("S", lambda e, qp=qp: e.activation(
                out=q2st[:], in_=qp, func=ACT.Square,
                bias=statc[:, C_G:C_G + 1], scale=float(D),
                accum_out=statc[:, C_G + 1:C_G + 2]),
                deps=[tk_mq] + bufs["q2st"].war(), tag=f"q2sq{g}")
            prev_sq = tk_sq
            set_bank(b0, tk_sq)
            set_bank(b0 + 1, tk_sq)
            tk_t4 = S.op("S", lambda e: e.activation(
                out=u2st[:], in_=q2st[:], func=ACT.Square,
                accum_out=statc[:, C_G + 2:C_G + 3]),
                deps=[tk_sq] + bufs["u2st"].war(), tag=f"t4{g}")
            tk_rt = S.op("S", lambda e: e.activation(
                out=statc[:, C_G + 3:C_G + 4], in_=statc[:, C_G + 2:C_G + 3],
                func=ACT.Sqrt), deps=[tk_t4], tag=f"rsq{g}")
            def v_rs(e):
                S.vf(e, e.tensor_scalar(statc[:, C_G + 4:C_G + 5],
                                        statc[:, C_G + 1:C_G + 2],
                                        float(D) * D * EPS_SUM, EPS_NORM,
                                        ALU.add, ALU.mult))
                S.vf(e, e.tensor_tensor(statc[:, C_G + 4:C_G + 5],
                                        statc[:, C_G + 3:C_G + 4],
                                        statc[:, C_G + 4:C_G + 5], ALU.max))
                S.vf(e, e.reciprocal(statc[:, C_G + 5:C_G + 6],
                                     statc[:, C_G + 4:C_G + 5]))
                i = e.tensor_scalar(q2st[:], q2st[:],
                                    statc[:, C_G + 5:C_G + 6], None, ALU.mult)
                return i
            tk_rsx = S.op("V", v_rs, deps=[tk_rt], tag=f"rs{g}")
            tk_dma = S.dma("dS", lambda e, g=g: e.dma_start(
                out=q2_dram.ap()[128 * g:128 * (g + 1), :], in_=q2st[:]),
                deps=[tk_rsx], tag=f"q2dma{g}")
            bufs["q2st"].consume(tk_dma)
            bufs["u2st"].consume(tk_t4)
            tk_q2_dma.append(tk_dma)

    tk_wkv = S.dma("dG", lambda e: e.dma_start(
        out=wqkv[:].rearrange("p (c f) -> p c f", c=DCH),
        in_=wqkv_d[:, 1024:3072].rearrange("(c p) f -> p c f", p=128)),
        deps=tk_qmm + tk_csq, tag="ldwkv")
    tk_bkv = S.dma("dG", lambda e: e.dma_start(
        out=aux_w[0:1, :], in_=bqkv_d[:, 1024:3072]),
        deps=tk_qmm + bufs["aux_w"].war(), tag="ldbkv")
    tk_cskv = []
    for fs in range(4):
        def t_cs(e, fs=fs):
            for ch in range(DCH):
                i = mm(e, bank(7), onescol[:],
                       wqkv[:, ch * 2048 + 512 * fs: ch * 2048 + 512 * fs + 512],
                       start=ch == 0)
            return i
        tk_m = S.op("T", t_cs, deps=[tk_wkv, tk_consts] + bdep(7),
                    tag=f"cskv{fs}")
        tk_c = S.op("V", lambda e, fs=fs: e.tensor_copy(
            aux_w[32:33, 512 * fs:512 * fs + 512], bank(7, 0, 1)),
            deps=[tk_m, tk_bkv], tag=f"cskvc{fs}")
        set_bank(7, tk_c)
        tk_cskv.append(tk_c)

    tk_kvmm = []
    tk_uev = []
    tk_vev = []
    prev_k2 = None
    for sub in range(NSUB):
        tk_rows, _ = emit_rows(sub, tk_ln2, tag="kv")
        tk_xs2 = emit_xs(sub, tk_rows, xs2, "xs2", tag="kv")
        for j in range(4):
            g = sub * 4 + j
            b0 = 0 if g % 2 == 0 else 2
            def t_kv(e, j=j, b0=b0):
                for ch in range(DCH):
                    lh = xs2[:, ch * 512 + 128 * j: ch * 512 + 128 * j + 128]
                    mm(e, bank(b0), lh, wqkv[:, ch * 2048:ch * 2048 + 512],
                       start=ch == 0, stop=False)
                    mm(e, bank(b0 + 1), lh,
                       wqkv[:, ch * 2048 + 512:ch * 2048 + 1024],
                       start=ch == 0, stop=False)
                    mm(e, bank(4), lh,
                       wqkv[:, ch * 2048 + 1024:ch * 2048 + 1536],
                       start=ch == 0, stop=False)
                    mm(e, bank(5), lh,
                       wqkv[:, ch * 2048 + 1536:ch * 2048 + 2048],
                       start=ch == 0, stop=False)
                ax = aux4[j][:]
                mm(e, bank(b0), ax, aux_w[:, 0:512], start=False)
                mm(e, bank(b0 + 1), ax, aux_w[:, 512:1024], start=False)
                mm(e, bank(4), ax, aux_w[:, 1024:1536], start=False)
                i = mm(e, bank(5), ax, aux_w[:, 1536:2048], start=False)
                return i
            tk_km = S.op("T", t_kv, deps=[tk_xs2, tk_rows, tk_bkv] + tk_cskv
                         + bdep(b0, b0 + 1, 4, 5), tag=f"kvmm{g}")
            bufs["xs2"].consume(tk_km)
            bufs["aux4"].consume(tk_km)
            bufs["aux_w"].consume(tk_km)
            tk_kvmm.append(tk_km)
            tk_v = S.op("V", lambda e, g=g: e.tensor_copy(
                vtm[:, g * D:(g + 1) * D], pall[:, 512 * 4:512 * 6]),
                deps=[tk_km], tag=f"vev{g}")
            set_bank(4, tk_v)
            set_bank(5, tk_v)
            tk_vev.append(tk_v)
            kp = pall[:, 512 * b0:512 * b0 + 1024]
            def v_mk(e, kp=kp):
                return e.tensor_reduce(statc[:, C_G:C_G + 1], kp, AX.X,
                                       ALU.add, negate=True)
            tk_mk = S.op("V", v_mk,
                         deps=[tk_km] + ([prev_k2] if prev_k2 else []),
                         tag=f"mk{g}")
            tk_k2 = S.op("S", lambda e, kp=kp, g=g: e.activation(
                out=utm[:, g * D:(g + 1) * D], in_=kp, func=ACT.Square,
                bias=statc[:, C_G:C_G + 1], scale=float(D),
                accum_out=statc[:, C_G + 1:C_G + 2]),
                deps=[tk_mk], tag=f"k2{g}")
            prev_k2 = tk_k2
            set_bank(b0, tk_k2)
            set_bank(b0 + 1, tk_k2)
            def v_u(e, g=g):
                S.vf(e, e.tensor_scalar(statc[:, C_G + 2:C_G + 3],
                                        statc[:, C_G + 1:C_G + 2],
                                        float(D) * D * EPS_SUM, None, ALU.add))
                S.vf(e, e.reciprocal(statc[:, C_G + 3:C_G + 4],
                                     statc[:, C_G + 2:C_G + 3]))
                i = e.tensor_scalar(utm[:, g * D:(g + 1) * D],
                                    utm[:, g * D:(g + 1) * D],
                                    statc[:, C_G + 3:C_G + 4], None, ALU.mult)
                return i
            tk_u = S.op("V", v_u, deps=[tk_k2], tag=f"uev{g}")
            tk_uev.append(tk_u)

    if taps:
        tk_tap_u = S.dma("dG", lambda e: e.dma_start(
            out=tap["tap_u"].rearrange("(g p) c -> p g c", p=128),
            in_=utm[:].rearrange("p (g c) -> p g c", g=NG)), deps=tk_uev)
        tk_tap_v = S.dma("dG", lambda e: e.dma_start(
            out=tap["tap_v"].rearrange("(g p) c -> p g c", p=128),
            in_=vtm[:].rearrange("p (g c) -> p g c", g=NG)), deps=tk_vev)

    tk_kvt_ev = []
    kv_mm_toks = []
    for chalf in range(2):
        def t_kvp(e, chalf=chalf):
            for g in range(NG):
                for dch in range(DCH):
                    i = mm(e, bank(dch),
                           vtm[:, g * D + 128 * dch: g * D + 128 * dch + 128],
                           utm[:, g * D + 512 * chalf: g * D + 512 * chalf + 512],
                           start=g == 0, stop=g == NG - 1)
            return i
        tk_kvm = S.op("T", t_kvp, deps=tk_uev + tk_vev + bdep(*range(8)),
                      tag=f"kvp{chalf}")
        kv_mm_toks.append(tk_kvm)
        def v_kvev(e, chalf=chalf):
            for dch in range(DCH):
                i = e.tensor_copy(
                    kvtb[:, dch * 1024 + 512 * chalf:
                         dch * 1024 + 512 * chalf + 512], bank(dch))
            return i
        tk_ev = S.op("V", v_kvev, deps=[tk_kvm] + tk_kvmm + tk_qmm,
                     tag=f"kvtev{chalf}")
        for dch in range(DCH):
            set_bank(dch, tk_ev)
        tk_kvt_ev.append(tk_ev)

    tk_s2mm = []
    tk_u2_last = None
    for g in range(NG):
        tk_u2 = S.op("S", lambda e, g=g: e.activation(
            out=u2st[:], in_=utm[:, g * D:(g + 1) * D], func=ACT.Square),
            deps=[tk_uev[g]] + bufs["u2st"].war()
            + ([tk_s2mm[-1]] if tk_s2mm else []), tag=f"u2.{g}")
        tk_u2_last = tk_u2
        def t_s2(e, g=g):
            mm(e, bank(6), onescol[:], u2st[:, 0:512], start=g == 0,
               stop=g == NG - 1)
            i = mm(e, bank(7), onescol[:], u2st[:, 512:1024], start=g == 0,
                   stop=g == NG - 1)
            return i
        tk_mm2 = S.op("T", t_s2,
                      deps=[tk_u2, tk_consts] + (bdep(6, 7) if g == 0 else []),
                      tag=f"s2mm{g}")
        bufs["u2st"].consume(tk_mm2)
        tk_s2mm.append(tk_mm2)
    tk_s2ev = S.op("V", lambda e: (
        e.tensor_copy(arstage[0:1, 0:512], bank(6, 0, 1)),
        e.tensor_copy(arstage[0:1, 512:1024], bank(7, 0, 1)))[-1],
        deps=[tk_s2mm[-1]] + bufs["arstage"].war(), tag="s2ev")
    set_bank(6, tk_s2ev)
    set_bank(7, tk_s2ev)
    tk_s2dma = S.dma("dS", lambda e: e.dma_start(
        out=ar_in.ap()[D:D + 1, :], in_=arstage[0:1, 0:1024]),
        deps=[tk_s2ev], tag="s2dma")
    bufs["arstage"].consume(tk_s2dma)

    tk_wln = S.dma("dG", lambda e: e.dma_start(
        out=wlns[:].rearrange("p (c f) -> p c f", c=DCH),
        in_=wln_d.rearrange("(c p) f -> p c f", p=128)),
        deps=tk_kvmm + tk_qmm + tk_cskv, tag="ldwln")

    ar_dma = [tk_s2dma]
    tk_kvw_mm = []
    for cch in range(DCH):
        for dhalf in range(2):
            b = 2 + (cch * 2 + dhalf) % 2
            def t_kvw(e, cch=cch, dhalf=dhalf, b=b):
                for dch in range(DCH):
                    i = mm(e, bank(b),
                           kvtb[:, dch * 1024 + 128 * cch:
                                dch * 1024 + 128 * cch + 128],
                           wlns[:, dch * 1024 + 512 * dhalf:
                                dch * 1024 + 512 * dhalf + 512],
                           start=dch == 0, stop=dch == DCH - 1)
                return i
            tk_m = S.op("T", t_kvw, deps=tk_kvt_ev + [tk_wln] + bdep(b),
                        tag=f"kvw{cch}.{dhalf}")
            tk_kvw_mm.append(tk_m)
            tk_st = S.op("V", lambda e, b=b: e.tensor_copy(
                arstage[:, 0:512], bank(b)),
                deps=[tk_m] + bufs["arstage"].war(), tag=f"kvwst{cch}.{dhalf}")
            set_bank(b, tk_st)
            tk_d = S.dma("dS", lambda e, cch=cch, dhalf=dhalf: e.dma_start(
                out=ar_in.ap()[128 * cch:128 * (cch + 1),
                               512 * dhalf:512 * dhalf + 512],
                in_=arstage[:, 0:512]), deps=[tk_st],
                tag=f"kvwdma{cch}.{dhalf}")
            bufs["arstage"].consume(tk_d)
            ar_dma.append(tk_d)

    if taps:
        tk_tap_arin = S.dma("dG", lambda e: e.dma_start(
            out=tap["tap_arin"], in_=ar_in.ap()), deps=ar_dma)
        tk_tap_kvt = S.dma("dG", lambda e: e.dma_start(
            out=tap["tap_kvt"].rearrange("(c p) f -> p c f", p=128),
            in_=kvtb[:].rearrange("p (c f) -> p c f", c=DCH)), deps=tk_kvt_ev)

    tk_cc = S.op("G", lambda e: e.collective_compute(
        "AllReduce", ALU.add, replica_groups=[[0, 1, 2, 3], [4, 5, 6, 7]],
        ins=[ar_in.ap().opt()], outs=[ar_out.ap().opt()]),
        deps=ar_dma + ([tk_tap_arin] if taps else []), tag="allreduce")

    vf_war = kv_mm_toks + [tk_u2_last] + ([tk_tap_u, tk_tap_v] if taps else [])
    tk_vf = []
    for g in range(NG):
        for dch in range(DCH):
            tk_vf.append(S.dma("dY", lambda e, g=g, dch=dch: e.dma_start_transpose(
                out=vF[:, dch * T + 128 * g: dch * T + 128 * g + 128],
                in_=vtm[:, g * D + 128 * dch: g * D + 128 * dch + 128]),
                deps=vf_war, tag=f"vf{g}.{dch}"))

    tk_vw_res = []
    for sub in range(NSUB):
        t0 = sub * 512
        vf_dep = [tk_vf[(sub * 4 + j) * DCH + dch]
                  for j in range(4) for dch in range(DCH)]
        for m in range(DCH):
            b = m % 2
            def t_vw(e, m=m, sub=sub, b=b):
                for dch in range(DCH):
                    i = mm(e, bank(b),
                           wlns[:, dch * 1024 + 128 * m:
                                dch * 1024 + 128 * m + 128],
                           vF[:, dch * T + 512 * sub: dch * T + 512 * sub + 512],
                           start=dch == 0, stop=dch == DCH - 1)
                return i
            tk_m = S.op("T", t_vw, deps=vf_dep + [tk_wln] + bdep(b),
                        tag=f"vw{sub}.{m}")
            def v_r(e, m=m, t0=t0, b=b):
                return e.scalar_tensor_tensor(
                    sx[:, m * T + t0: m * T + t0 + 512], bank(b),
                    blnc[:, m:m + 1], sx[:, m * T + t0: m * T + t0 + 512],
                    ALU.add, ALU.add)
            tk_r = S.op("V", v_r, deps=[tk_m, tk_blnc], tag=f"vwres{sub}.{m}")
            set_bank(b, tk_r)
            tk_vw_res.append(tk_r)

    qf_war = kv_mm_toks + [tk_vf[-1]] + ([tk_tap_v] if taps else [])
    tk_qf = []
    for g in range(NG):
        for cch in range(DCH):
            tk_qf.append(S.dma("dY", lambda e, g=g, cch=cch: e.dma_start_transpose(
                out=q2F[:, cch * T + 128 * g: cch * T + 128 * g + 128],
                in_=q2_dram.ap()[128 * g:128 * (g + 1),
                                 128 * cch:128 * (cch + 1)]),
                deps=qf_war + [tk_q2_dma[g]], tag=f"qf{g}.{cch}"))

    tk_s2l = S.dma("dS", lambda e: e.dma_start(
        out=arstage[0:1, :], in_=ar_out.ap()[D:D + 1, :]),
        deps=[tk_cc] + bufs["arstage"].war(), tag="lds2")
    tk_rcp_rows = []
    for r in range(2):
        tk_rr = S.op("V", lambda e, r=r: (
            S.vf(e, e.tensor_copy(trh[0:1, :],
                                  arstage[0:1, 512 * r:512 * r + 512])),
            e.scalar_tensor_tensor(trl[0:1, :],
                                   arstage[0:1, 512 * r:512 * r + 512], 1.0,
                                   trh[0:1, :], ALU.mult, ALU.subtract)
        )[-1], deps=[tk_s2l] + bufs["rowstage"].war(), tag=f"rcprow{r}")
        def t_rc(e, r=r):
            for j in range(4):
                mm(e, bank(6, c0=4 * r + j, c1=4 * r + j + 1),
                   trh[:, 128 * j:128 * (j + 1)], idenb[:, 0:1],
                   start=True, stop=False)
                i = mm(e, bank(6, c0=4 * r + j, c1=4 * r + j + 1),
                       trl[:, 128 * j:128 * (j + 1)], idenb[:, 0:1],
                       start=False, stop=True)
            return i
        tk_rm = S.op("T", t_rc, deps=[tk_rr, tk_idenb]
                     + (bdep(6) if r == 0 else []), tag=f"rcpmm{r}")
        bufs["rowstage"].consume(tk_rm)
        tk_rcp_rows.append(tk_rm)
    tk_rcpt = tk_rcp_rows[-1]
    bufs["arstage"].consume(tk_rcpt)
    tk_rcpc = S.op("V", lambda e: e.tensor_copy(
        statc[:, C_RCP:C_RCP + 8], bank(6, c0=0, c1=8)), deps=[tk_rcpt],
        tag="rcpcp")
    set_bank(6, tk_rcpc)
    tk_rcps = S.op("S", lambda e: e.activation(
        out=statc[:, C_RCP:C_RCP + 8], in_=statc[:, C_RCP:C_RCP + 8],
        func=ACT.Sqrt, scale=float(SEQ)), deps=[tk_rcpc], tag="rcpsqrt")
    def v_rcp(e):
        S.vf(e, e.tensor_scalar_max(statc[:, C_RCP:C_RCP + 8],
                                    statc[:, C_RCP:C_RCP + 8],
                                    EPS_NORM * math.sqrt(SEQ)))
        S.vf(e, e.reciprocal(statc[:, C_RCP:C_RCP + 8],
                             statc[:, C_RCP:C_RCP + 8]))
        return vtail(e)
    tk_rcp = S.op("V", v_rcp, deps=[tk_rcps], tag="rcp")

    tk_kvwb = []
    for cch in range(DCH):
        tk_l = S.dma("dS", lambda e, cch=cch: e.dma_start(
            out=arstage[:], in_=ar_out.ap()[128 * cch:128 * (cch + 1), :]),
            deps=[tk_s2l] + bufs["arstage"].war(), tag=f"ldkvw{cch}")
        tk_c = S.op("V", lambda e, cch=cch: e.tensor_scalar(
            kvwb[:, cch * 1024:(cch + 1) * 1024], arstage[:],
            statc[:, C_RCP + cch:C_RCP + cch + 1], None, ALU.mult),
            deps=[tk_l, tk_rcp] + (tk_kvw_mm if cch == 0 else []),
            tag=f"kvwb{cch}")
        bufs["arstage"].consume(tk_c)
        tk_kvwb.append(tk_c)

    if taps:
        tk_tap_arout = S.dma("dG", lambda e: e.dma_start(
            out=tap["tap_arout"], in_=ar_out.ap()), deps=[tk_cc])
        tk_tap_q2 = S.dma("dG", lambda e: e.dma_start(
            out=tap["tap_q2"], in_=q2_dram.ap()), deps=tk_q2_dma)

    tk_t2_res = []
    for sub in range(NSUB):
        t0 = sub * 512
        qf_dep = [tk_qf[(sub * 4 + j) * DCH + cch]
                  for j in range(4) for cch in range(DCH)]
        for m in range(DCH):
            b = 2 + m % 2
            def t_t2(e, m=m, sub=sub, b=b):
                for cch in range(DCH):
                    i = mm(e, bank(b),
                           kvwb[:, cch * 1024 + 128 * m:
                                cch * 1024 + 128 * m + 128],
                           q2F[:, cch * T + 512 * sub: cch * T + 512 * sub + 512],
                           start=cch == 0, stop=cch == DCH - 1)
                return i
            tk_m = S.op("T", t_t2, deps=qf_dep + tk_kvwb + bdep(b),
                        tag=f"t2.{sub}.{m}")
            def v_r(e, m=m, t0=t0, b=b):
                return e.scalar_tensor_tensor(
                    sx[:, m * T + t0: m * T + t0 + 512], bank(b), 1.0,
                    sx[:, m * T + t0: m * T + t0 + 512], ALU.mult, ALU.add)
            tk_r = S.op("V", v_r, deps=[tk_m], tag=f"t2res{sub}.{m}")
            set_bank(b, tk_r)
            tk_t2_res.append(tk_r)

    if taps:
        tap_x2 = [S.dma("dY", lambda e, ch=ch: e.dma_start(
            out=tap["tap_x2"][128 * ch:128 * (ch + 1), :],
            in_=sx[:, ch * T:(ch + 1) * T]), deps=tk_t2_res + tk_vw_res)
            for ch in range(DCH)]

    tk_w1t2 = S.dma("dG", lambda e: e.dma_start(
        out=w1t[:].rearrange("p (c r) -> p c r", c=DCH),
        in_=w1t_d.rearrange("(c p) r -> p c r", p=128)),
        deps=tk_t2_res + tk_vw_res + tk_kvw_mm, tag="ldw1t2")
    tk_w2t2 = S.dma("dG", lambda e: e.dma_start(
        out=w2t[:].rearrange("p (c d) -> p c d", c=4),
        in_=w2t_d.rearrange("(c p) d -> p c d", p=128)),
        deps=tk_t2_res + tk_vw_res + tk_kvw_mm, tag="ldw2t2")
    ffn2_w = [tk_w1t2, tk_w2t2, tk_csw1, tk_csbg, tk_consts]
    res_f2 = emit_ffn(tk_t2_res + tk_vw_res + [tk_w1t2, tk_w2t2], ffn2_w,
                      tag="f2")

    tk_out = [S.dma("dY", lambda e, ch=ch: e.dma_start(
        out=out_t[128 * ch:128 * (ch + 1), :],
        in_=sx[:, ch * T:(ch + 1) * T]), deps=res_f2, tag=f"out{ch}")
        for ch in range(DCH)]
    if taps:
        tk_out += [tk_tap_arout, tk_tap_q2, tk_tap_kvt] + tap_f1 + tap_x2
        tk_tap_st = S.dma("dY", lambda e: e.dma_start(
            out=tap["tap_stat"], in_=statc[:]), deps=res_f2)
        tk_tap_gw = S.dma("dY", lambda e: e.dma_start(
            out=tap["tap_gw"], in_=gw[:]), deps=res_f2)
        tk_out += [tk_tap_st, tk_tap_gw]
    S.op("Y", lambda e: e.nop(), deps=tk_out, tag="final")

    S.prepare(nc, serialize=serialize, ser_range=ser_range)
    with nc.Block() as block:
        @block.tensor
        def _(eng):
            S.emit_engine("T", eng)
        @block.vector
        def _(eng):
            S.emit_engine("V", eng)
        @block.scalar
        def _(eng):
            S.emit_engine("S", eng)
        @block.gpsimd
        def _(eng):
            S.emit_engine("G", eng)
        @block.sync
        def _(eng):
            S.emit_engine("Y", eng)
    return nc


def host_inputs(inputs, core):
    x = inputs["L0_attn_x"]
    b, s = core // 4, core % 4
    xt = np.ascontiguousarray(np.asarray(x)[b, s * T:(s + 1) * T, :].T)
    sel = np.zeros((128, 512), dtype=ml_dtypes.bfloat16)
    for r in range(512):
        sel[r // 64, r] = 1.0
    jsel = np.zeros((128, 512), dtype=ml_dtypes.bfloat16)
    for n in range(512):
        jsel[32 * (n // 128), n] = 1.0
    W1 = np.asarray(inputs["L0_ffn_W1"], dtype=np.float32)
    W2 = np.asarray(inputs["L0_ffn_W2"], dtype=np.float32)
    Wg = np.asarray(inputs["L0_ffn_Wg"], dtype=np.float32)
    wgt = np.zeros((D, 128), dtype=np.float32)
    wgt[:, 0:8] = Wg.T
    return {
        "xt": xt.astype(np.float32),
        "wqkv": np.ascontiguousarray(np.asarray(inputs["L0_attn_Wqkv"],
                                                dtype=np.float32)),
        "bqkv": np.asarray(inputs["L0_attn_bqkv"],
                           dtype=np.float32).reshape(1, 3 * D),
        "wln": np.ascontiguousarray(np.asarray(inputs["L0_attn_Wln"],
                                               dtype=np.float32)),
        "blnc": np.ascontiguousarray(
            np.asarray(inputs["L0_attn_bln"], dtype=np.float32)
            .reshape(8, 128).T),
        "w1t": np.ascontiguousarray(W1.reshape(512, D).T),
        "w2t": np.ascontiguousarray(W2.transpose(0, 2, 1).reshape(512, D)),
        "wgt": wgt,
        "bg": np.asarray(inputs["L0_ffn_bg"], dtype=np.float32).reshape(1, 8),
        "iden32": np.eye(128, dtype=np.float32),
        "idenb": np.eye(128, dtype=ml_dtypes.bfloat16),
        "selb": sel,
        "jsel": jsel,
    }


def kernel(**inputs):
    nc = build(taps=False, ser_range=(0, 40))
    in_maps = [host_inputs(inputs, c) for c in range(8)]
    res = run_bass_kernel_spmd(nc, in_maps, core_ids=list(range(8)))
    out = np.empty((2, SEQ, D), dtype=np.float32)
    for c in range(8):
        b, s = c // 4, c % 4
        out[b, s * T:(s + 1) * T, :] = res.results[c]["out_t"].T
    return out
